# version 31
# baseline (speedup 1.0000x reference)
import contextlib
import numpy as np
import ml_dtypes
import concourse.bass as bass
import concourse.mybir as mybir
from concourse.bass_utils import run_bass_kernel_spmd

F32 = mybir.dt.float32
BF16 = mybir.dt.bfloat16
AF = mybir.ActivationFunctionType
ALU = mybir.AluOpType
AX = mybir.AxisListType

ENGINES = ("sync", "gpsimd", "scalar", "vector", "tensor")
SEMS = ("sx0", "sx1", "sx2", "sx3", "sy0", "sy1", "sz0", "sz1", "sxm", "sxl", "sw0", "sw1", "sw2", "sw3", "so0", "so1", "st", "sv", "ss")

LN_EPS = 1e-5
ATT_EPS = 1e-6
RMS_EPS = 1e-6
NEG_BIG = -1e30


class Sched:

    AUTO = {"vector": "sv", "scalar": "ss"}

    def __init__(self):
        self.ops = {e: [] for e in ENGINES}
        self.cnt = {s: 0 for s in SEMS}

    def do(self, eng, build, sem=None, inc=1, dma=False):
        if eng in self.AUTO and not dma:
            own = self.AUTO[eng]
            prev = self.cnt[own]
            if prev > 0:
                self.ops[eng].append(
                    lambda e, S, s=own, v=prev: e.wait_ge(S[s], v))
            self.cnt[own] += 1
            self.ops[eng].append(
                lambda e, S, b=build, s=own: b(e).then_inc(S[s], 1))
            return self.cnt[own]
        if sem is None:
            self.ops[eng].append(lambda e, S, b=build: b(e))
            return None
        self.cnt[sem] += inc
        self.ops[eng].append(
            lambda e, S, b=build, s=sem, i=inc: b(e).then_inc(S[s], i))
        return self.cnt[sem]

    def wait(self, eng, sem, val=None):
        v = self.cnt[sem] if val is None else val
        if v > 0:
            self.ops[eng].append(lambda e, S, s=sem, v=v: e.wait_ge(S[s], v))

    def mark(self, sem):
        return (sem, self.cnt[sem])

    def wait_m(self, eng, mark):
        if mark is not None:
            self.wait(eng, mark[0], mark[1])

    def barrier(self):
        snap = dict(self.cnt)
        for eng in ENGINES:
            for s, v in snap.items():
                if v > 0:
                    self.wait(eng, s, v)


def build_program(cfg):
    D = cfg["D"]; T = cfg["T"]; H = cfg["H"]; M = cfg["M"]; E = cfg["E"]
    CHF = cfg["CHF"]; CHP1 = cfg["CHP1"]; CHP2 = cfg["CHP2"]
    KD = D // 128
    HM = H * M
    KHM = HM // 128
    assert H * 128 == D and M == 64 and E == 8

    def nsplit(n):
        out = []
        c = 0
        while c < n:
            w = min(512, n - c)
            out.append((c, c + w))
            c += w
        return out

    nc = bass.Bass("TRN2", target_bir_lowering=False)

    x0 = nc.dram_tensor("x0", [D, T], F32, kind="ExternalInput").ap()
    out = nc.dram_tensor("out_fm", [D, T], F32, kind="ExternalOutput").ap()
    x1s = nc.dram_tensor("x1s", [D, T], F32, kind="Internal").ap()
    x2s = nc.dram_tensor("x2s", [D, T], F32, kind="Internal").ap()
    xlns = nc.dram_tensor("xlns", [D, T], BF16, kind="Internal").ap()
    ysc = nc.dram_tensor("ysc", [D, T], BF16, kind="Internal").ap()

    EW = nc.dram_tensor("expert_w", [E, D, D], F32, kind="ExternalInput").ap()
    EB = nc.dram_tensor("expert_b", [E, D], F32, kind="ExternalInput").ap()
    RW = nc.dram_tensor("router_w", [D, E], F32, kind="ExternalInput").ap()
    RB = nc.dram_tensor("router_b", [E], F32, kind="ExternalInput").ap()
    PW = nc.dram_tensor("proj_w", [D, D], F32, kind="ExternalInput").ap()
    PB = nc.dram_tensor("proj_b", [D], F32, kind="ExternalInput").ap()
    NW = nc.dram_tensor("norm_w", [D], F32, kind="ExternalInput").ap()
    WQ = nc.dram_tensor("wq", [D, D], F32, kind="ExternalInput").ap()
    BQ = nc.dram_tensor("bq", [D], F32, kind="ExternalInput").ap()
    WK = nc.dram_tensor("wk", [D, D], F32, kind="ExternalInput").ap()
    BK = nc.dram_tensor("bk", [D], F32, kind="ExternalInput").ap()
    WVT = nc.dram_tensor("wvT", [D, D], F32, kind="ExternalInput").ap()
    BV = nc.dram_tensor("bv", [D], F32, kind="ExternalInput").ap()
    WOT = nc.dram_tensor("woT", [D, D], F32, kind="ExternalInput").ap()
    BO = nc.dram_tensor("bo", [D], F32, kind="ExternalInput").ap()
    WMQT = nc.dram_tensor("wmap_qT", [128, M], F32, kind="ExternalInput").ap()
    WMKT = nc.dram_tensor("wmap_kT", [128, M], F32, kind="ExternalInput").ap()
    IDF = nc.dram_tensor("idf32", [128, 128], F32, kind="ExternalInput").ap()
    SEL8 = nc.dram_tensor("sel8_c", [E, E * 128], F32, kind="ExternalInput").ap()
    SELH = nc.dram_tensor("selh_c", [H, KHM * 128], F32, kind="ExternalInput").ap()

    b = Sched()
    RT = max(CHF, CHP1, CHP2)

    with contextlib.ExitStack() as _stk:
        en = _stk.enter_context
        ps_kv = en(nc.psum_tensor("ps_kv", [128, 2048], F32))
        ps_a = en(nc.psum_tensor("ps_a", [128, 1024], F32))
        ps_b = en(nc.psum_tensor("ps_b", [128, 1024], F32))
        idf = en(nc.sbuf_tensor("idf_sb", [128, 128], F32))
        ones_bf = en(nc.sbuf_tensor("ones_bf", [128, 1], BF16))
        ones_f1 = en(nc.sbuf_tensor("ones_f1", [1, 128], F32))
        ones_fc = en(nc.sbuf_tensor("ones_fc", [128, 1], F32))
        rw_f = en(nc.sbuf_tensor("rw_f", [128, KD * E], F32))
        rw_hi = en(nc.sbuf_tensor("rw_hi", [128, KD * E], BF16))
        rw_lo = en(nc.sbuf_tensor("rw_lo", [128, KD * E], BF16))
        expb = en(nc.sbuf_tensor("expb", [128, E * KD], F32))
        projb = en(nc.sbuf_tensor("projb", [128, KD], F32))
        halfnw = en(nc.sbuf_tensor("halfnw", [128, KD], F32))
        bo_c = en(nc.sbuf_tensor("bo_c", [128, KD], F32))
        sbc = en(nc.sbuf_tensor("sbc", [128, 2 * E], F32))
        wmq = en(nc.sbuf_tensor("wmq", [128, M], BF16))
        wmk = en(nc.sbuf_tensor("wmk", [128, M], BF16))
        bqkr = en(nc.sbuf_tensor("bqkr", [128, 2 * H], BF16))
        bq_c = en(nc.sbuf_tensor("bq_c", [128, KHM], F32))
        rows_m = en(nc.sbuf_tensor("rows_m", [1, RT], F32))
        rows_is = en(nc.sbuf_tensor("rows_is", [1, RT], F32))
        rtmp = en(nc.sbuf_tensor("rtmp", [1, 2 * RT], F32))
        epsc = en(nc.sbuf_tensor("epsc", [1, 2], F32))
        sel8 = en(nc.sbuf_tensor("sel8", [E, E * 128], F32))
        S = {s: en(nc.semaphore(s)) for s in SEMS}

        def bcast_row(row_ap, n, ps_ap, last_sem=True):
            sp = nsplit(n)
            for i, (a0, a1) in enumerate(sp):
                b.do("tensor", lambda e, a0=a0, a1=a1: e.matmul(
                    ps_ap[:, a0:a1], ones_f1[:, :], row_ap[:, a0:a1],
                    start=True, stop=True, skip_group_check=True),
                    "st" if (last_sem and i == len(sp) - 1) else None)

        def newton_invsqrt(var_ap, eps_ap, out_ap, t0):
            b.do("vector", lambda e: e.tensor_scalar(var_ap, var_ap,
                                                     eps_ap, None, ALU.add))
            b.wait("scalar", "sv")
            b.do("scalar", lambda e: e.activation(out=t0, in_=var_ap,
                                                  func=AF.Sqrt))
            b.wait("vector", "ss")
            b.do("vector", lambda e: e.reciprocal(out_ap, t0))
            b.do("vector", lambda e: e.tensor_mul(t0, out_ap, out_ap))
            b.do("vector", lambda e: e.tensor_mul(t0, t0, var_ap))
            b.do("vector", lambda e: e.tensor_scalar(t0, t0, -0.5, 1.5,
                                                     ALU.mult, ALU.add))
            b.do("vector", lambda e: e.tensor_mul(out_ap, out_ap, t0))

        b.do("sync", lambda e: e.dma_start(out=idf[:], in_=IDF[:, :]), "sxm", 16)
        b.do("sync", lambda e: e.dma_start(
            out=rw_f[:].rearrange("p (k e) -> p k e", k=KD),
            in_=RW.rearrange("(k p) e -> p k e", p=128)), "sxm", 16)
        b.do("sync", lambda e: e.dma_start(
            out=expb[:].rearrange("p (e f) -> p e f", e=E),
            in_=EB.rearrange("e (f p) -> p e f", p=128),
            allow_slow_non_contiguous=True), "sxm", 16)
        b.do("sync", lambda e: e.dma_start(
            out=projb[:], in_=PB.rearrange("(f p) -> p f", p=128),
            allow_slow_non_contiguous=True), "sxm", 16)
        b.do("sync", lambda e: e.dma_start(
            out=halfnw[:], in_=NW.rearrange("(f p) -> p f", p=128),
            allow_slow_non_contiguous=True), "sxm", 16)
        b.do("sync", lambda e: e.dma_start(
            out=bo_c[:], in_=BO.rearrange("(f p) -> p f", p=128),
            allow_slow_non_contiguous=True), "sxm", 16)
        b.do("sync", lambda e: e.dma_start(
            out=rows_m[0:1, 0:E], in_=RB[None, :]), "sxm", 16)
        b.do("gpsimd", lambda e: e.dma_start(
            out=bqkr[:, 0:H], in_=BQ.rearrange("(h p) -> p h", p=128),
            allow_slow_non_contiguous=True), "sw0", 16)
        b.do("gpsimd", lambda e: e.dma_start(
            out=bqkr[:, H:2 * H], in_=BK.rearrange("(h p) -> p h", p=128),
            allow_slow_non_contiguous=True), "sw0", 16)
        b.do("gpsimd", lambda e: e.dma_start(out=wmq[:], in_=WMQT[:, :]), "sw0", 16)
        b.do("gpsimd", lambda e: e.dma_start(out=wmk[:], in_=WMKT[:, :]), "sw0", 16)

        b.do("vector", lambda e: e.memset(ones_bf[:], 1.0))
        b.do("vector", lambda e: e.memset(epsc[0:1, 0:1], LN_EPS))
        b.do("vector", lambda e: e.memset(epsc[0:1, 1:2], RMS_EPS))
        b.do("vector", lambda e: e.memset(ones_f1[:], 1.0))
        b.do("vector", lambda e: e.memset(ones_fc[:], 1.0))
        b.do("sync", lambda e: e.dma_start(out=sel8[:], in_=SEL8[:, :]), "sxm", 16)

        b.wait("scalar", "sxm"); b.wait("scalar", "sw0")
        b.do("scalar", lambda e: e.mul(halfnw[:], halfnw[:], 0.5))
        b.do("scalar", lambda e: e.copy(rw_hi[:], rw_f[:]))
        b.wait("vector", "ss")
        b.do("vector", lambda e: e.tensor_sub(rw_lo[:], rw_f[:], rw_hi[:]))

        b.wait("tensor", "sxm"); b.wait("tensor", "sw0")
        b.wait("tensor", "sv"); b.wait("tensor", "ss")
        for k in range(KD):
            b.do("tensor", lambda e, k=k: e.matmul(
                ps_a[0:1, 0:E], ones_fc[:, :], rw_f[:, k * E:(k + 1) * E],
                start=(k == 0), stop=(k == KD - 1), skip_group_check=True),
                "st" if k == KD - 1 else None)
        b.do("tensor", lambda e: e.matmul(
            ps_b[0:64, 0:KHM], wmq[:, :], bqkr[:, 0:H:2],
            start=True, stop=True, skip_group_check=True), None)
        b.do("tensor", lambda e: e.matmul(
            ps_b[64:128, 0:KHM], wmq[:, :], bqkr[:, 1:H:2],
            start=True, stop=True, skip_group_check=True), None)
        b.do("tensor", lambda e: e.matmul(
            ps_b[0:1, 512:512 + 1], ones_fc[0:1, 0:1], ones_fc[0:1, 0:1],
            start=True, stop=True, skip_group_check=True), "st")
        b.wait("vector", "st")
        b.do("vector", lambda e: e.tensor_copy(rtmp[0:1, 0:E], ps_a[0:1, 0:E]))
        b.do("vector", lambda e: e.tensor_copy(bq_c[:, :], ps_b[:, 0:KHM]))
        b.wait("tensor", "sv")
        bcast_row(rtmp[0:1, 0:E], E, ps_a[:, 0:E], last_sem=False)
        bcast_row(rows_m[0:1, 0:E], E, ps_a[:, 512:512 + E], last_sem=True)
        b.wait("vector", "st")
        b.do("vector", lambda e: e.tensor_copy(sbc[:, 0:E], ps_a[:, 0:E]))
        b.do("vector", lambda e: e.tensor_copy(sbc[:, E:2 * E], ps_a[:, 512:512 + E]))
        b.barrier()

        def emit_stats(src, c0, CH, bufs, router, stats_ps, raw_ps=None,
                       xh_res=None, xkey="xb", sems=("sx0", "sx1")):
            xb, xq = bufs[xkey], bufs["xq"]
            xh = xh_res if xh_res is not None else bufs["xh"]
            xl = bufs.get("xl")

            def hsl(d, par):
                if xh_res is not None:
                    return xh[:, d * CH:(d + 1) * CH]
                return xh[:, par * CH:(par + 1) * CH]
            g_xb = bufs.setdefault("_g_xbS", [None, None])
            g_xd = bufs.setdefault("_g_xd", [None, None])
            fm_s = {}
            fm_s2 = {}
            fm_r = {}

            for d in range(KD):
                par = d & 1
                b.wait_m("sync", g_xb[par])
                b.do("sync", lambda e, d=d, par=par: e.dma_start(
                    out=xb[:, par * CH:(par + 1) * CH],
                    in_=src[d * 128:(d + 1) * 128, c0:c0 + CH]), sems[par], 16)
                b.wait("scalar", sems[par])
                b.wait_m("scalar", g_xd[par])
                b.do("scalar", lambda e, d=d, par=par: e.copy(
                    hsl(d, par),
                    xb[:, par * CH:(par + 1) * CH]))
                b.do("scalar", lambda e, par=par: e.activation(
                    out=xq[:, par * CH:(par + 1) * CH],
                    in_=xb[:, par * CH:(par + 1) * CH], func=AF.Square))
                g_xb[par] = b.mark("ss")
                if router:
                    b.wait("vector", "ss")
                    b.wait_m("vector", g_xd[par])
                    b.do("vector", lambda e, d=d, par=par: e.tensor_sub(
                        xl[:, par * CH:(par + 1) * CH],
                        xb[:, par * CH:(par + 1) * CH],
                        hsl(d, par)))
                    g_xb[par] = b.mark("sv")
                b.wait("tensor", "ss")
                b.wait("tensor", "sv")
                for (a0, a1) in nsplit(CH):
                    last_s = (d == KD - 1 and a1 == CH)
                    fs = fm_s.pop(a0, True); fm_s[a0] = False if fs else False
                    fm_s[a0] = False
                    b.do("tensor", lambda e, d=d, par=par, a0=a0, a1=a1,
                         fm=fs: e.matmul(
                        stats_ps[0:1, a0:a1], ones_bf[:, :],
                        hsl(d, par)[:, a0:a1],
                        start=fm, stop=False, skip_group_check=True), None)
                    fs2 = fm_s2.pop(a0, True)
                    fm_s2[a0] = False
                    b.do("tensor", lambda e, par=par, a0=a0, a1=a1,
                         last=last_s, fm=fs2: e.matmul(
                        stats_ps[32:33, a0:a1], ones_bf[:, :],
                        xq[:, par * CH + a0: par * CH + a1],
                        start=fm, stop=last, skip_group_check=True),
                        "st" if (a1 == CH and not router) else None)
                    if router:
                        last_r = (d == KD - 1 and a1 == CH)
                        fr = fm_r.pop(a0, True)
                        fm_r[a0] = False
                        b.do("tensor", lambda e, d=d, par=par, a0=a0, a1=a1,
                             fm=fr: e.matmul(
                            raw_ps[0:8, a0:a1], rw_hi[:, d * E:(d + 1) * E],
                            hsl(d, par)[:, a0:a1],
                            start=fm, stop=False, skip_group_check=True), None)
                        b.do("tensor", lambda e, d=d, par=par, a0=a0, a1=a1: e.matmul(
                            raw_ps[0:8, a0:a1], rw_hi[:, d * E:(d + 1) * E],
                            xl[:, par * CH + a0: par * CH + a1],
                            start=False, stop=False, skip_group_check=True), None)
                        b.do("tensor", lambda e, d=d, par=par, a0=a0, a1=a1,
                             last=last_r: e.matmul(
                            raw_ps[0:8, a0:a1], rw_lo[:, d * E:(d + 1) * E],
                            hsl(d, par)[:, a0:a1],
                            start=False, stop=last, skip_group_check=True),
                            "st" if a1 == CH else None)
                g_xd[par] = b.mark("st")

        def emit_rows(bufs, router, stats_ps, raw_ps=None, CH=None):
            b.wait("vector", "st")
            b.do("vector", lambda e: e.tensor_scalar(
                rows_m[0:1, 0:CH], stats_ps[0:1, 0:CH], 1.0 / D, None, ALU.mult))
            b.do("vector", lambda e: e.tensor_scalar(
                rtmp[0:1, 0:CH], stats_ps[32:33, 0:CH], 1.0 / D, None, ALU.mult))
            if router:
                b.do("vector", lambda e: e.tensor_copy(
                    bufs["rawf"][0:8, 0:CH], raw_ps[0:8, 0:CH]))
            b.do("vector", lambda e: e.tensor_mul(
                rtmp[0:1, CH:2 * CH], rows_m[0:1, 0:CH], rows_m[0:1, 0:CH]))
            b.do("vector", lambda e: e.tensor_sub(
                rtmp[0:1, 0:CH], rtmp[0:1, 0:CH], rtmp[0:1, CH:2 * CH]))
            newton_invsqrt(rtmp[0:1, 0:CH], epsc[0:1, 0:1], rows_is[0:1, 0:CH],
                           rtmp[0:1, CH:2 * CH])

        def emit_bcasts(bufs, CH, bc_m, bc_i):
            mbc, isbc = bufs["mbc"], bufs["isbc"]
            b.wait("tensor", "sv")
            b.wait("tensor", "ss")
            bcast_row(rows_m[0:1, 0:CH], CH, bc_m, last_sem=False)
            bcast_row(rows_is[0:1, 0:CH], CH, bc_i, last_sem=True)
            b.wait("vector", "st")
            b.do("vector", lambda e: e.tensor_copy(mbc[:, 0:CH], bc_m))
            b.do("vector", lambda e: e.tensor_copy(isbc[:, 0:CH], bc_i))
            return b.mark("sv")

        def emit_xln_tile(src, c0, CH, bufs, d, mbc_done,
                          xkey="xb", sems=("sx0", "sx1")):
            xb = bufs[xkey]
            tmpf, xln, mbc, isbc = (bufs["tmpf"], bufs["xln"],
                                    bufs["mbc"], bufs["isbc"])
            gk = "_g_xb" if xkey == "xb" else "_g_xbS"
            g_xb = bufs.setdefault(gk, [None, None])
            par = d & 1
            b.wait_m("sync", g_xb[par])
            b.do("sync", lambda e, d=d, par=par: e.dma_start(
                out=xb[:, par * CH:(par + 1) * CH],
                in_=src[d * 128:(d + 1) * 128, c0:c0 + CH]), sems[par], 16)
            b.wait("vector", sems[par])
            b.wait_m("vector", mbc_done)
            b.do("vector", lambda e, par=par: e.tensor_sub(
                tmpf[:, par * CH:(par + 1) * CH],
                xb[:, par * CH:(par + 1) * CH], mbc[:, 0:CH]))
            g_xb[par] = b.mark("sv")
            b.do("vector", lambda e, d=d, par=par: e.tensor_mul(
                xln[:, d * CH:(d + 1) * CH],
                tmpf[:, par * CH:(par + 1) * CH], isbc[:, 0:CH]))
            bufs.setdefault("_xln_marks", [None] * KD)[d] = b.mark("sv")

        def emit_xln_inplace(bufs, d, CH, mbc_done):
            tmpf, xln, mbc, isbc = (bufs["tmpf"], bufs["xln"],
                                    bufs["mbc"], bufs["isbc"])
            par = d & 1
            b.wait_m("vector", mbc_done)
            b.do("vector", lambda e, d=d, par=par: e.tensor_sub(
                tmpf[:, par * CH:(par + 1) * CH],
                xln[:, d * CH:(d + 1) * CH], mbc[:, 0:CH]))
            b.do("vector", lambda e, d=d, par=par: e.tensor_mul(
                xln[:, d * CH:(d + 1) * CH],
                tmpf[:, par * CH:(par + 1) * CH], isbc[:, 0:CH]))
            bufs.setdefault("_xln_marks", [None] * KD)[d] = b.mark("sv")

        def emit_ln(src, c0, CH, bufs, router, stats_ps, raw_ps=None,
                    bc_m=None, bc_i=None, xh_res=None, skip_xln=False):
            emit_stats(src, c0, CH, bufs, router, stats_ps, raw_ps,
                       xh_res=xh_res)
            emit_rows(bufs, router, stats_ps, raw_ps, CH=CH)
            if skip_xln:
                return bufs.setdefault("_g_xbS", [None, None])
            if bc_m is None:
                bc_m, bc_i = ps_a[:, 0:CH], ps_b[:, 0:CH]
            md = emit_bcasts(bufs, CH, bc_m, bc_i)
            for d in range(KD):
                emit_xln_tile(src, c0, CH, bufs, d, md)
            return bufs["_g_xb"]

        def phase_ffn(bufs, src, dst):
            CH = CHF
            nb = CH // 128
            NB8 = nb * E
            xln = bufs["xln"]; cb = bufs["cb"]; wbc = bufs["wbc"]
            xb = bufs["xb"]; tmpf = bufs["tmpf"]; sil = bufs["sil"]
            cc = bufs["cc"]; wst = bufs["wst"]; x1b = bufs["x1b"]
            lgt = bufs["lgt"]; wrows = bufs["wrows"]
            mbc = bufs["mbc"]; isbc = bufs["isbc"]; mstm = bufs["mstm"]
            V1 = bufs["v1"]; V2 = bufs["v2"]; W1 = bufs["w1"]; W2 = bufs["w2"]
            WKW = KD * 128

            g_wst = [None, None, None, None]
            g_ps = [None, None]
            g_sil = [None, None]
            g_cc = None
            g_x1b = [None, None]

            nch = T // CH

            def emit_router_tm():
                sl3v = lgt[:, 3 * NB8:4 * NB8]
                b.wait("tensor", "sv")
                for tb in range(nb):
                    b.do("tensor", lambda e, tb=tb: e.transpose(
                        ps_kv[:, tb * 8:(tb + 1) * 8],
                        bufs["rawf"][0:8, tb * 128:(tb + 1) * 128],
                        idf[0:8, 0:8]), "st" if tb == nb - 1 else None)
                for tb in range(nb):
                    b.do("tensor", lambda e, tb=tb: e.transpose(
                        ps_kv[:, 64 + tb * 2:64 + tb * 2 + 1],
                        rows_m[0:1, tb * 128:(tb + 1) * 128],
                        idf[0:1, 0:1]), None)
                    b.do("tensor", lambda e, tb=tb: e.transpose(
                        ps_kv[:, 64 + tb * 2 + 1:64 + tb * 2 + 2],
                        rows_is[0:1, tb * 128:(tb + 1) * 128],
                        idf[0:1, 0:1]), "st" if tb == nb - 1 else None)
                b.wait("vector", "st")
                b.do("vector", lambda e, sl3v=sl3v: e.tensor_copy(
                    sl3v, ps_kv[:, 0:NB8]))
                b.do("vector", lambda e: e.tensor_copy(
                    mstm[:, 0:2 * nb], ps_kv[:, 64:64 + 2 * nb]))

            def emit_router_chain():
                sl = [lgt[:, i * NB8:(i + 1) * NB8] for i in range(4)]
                sl3 = [s.rearrange("p (t e) -> p t e", t=nb) for s in sl]
                m_tm = mstm[:, 0:2 * nb:2][:, :, None].broadcast_to([128, nb, E])
                i_tm = mstm[:, 1:2 * nb:2][:, :, None].broadcast_to([128, nb, E])
                S_b = sbc[:, 0:E][:, None, :].broadcast_to([128, nb, E])
                rb_b = sbc[:, E:2 * E][:, None, :].broadcast_to([128, nb, E])
                b.do("vector", lambda e: e.tensor_tensor(sl3[0], m_tm, S_b, ALU.mult))
                b.do("vector", lambda e: e.tensor_sub(sl[1], sl[3], sl[0]))
                b.do("vector", lambda e: e.tensor_tensor(sl3[1], sl3[1], i_tm, ALU.mult))
                b.do("vector", lambda e: e.tensor_tensor(sl3[0], sl3[1], rb_b, ALU.add))
                b.do("vector", lambda e: e.reduce_max(V1[:, :, None], sl3[0], axis=AX.X))
                v1b = V1[:, :, None].broadcast_to([128, nb, E])
                b.do("vector", lambda e: e.tensor_tensor(sl3[1], sl3[0], v1b, ALU.is_equal))
                b.do("vector", lambda e: e.scalar_tensor_tensor(
                    sl[2], sl[1], NEG_BIG, sl[0], ALU.mult, ALU.add))
                b.do("vector", lambda e: e.reduce_max(V2[:, :, None], sl3[2], axis=AX.X))
                v2b = V2[:, :, None].broadcast_to([128, nb, E])
                b.do("vector", lambda e: e.tensor_tensor(sl3[2], sl3[2], v2b, ALU.is_equal))
                b.do("vector", lambda e: e.tensor_sub(W2[:, :], V2[:, :], V1[:, :]))
                b.wait("scalar", "sv")
                b.do("scalar", lambda e: e.activation(out=W2[:, :], in_=W2[:, :], func=AF.Exp))
                b.wait("vector", "ss")
                b.do("vector", lambda e: e.tensor_scalar(W1[:, :], W2[:, :], 1.0, None, ALU.add))
                b.do("vector", lambda e: e.reciprocal(W1[:, :], W1[:, :]))
                b.do("vector", lambda e: e.tensor_mul(W2[:, :], W2[:, :], W1[:, :]))
                w1b = W1[:, :, None].broadcast_to([128, nb, E])
                w2b = W2[:, :, None].broadcast_to([128, nb, E])
                b.do("vector", lambda e: e.tensor_tensor(sl3[1], sl3[1], w1b, ALU.mult))
                b.do("vector", lambda e: e.tensor_tensor(sl3[2], sl3[2], w2b, ALU.mult))
                b.do("vector", lambda e: e.tensor_add(sl[0], sl[1], sl[2]))
                b.wait("tensor", "sv")
                for tb in range(nb):
                    b.do("tensor", lambda e, tb=tb, sl=sl: e.transpose(
                        ps_kv[0:8, tb * 128:(tb + 1) * 128],
                        sl[0][:, tb * 8:(tb + 1) * 8], idf[:, :]),
                        "st" if tb == nb - 1 else None)
                b.wait("vector", "st")
                b.do("vector", lambda e: e.tensor_copy(
                    wrows[0:8, 0:CH], ps_kv[0:8, 0:CH]))

            ST_PS = ps_a[:, 0:CH]
            RAW_PS = ps_a[64:72, 0:CH]
            md_pend = [None]
            for ci in range(nch):
                c0 = ci * CH
                if ci == 0:
                    emit_stats(src, c0, CH, bufs, router=True,
                               stats_ps=ST_PS, raw_ps=RAW_PS,
                               xh_res=xln, xkey="xbS", sems=("sx2", "sx3"))
                    emit_rows(bufs, True, ST_PS, RAW_PS, CH=CH)
                    md_pend[0] = emit_bcasts(bufs, CH, ps_kv[:, 0:CH],
                                             ps_kv[:, 1024:1024 + CH])
                    emit_router_tm()
                    for d in range(KD):
                        emit_xln_inplace(bufs, d, CH, md_pend[0])
                    emit_router_chain()
                g_xb = bufs.setdefault("_g_xb", [None, None])

                xmarks = bufs["_xln_marks"]
                sl = [lgt[:, i * NB8:(i + 1) * NB8] for i in range(4)]
                sl3 = [s.rearrange("p (t e) -> p t e", t=nb) for s in sl]

                def expert_mm_silu(f, e_i, first_pair=False):
                    it = f * E + e_i
                    par = it & 1
                    wp = it % 3
                    reg = (ps_a, ps_b)[par]
                    b.wait_m("gpsimd", g_wst[wp])
                    b.do("gpsimd", lambda e, e_i=e_i, f=f, wp=wp: e.dma_start(
                        out=wst[:, wp * WKW:(wp + 1) * WKW]
                            .rearrange("p (k f) -> p k f", k=KD),
                        in_=EW[e_i].rearrange("(k p) f -> p k f", p=128)
                            [:, :, f * 128:(f + 1) * 128]), "sw%d" % wp, 16)
                    b.wait("tensor", "sw%d" % wp)
                    b.wait_m("tensor", g_ps[par])
                    for k in range(KD):
                        if first_pair:
                            b.wait_m("tensor", xmarks[k])
                        for (a0, a1) in nsplit(CH):
                            last = (k == KD - 1 and a1 == CH)
                            b.do("tensor", lambda e, k=k, a0=a0, a1=a1,
                                 wp=wp, reg=reg: e.matmul(
                                reg[:, a0:a1],
                                wst[:, wp * WKW + k * 128: wp * WKW + (k + 1) * 128],
                                xln[:, k * CH + a0: k * CH + a1],
                                start=(k == 0), stop=(k == KD - 1)),
                                "st" if last else None)
                    g_wst[wp] = b.mark("st")
                    b.wait("scalar", "st")
                    b.wait_m("scalar", g_sil[par])
                    b.do("scalar", lambda e, e_i=e_i, f=f, par=par, reg=reg:
                         e.activation(
                        out=sil[:, par * CH:(par + 1) * CH], in_=reg[:, 0:CH],
                        func=AF.Silu,
                        bias=expb[:, e_i * KD + f: e_i * KD + f + 1]))
                    g_ps[par] = b.mark("ss")
                    return par

                def expert_combine(e_i, par, wbc_gate, g_cc_in):
                    b.wait("vector", "ss")
                    if e_i == 0:
                        b.wait_m("vector", wbc_gate)
                        if g_cc_in is not None:
                            b.wait_m("vector", g_cc_in)
                        b.do("vector", lambda e, e_i=e_i, par=par: e.tensor_tensor(
                            cc[:, 0:CH], sil[:, par * CH:(par + 1) * CH],
                            wbc[:, e_i * CH:(e_i + 1) * CH], ALU.mult))
                    else:
                        b.do("vector", lambda e, e_i=e_i, par=par: e.tensor_tensor(
                            tmpf[:, par * CH:(par + 1) * CH],
                            sil[:, par * CH:(par + 1) * CH],
                            wbc[:, e_i * CH:(e_i + 1) * CH], ALU.mult))
                        b.do("vector", lambda e, par=par: e.tensor_add(
                            cc[:, 0:CH], cc[:, 0:CH],
                            tmpf[:, par * CH:(par + 1) * CH]))
                    g_sil[par] = b.mark("sv")

                par0 = expert_mm_silu(0, 0, first_pair=True)
                par1 = expert_mm_silu(0, 1)

                b.wait("tensor", "ss")
                g_kreg = [b.mark("sv"), b.mark("sv")]
                for e_i in range(E):
                    kreg = (ps_kv[:, 0:1024], ps_b[:, 0:1024])[e_i & 1]
                    b.wait_m("tensor", g_kreg[e_i & 1])
                    b.wait("tensor", "sv")
                    for i_n, (a0, a1) in enumerate(nsplit(CH)):
                        b.do("tensor", lambda e, e_i=e_i, a0=a0, a1=a1, kreg=kreg: e.matmul(
                            kreg[:, a0:a1], sel8[:, e_i * 128:(e_i + 1) * 128],
                            wrows[0:E, a0:a1],
                            start=True, stop=True, skip_group_check=True),
                            "st" if a1 == CH else None)
                    b.wait("vector", "st")
                    b.do("vector", lambda e, e_i=e_i, kreg=kreg: e.tensor_copy(
                        wbc[:, e_i * CH:(e_i + 1) * CH], kreg[:, 0:CH]))
                    g_kreg[e_i & 1] = b.mark("sv")
                g_ps[1] = b.mark("sv")
                wbc_done = b.mark("sv")

                expert_combine(0, par0, wbc_done, g_cc)
                expert_combine(1, par1, wbc_done, None)
                for f in range(KD):
                    e_start = 2 if f == 0 else 0
                    for e_i in range(e_start, E):
                        par = expert_mm_silu(f, e_i)
                        expert_combine(e_i, par, wbc_done, g_cc if e_i == 0 else None)
                    b.wait("scalar", "sv")
                    b.do("scalar", lambda e, f=f: e.copy(
                        cb[:, f * CH:(f + 1) * CH], cc[:, 0:CH]))
                    g_cc = b.mark("ss")
                    if f == 2 and ci + 1 < nch:
                        emit_stats(src, c0 + CH, CH, bufs, router=True,
                                   stats_ps=ps_kv[:, 0:CH],
                                   raw_ps=ps_kv[64:72, 0:CH],
                                   xkey="xbS", sems=("sx2", "sx3"))
                        emit_rows(bufs, True, ps_kv[:, 0:CH],
                                  ps_kv[64:72, 0:CH], CH=CH)
                        md_pend[0] = emit_bcasts(bufs, CH, ps_kv[:, 0:CH],
                                                 ps_kv[:, 1024:1024 + CH])
                        emit_router_tm()
                    if f == 3 and ci + 1 < nch:
                        emit_router_chain()
                cb_done = b.mark("ss")
                fm_y = {}
                ysq_mark = [None, None]
                for f2 in range(KD):
                    par = f2 & 1
                    wp = f2 % 3
                    reg = (ps_a, ps_b)[par]
                    b.wait_m("gpsimd", g_wst[wp])
                    b.do("gpsimd", lambda e, f2=f2, wp=wp: e.dma_start(
                        out=wst[:, wp * WKW:(wp + 1) * WKW]
                            .rearrange("p (k f) -> p k f", k=KD),
                        in_=PW.rearrange("(k p) f -> p k f", p=128)
                            [:, :, f2 * 128:(f2 + 1) * 128]), "sw%d" % wp, 16)
                    b.wait("tensor", "sw%d" % wp)
                    b.wait_m("tensor", g_ps[par])
                    if f2 == 0:
                        b.wait_m("tensor", cb_done)
                    for k in range(KD):
                        for (a0, a1) in nsplit(CH):
                            last = (k == KD - 1 and a1 == CH)
                            b.do("tensor", lambda e, k=k, a0=a0, a1=a1, wp=wp,
                                 reg=reg: e.matmul(
                                reg[:, a0:a1],
                                wst[:, wp * WKW + k * 128: wp * WKW + (k + 1) * 128],
                                cb[:, k * CH + a0: k * CH + a1],
                                start=(k == 0), stop=(k == KD - 1)),
                                "st" if last else None)
                    g_wst[wp] = b.mark("st")
                    b.wait("vector", "st")
                    b.do("vector", lambda e, f2=f2, par=par, reg=reg:
                         e.scalar_tensor_tensor(
                        xln[:, f2 * CH:(f2 + 1) * CH], reg[:, 0:CH],
                        projb[:, f2:f2 + 1],
                        xln[:, f2 * CH:(f2 + 1) * CH], ALU.add, ALU.add))
                    g_ps[par] = b.mark("sv")
                    if ci + 1 < nch:
                        b.wait("scalar", "sv")
                        b.do("scalar", lambda e, f2=f2, par=par, c0=c0: e.dma_start(
                            out=ysc[f2 * 128:(f2 + 1) * 128, c0:c0 + CH],
                            in_=xln[:, f2 * CH:(f2 + 1) * CH]), "sy%d" % (f2 & 1),
                            16, dma=True)
                    b.wait("scalar", "sv")
                    b.wait_m("scalar", bufs.setdefault("_g_ysq", [None, None])[par])
                    b.do("scalar", lambda e, f2=f2, par=par: e.activation(
                        out=bufs["ysq"][:, par * CH:(par + 1) * CH],
                        in_=xln[:, f2 * CH:(f2 + 1) * CH], func=AF.Square))
                    ysq_mark[par] = b.mark("ss")

                    def y2_mms(fp, last_f2):
                        b.wait_m("tensor", ysq_mark[fp])
                        b.wait("tensor", "sv")
                        for (a0, a1) in nsplit(CH):
                            lastq = (last_f2 and a1 == CH)
                            fy = fm_y.pop(a0, True)
                            fm_y[a0] = False
                            b.do("tensor", lambda e, a0=a0, a1=a1, fp=fp,
                                 fm=fy, lastq=lastq: e.matmul(
                                ps_kv[0:1, 1024 + a0:1024 + a1], ones_bf[:, :],
                                bufs["ysq"][:, fp * CH + a0: fp * CH + a1],
                                start=fm, stop=lastq, skip_group_check=True),
                                "st" if a1 == CH else None)
                        bufs["_g_ysq"][fp] = b.mark("st")

                    if f2 > 0:
                        y2_mms((f2 - 1) & 1, False)
                    if f2 == KD - 1:
                        y2_mms(f2 & 1, True)
                    for d in ([f2 - 1] if f2 > 0 else []) + \
                             ([KD - 1] if f2 == KD - 1 else []):
                        if ci + 1 < nch:
                            b.wait("vector", "sy%d" % (d & 1))
                            b.wait_m("vector", ysq_mark[d & 1])
                            emit_xln_tile(src, c0 + CH, CH, bufs, d,
                                          md_pend[0], xkey="xbS",
                                          sems=("sx2", "sx3"))
                b.wait("vector", "st")
                b.do("vector", lambda e: e.tensor_scalar(
                    rtmp[0:1, 0:CH], ps_kv[0:1, 1024:1024 + CH], 1.0 / D, None, ALU.mult))
                newton_invsqrt(rtmp[0:1, 0:CH], epsc[0:1, 1:2], rows_m[0:1, 0:CH],
                               rtmp[0:1, CH:2 * CH])
                b.wait("tensor", "sv")
                bcast_row(rows_m[0:1, 0:CH], CH, ps_kv[:, 1024:1024 + CH])
                rms_done = b.mark("st")
                b.wait("sync", "sy0")
                b.wait("sync", "sy1")
                g_yld = bufs.setdefault("_g_yld", [None, None])
                g_ysq = bufs["_g_ysq"]
                last_ch = (ci + 1 == nch)
                for f2 in range(KD):
                    par = f2 & 1
                    b.wait_m("sync", g_xb[par])
                    b.do("sync", lambda e, f2=f2, par=par, c0=c0: e.dma_start(
                        out=xb[:, par * CH:(par + 1) * CH],
                        in_=src[f2 * 128:(f2 + 1) * 128, c0:c0 + CH]), "sx%d" % par, 16)
                    if not last_ch:
                        b.wait_m("sync", g_yld[par])
                        b.wait_m("sync", g_ysq[par])
                        b.do("sync", lambda e, f2=f2, par=par, c0=c0: e.dma_start(
                            out=bufs["ysq"][:, par * CH:(par + 1) * CH],
                            in_=ysc[f2 * 128:(f2 + 1) * 128, c0:c0 + CH]),
                            "sz%d" % par, 16)
                    b.wait("vector", "sx%d" % par)
                    b.wait_m("vector", rms_done)
                    if not last_ch:
                        b.wait("vector", "sz%d" % par)
                        b.do("vector", lambda e, f2=f2, par=par: e.tensor_tensor(
                            bufs["ysq"][:, par * CH:(par + 1) * CH],
                            bufs["ysq"][:, par * CH:(par + 1) * CH],
                            ps_kv[:, 1024:1024 + CH], ALU.mult))
                    else:
                        b.wait_m("vector", g_ysq[par])
                        b.do("vector", lambda e, f2=f2, par=par: e.tensor_tensor(
                            bufs["ysq"][:, par * CH:(par + 1) * CH],
                            xln[:, f2 * CH:(f2 + 1) * CH],
                            ps_kv[:, 1024:1024 + CH], ALU.mult))
                    b.wait_m("vector", g_x1b[par])
                    b.do("vector", lambda e, f2=f2, par=par: e.scalar_tensor_tensor(
                        x1b[:, par * CH:(par + 1) * CH],
                        bufs["ysq"][:, par * CH:(par + 1) * CH],
                        halfnw[:, f2:f2 + 1],
                        xb[:, par * CH:(par + 1) * CH], ALU.mult, ALU.add))
                    g_xb[par] = b.mark("sv")
                    g_yld[par] = b.mark("sv")
                    b.wait("scalar", "sv")
                    b.do("scalar", lambda e, f2=f2, par=par, c0=c0: e.dma_start(
                        out=dst[f2 * 128:(f2 + 1) * 128, c0:c0 + CH],
                        in_=x1b[:, par * CH:(par + 1) * CH]), "so%d" % par, 16,
                        dma=True)
                    g_x1b[par] = b.mark("so%d" % par)
                g_ysq[0] = b.mark("sv")
                g_ysq[1] = b.mark("sv")

        def g_xd_dummy(bufs, par):
            g = bufs.get("_g_xd")
            return g[par] if g else None

        def phase_attn1(bufs, src):
            CH = CHP1
            nb = CH // 128
            xh = bufs["xh_res"]; kt = bufs["kt"]; vt = bufs["vt"]
            wvt = bufs["wvt"]; wke = bufs["wke"]; wqh = bufs["wqh"]
            et0 = bufs["et0"]; et1 = bufs["et1"]
            kv_sb = bufs["kv_sb"]; ksum_r = rtmp[0:1, RT:RT + HM]
            ksum_b = bufs["ksum_b"]; bv_b = bufs["bv_b"]
            strukt = bufs["strukt"]; bk_row = rtmp[0:1, RT:RT + HM]
            bkbc = bufs["bkbc"]; csk = bufs["csk"]; csv = bufs["csv"]
            kcs_b = bufs["kcs_b"]
            mst = bufs["mst"]; mstb = bufs["mstb"]
            W2c = D // 2

            b.do("gpsimd", lambda e: e.dma_start(out=bv_b[:], in_=BV[None, :]), "sw0", 16)
            b.do("vector", lambda e: e.memset(strukt[:], 0.0))
            b.wait("tensor", "sv"); b.wait("tensor", "ss")
            for h in range(H):
                b.do("tensor", lambda e, h=h: e.matmul(
                    ps_b[0:1, h * M:(h + 1) * M],
                    bqkr[:, H + h:H + h + 1], wmk[:, :],
                    start=True, stop=True, skip_group_check=True),
                    "st" if h == H - 1 else None)
            b.wait("vector", "st")
            b.do("vector", lambda e: e.tensor_copy(bk_row[:, :], ps_b[0:1, 0:HM]))
            b.wait("tensor", "sv")
            bcast_row(bk_row[0:1, :], HM, ps_b[:, 0:HM], last_sem=True)
            b.wait("vector", "st")
            b.do("vector", lambda e: e.tensor_copy(bkbc[:, :], ps_b[:, 0:HM]))
            for k in range(KD):
                b.do("gpsimd", lambda e, k=k: e.dma_start(
                    out=wvt[:, k * D:(k + 1) * D],
                    in_=WVT[k * 128:(k + 1) * 128, :]), "sw0", 16)

            def build_weff(wsrc, wmap_sb, wdst):
                g_wqh = [None, None]
                g_psa = None
                for h in range(H):
                    wp = h & 1
                    b.wait_m("gpsimd", g_wqh[wp])
                    b.do("gpsimd", lambda e, h=h, wp=wp, wsrc=wsrc: e.dma_start(
                        out=wqh[:, wp * D:(wp + 1) * D],
                        in_=wsrc[h * 128:(h + 1) * 128, :]), "sw%d" % wp, 16)
                    b.wait("tensor", "sw%d" % wp)
                    b.wait_m("tensor", g_psa)
                    if h == 0:
                        b.wait("tensor", "sv")
                        b.wait("tensor", "ss")
                    for db in range(KD):
                        b.do("tensor", lambda e, db=db, wp=wp, wmap_sb=wmap_sb: e.matmul(
                            ps_a[:, db * M:(db + 1) * M],
                            wqh[:, wp * D + db * 128: wp * D + (db + 1) * 128],
                            wmap_sb[:, :], start=True, stop=True,
                            skip_group_check=True),
                            "st" if db == KD - 1 else None)
                    g_wqh[wp] = b.mark("st")
                    b.wait("vector", "st")
                    b.do("vector", lambda e, h=h, wdst=wdst: e.tensor_copy(
                        wdst[:].rearrange("p (k hm) -> p k hm", k=KD)
                            [:, :, h * M:(h + 1) * M],
                        ps_a[:, 0:KD * M].rearrange("p (k m) -> p k m", k=KD)))
                    g_psa = b.mark("sv")
                return b.mark("sv")

            kv_first = [True, True, True, True]
            ks_first = [True, True]
            kc_first = [True, True]
            g_kt = [None, None]
            g_vt = [None, None]
            g_ra = None
            g_rb = None
            total_tb = (T // CH) * nb
            tbc = [0]

            g_xh = [None, None]
            g_store = [None, None]
            for ci in range(T // CH):
                c0 = ci * CH
                J0 = c0 // 128
                cpar = ci & 1
                xhv = xh[:, cpar * KD * CH:(cpar + 1) * KD * CH]
                b.wait_m("scalar", g_xh[cpar])
                b.wait_m("scalar", g_store[cpar])
                b.wait_m("tensor", g_ra)
                g_xb = emit_ln(src, c0, CH, bufs, router=False,
                               stats_ps=ps_a[:, 0:CH], xh_res=xhv,
                               skip_xln=True)
                if ci == 0:
                    g_ra = build_weff(WK, wmk, wke)
                    b.wait("tensor", "sv")
                    for kd in range(KD):
                        for (a0, a1) in nsplit(HM):
                            b.do("tensor", lambda e, kd=kd, a0=a0, a1=a1: e.matmul(
                                ps_b[0:1, a0:a1], ones_bf[:, :],
                                wke[:, kd * HM + a0: kd * HM + a1],
                                start=(kd == 0), stop=(kd == KD - 1),
                                skip_group_check=True),
                                "st" if (kd == KD - 1 and a1 == HM) else None)
                    b.wait("vector", "st")
                    b.do("vector", lambda e: e.tensor_copy(rtmp[0:1, 0:HM], ps_b[0:1, 0:HM]))
                    b.wait("tensor", "sv")
                    bcast_row(rtmp[0:1, 0:HM], HM, ps_b[:, 0:HM], last_sem=True)
                    b.wait("vector", "st")
                    b.do("vector", lambda e: e.tensor_copy(csk[:, :], ps_b[:, 0:HM]))
                    b.wait("tensor", "sw0")
                    b.wait("tensor", "sv")
                    for kd in range(KD):
                        for (a0, a1) in nsplit(D):
                            b.do("tensor", lambda e, kd=kd, a0=a0, a1=a1: e.matmul(
                                ps_kv[0:1, a0:a1], ones_bf[:, :],
                                wvt[:, kd * D + a0: kd * D + a1],
                                start=(kd == 0), stop=(kd == KD - 1),
                                skip_group_check=True),
                                "st" if (kd == KD - 1 and a1 == D) else None)
                    b.wait("scalar", "st")
                    b.do("scalar", lambda e: e.copy(csv[:, :], ps_kv[0:1, 0:D]))
                b.do("vector", lambda e: e.tensor_mul(
                    rtmp[0:1, CH:2 * CH], rows_m[0:1, 0:CH], rows_is[0:1, 0:CH]))
                b.do("vector", lambda e: e.tensor_scalar(
                    rtmp[0:1, CH:2 * CH], rtmp[0:1, CH:2 * CH], -1.0, None, ALU.mult))
                b.wait("tensor", "sv")
                def emit_mst_tm(J0=J0):
                    b.wait("tensor", "ss")
                    for tb in range(nb):
                        b.do("tensor", lambda e, tb=tb: e.transpose(
                            ps_b[:, 512 + 2 * tb:512 + 2 * tb + 1],
                            rows_is[0:1, tb * 128:(tb + 1) * 128],
                            idf[0:1, 0:1]), None)
                        b.do("tensor", lambda e, tb=tb: e.transpose(
                            ps_b[:, 512 + 2 * tb + 1:512 + 2 * tb + 2],
                            rtmp[0:1, CH + tb * 128:CH + (tb + 1) * 128],
                            idf[0:1, 0:1]), "st" if tb == nb - 1 else None)
                    b.wait("vector", "st")
                    b.do("vector", lambda e, J0=J0: e.tensor_copy(
                        mst[:, 2 * J0:2 * (J0 + nb)], ps_b[:, 512:512 + 2 * nb]))
                    b.wait("scalar", "sv")
                    b.do("scalar", lambda e, J0=J0: e.copy(
                        mstb[:, 2 * J0:2 * (J0 + nb)],
                        mst[:, 2 * J0:2 * (J0 + nb)]))
                b.wait("sync", "ss")
                b.do("sync", lambda e, c0=c0, xhv=xhv: e.dma_start(
                    out=xlns.rearrange("(k p) t -> p k t", p=128)[:, :, c0:c0 + CH],
                    in_=xhv.rearrange("p (k t) -> p k t", k=KD)),
                    "sxm" if cpar == 0 else "sw3", 16)
                g_store[cpar] = b.mark("sxm" if cpar == 0 else "sw3")

                for tb in range(nb):
                    tbc[0] += 1
                    last_tb = tbc[0] == total_tb
                    par = tb & 1
                    J = J0 + tb
                    a_col = mst[:, 2 * J:2 * J + 1]
                    c_col = mst[:, 2 * J + 1:2 * J + 2]
                    c_colb = mstb[:, 2 * J + 1:2 * J + 2]
                    b.wait_m("tensor", g_ra)
                    b.wait("tensor", "ss")
                    if tb == 0:
                        b.wait("tensor", "sv")
                    for d in range(KD):
                        for (a0, a1) in nsplit(HM):
                            last = (d == KD - 1 and a1 == HM)
                            b.do("tensor", lambda e, d=d, tb=tb, a0=a0, a1=a1,
                                 xhv=xhv: e.matmul(
                                ps_a[:, a0:a1],
                                xhv[:, d * CH + tb * 128: d * CH + (tb + 1) * 128],
                                wke[:, d * HM + a0: d * HM + a1],
                                start=(d == 0), stop=(d == KD - 1)),
                                "st" if last else None)
                    if tb == 0:
                        emit_mst_tm()
                    b.wait("vector", "st")
                    b.wait_m("vector", g_kt[par])
                    b.do("vector", lambda e, a_col=a_col: e.scalar_tensor_tensor(
                        et0[:, 0:HM], ps_a[:, 0:HM], a_col, bkbc[:, :],
                        ALU.mult, ALU.add))
                    b.do("vector", lambda e, c_col=c_col: e.scalar_tensor_tensor(
                        et0[:, 0:HM], csk[:, :], c_col, et0[:, 0:HM],
                        ALU.mult, ALU.add))
                    b.do("vector", lambda e: e.tensor_scalar_min(
                        et1[:, 0:HM], et0[:, 0:HM], 0.0))
                    b.wait("scalar", "sv")
                    b.do("scalar", lambda e: e.activation(
                        out=et1[:, 0:HM], in_=et1[:, 0:HM], func=AF.Exp))
                    b.wait("vector", "ss")
                    b.do("vector", lambda e, par=par: e.scalar_tensor_tensor(
                        kt[:, par * HM:(par + 1) * HM], et0[:, 0:HM], 0.0,
                        et1[:, 0:HM], ALU.max, ALU.add))
                    g_ra = b.mark("sv")
                    kt_done = b.mark("sv")
                    for half in (0, 1):
                        reg = ps_b if half == 0 else ps_a
                        b.wait_m("tensor", g_rb if half == 0 else g_ra)
                        if tb == 0 and half == 0:
                            b.wait("tensor", "sv")
                        for d in range(KD):
                            for (a0, a1) in nsplit(W2c):
                                last = (d == KD - 1 and a1 == W2c)
                                b.do("tensor", lambda e, d=d, tb=tb, a0=a0, a1=a1,
                                     half=half, reg=reg, xhv=xhv: e.matmul(
                                    reg[:, a0:a1],
                                    xhv[:, d * CH + tb * 128: d * CH + (tb + 1) * 128],
                                    wvt[:, d * D + half * W2c + a0: d * D + half * W2c + a1],
                                    start=(d == 0), stop=(d == KD - 1)),
                                    "st" if last else None)
                        b.wait("vector", "st")
                        b.wait_m("vector", g_vt[par] if half == 0 else None)
                        b.do("vector", lambda e, par=par, half=half, reg=reg,
                             a_col=a_col: e.tensor_scalar(
                            vt[:, par * D + half * W2c: par * D + (half + 1) * W2c],
                            reg[:, 0:W2c], a_col, None, ALU.mult))
                        if half == 0:
                            g_rb = b.mark("sv")
                        else:
                            g_ra = b.mark("sv")
                    vt_done = b.mark("sv")
                    b.wait_m("tensor", kt_done)
                    b.wait_m("tensor", vt_done)
                    for h in range(H):
                        bank = (h * 64) // 512
                        fm = kv_first[bank]
                        kv_first[bank] = False
                        b.do("tensor", lambda e, h=h, par=par, fm=fm: e.matmul(
                            ps_kv[:, h * 64:(h + 1) * 64],
                            vt[:, par * D + h * 128: par * D + (h + 1) * 128],
                            kt[:, par * HM + h * M: par * HM + (h + 1) * M],
                            start=fm, stop=False, skip_group_check=True), None)
                    for i_k, (a0, a1) in enumerate(nsplit(HM)):
                        lastk = (a1 == HM)
                        fmk = ks_first[i_k] if i_k < 2 else False
                        if i_k < 2:
                            ks_first[i_k] = False
                        b.do("tensor", lambda e, par=par, a0=a0, a1=a1, fmk=fmk,
                             stp=(last_tb and lastk): e.matmul(
                            ps_kv[0:1, 1024 + a0:1024 + a1], ones_bf[:, :],
                            kt[:, par * HM + a0: par * HM + a1],
                            start=fmk, stop=False, skip_group_check=True), None)
                        fmc = kc_first[i_k] if i_k < 2 else False
                        if i_k < 2:
                            kc_first[i_k] = False
                        b.do("tensor", lambda e, par=par, a0=a0, a1=a1, fmc=fmc,
                             c_colb=c_colb,
                             stp=(last_tb and lastk): e.matmul(
                            ps_kv[32:33, 1024 + a0:1024 + a1], c_colb,
                            kt[:, par * HM + a0: par * HM + a1],
                            start=fmc, stop=stp, skip_group_check=True),
                            "st" if lastk else None)
                    g_kt[par] = b.mark("st")
                    g_vt[par] = b.mark("st")
                g_xh[cpar] = b.mark("st")

            b.wait("vector", "st")
            b.do("vector", lambda e: e.tensor_copy(ksum_r[:, :], ps_kv[0:1, 1024:1024 + HM]))
            b.do("vector", lambda e: e.tensor_copy(rtmp[0:1, 0:HM], ps_kv[32:33, 1024:1024 + HM]))
            b.wait("scalar", "sv")
            b.do("scalar", lambda e: e.copy(ksum_b[:, :], ksum_r[:, :]))
            b.do("scalar", lambda e: e.copy(kcs_b[:, :], rtmp[0:1, 0:HM]))
            b.wait("tensor", "ss")
            for h in range(H):
                b.do("tensor", lambda e, h=h: e.matmul(
                    ps_kv[:, h * 64:(h + 1) * 64],
                    bv_b[0:1, h * 128:(h + 1) * 128],
                    ksum_b[0:1, h * M:(h + 1) * M],
                    start=False, stop=False, skip_group_check=True), None)
                b.do("tensor", lambda e, h=h: e.matmul(
                    ps_kv[:, h * 64:(h + 1) * 64],
                    csv[0:1, h * 128:(h + 1) * 128],
                    kcs_b[0:1, h * M:(h + 1) * M],
                    start=False, stop=(h == H - 1), skip_group_check=True),
                    "st" if h == H - 1 else None)
            b.wait("vector", "st")
            b.do("vector", lambda e: e.tensor_copy(kv_sb[:, :], ps_kv[:, 0:HM]))
            b.wait("tensor", "sv")
            b.wait("tensor", "ss")
            for j in range(KHM):
                b.do("tensor", lambda e, j=j: e.transpose(
                    ps_a[:, j:j + 1], ksum_r[0:1, j * 128:(j + 1) * 128],
                    idf[0:1, 0:1]), "st" if j == KHM - 1 else None)
            b.wait("vector", "st")
            for j in range(KHM):
                b.do("vector", lambda e, j=j: e.tensor_copy(
                    strukt[0:64, j * H + 2 * j: j * H + 2 * j + 1],
                    ps_a[0:64, j:j + 1]))
                b.do("vector", lambda e, j=j: e.tensor_copy(
                    strukt[64:128, j * H + 2 * j + 1: j * H + 2 * j + 2],
                    ps_a[64:128, j:j + 1]))

        def phase_attn2(bufs, src, dst):
            CH = CHP2
            nb2 = CH // 128
            xh = bufs["xh"]; qe = bufs["qe"]; kvo = bufs["kvo"]
            wqe = bufs["wqe"]; wqh = bufs["wqh"]
            ninv = bufs["ninv"]; x2b = bufs["x2b"]; qtmp = bufs["qtmp"]
            xb = bufs["xb"]; tmpf = bufs["tmpf"]
            abc = bufs["abc"]; cbc = bufs["cbc"]; csq = bufs["csq"]
            kv_sb = bufs["kv_sb"]; strukt = bufs["strukt"]; sel = bufs["sel"]
            mst = bufs["mst"]
            b.do("sync", lambda e: e.dma_start(out=sel[:], in_=SELH[:, :]), "sxm", 16)
            b.wait("tensor", "sxm")
            g_wqh = [None, None]
            g_psa = None
            for h in range(H):
                wp = h & 1
                b.wait_m("gpsimd", g_wqh[wp])
                b.do("gpsimd", lambda e, h=h, wp=wp: e.dma_start(
                    out=wqh[:, wp * D:(wp + 1) * D],
                    in_=WQ[h * 128:(h + 1) * 128, :]), "sw%d" % wp, 16)
                b.wait("tensor", "sw%d" % wp)
                b.wait_m("tensor", g_psa)
                for db in range(KD):
                    b.do("tensor", lambda e, db=db, wp=wp: e.matmul(
                        ps_a[:, db * M:(db + 1) * M],
                        wqh[:, wp * D + db * 128: wp * D + (db + 1) * 128],
                        wmq[:, :], start=True, stop=True, skip_group_check=True),
                        "st" if db == KD - 1 else None)
                g_wqh[wp] = b.mark("st")
                b.wait("vector", "st")
                b.do("vector", lambda e, h=h: e.tensor_copy(
                    wqe[:].rearrange("p (k hm) -> p k hm", k=KD)
                        [:, :, h * M:(h + 1) * M],
                    ps_a[:, 0:KD * M].rearrange("p (k m) -> p k m", k=KD)))
                g_psa = b.mark("sv")
            b.wait_m("tensor", g_psa)
            for kd in range(KD):
                for (a0, a1) in nsplit(HM):
                    b.do("tensor", lambda e, kd=kd, a0=a0, a1=a1: e.matmul(
                        ps_b[0:1, a0:a1], ones_bf[:, :],
                        wqe[:, kd * HM + a0: kd * HM + a1],
                        start=(kd == 0), stop=(kd == KD - 1),
                        skip_group_check=True),
                        "st" if (kd == KD - 1 and a1 == HM) else None)
            b.wait("vector", "st")
            b.do("vector", lambda e: e.tensor_copy(rtmp[0:1, 0:HM], ps_b[0:1, 0:HM]))
            b.wait("tensor", "sv")
            for j in range(KHM):
                b.do("tensor", lambda e, j=j: e.transpose(
                    ps_b[:, 512 + j:512 + j + 1],
                    rtmp[0:1, j * 128:(j + 1) * 128],
                    idf[0:1, 0:1]), "st" if j == KHM - 1 else None)
            b.wait("vector", "st")
            b.do("vector", lambda e: e.tensor_copy(csq[:, :], ps_b[:, 512:512 + KHM]))

            PC = D // 512
            g_regE = None
            g_regO = None
            for h in range(H):
                wp = h & 1
                j = h // 2
                r0 = (h & 1) * 64
                b.wait_m("gpsimd", g_wqh[wp])
                b.do("gpsimd", lambda e, h=h, wp=wp: e.dma_start(
                    out=wqh[:, wp * D:(wp + 1) * D],
                    in_=WOT[h * 128:(h + 1) * 128, :]), "sw%d" % wp, 16)
                b.wait("tensor", "sw%d" % wp)
                b.wait("tensor", "sv")
                if r0 == 0:
                    b.wait_m("tensor", g_regE if j % 2 == 0 else g_regO)
                for i in range(PC):
                    if j % 2 == 0:
                        reg = ps_kv[:, i * 512:(i + 1) * 512]
                    else:
                        reg = (ps_a if i < 2 else ps_b)[:, (i % 2) * 512:
                                                        ((i % 2) + 1) * 512]
                    b.do("tensor", lambda e, h=h, wp=wp, i=i, r0=r0, reg=reg:
                         e.matmul(
                        reg[r0:r0 + 64, 0:512],
                        kv_sb[:, h * 64:(h + 1) * 64],
                        wqh[:, wp * D + i * 512: wp * D + (i + 1) * 512],
                        start=True, stop=True, skip_group_check=True),
                        "st" if i == PC - 1 else None)
                g_wqh[wp] = b.mark("st")
                if h & 1:
                    if j % 2 == 0:
                        b.wait("vector", "st")
                        b.do("vector", lambda e, j=j: e.tensor_copy(
                            kvo[:, j * D:(j + 1) * D], ps_kv[:, 0:D]))
                        g_regE = b.mark("sv")
                    else:
                        b.wait("scalar", "st")
                        for i in range(PC):
                            regc = (ps_a if i < 2 else ps_b)[:, (i % 2) * 512:
                                                             ((i % 2) + 1) * 512]
                            b.do("scalar", lambda e, j=j, i=i, regc=regc: e.copy(
                                kvo[:, j * D + i * 512: j * D + (i + 1) * 512],
                                regc[:, :]))
                        g_regO = b.mark("ss")
            b.wait("tensor", "sv")
            b.wait("tensor", "ss")

            slots = [ps_a[:, 0:512], ps_a[:, 512:1024],
                     ps_b[:, 0:512], ps_b[:, 512:1024]]
            g_reg = [None, None, b.mark("sv"), b.mark("sv")]
            g_qt = [None, None]
            g_x2b = [None, None]
            g_at = [None]
            g_ninv = None
            g_tm = [None, None]
            g_xb = [None, None]
            g_qps = [None, None]
            NCH2 = T // CH

            def emit_ac_rows(c):
                par = c & 1
                J0 = (c * CH) // 128
                po = par * CH
                b.wait_m("tensor", g_qps[0])
                b.wait_m("tensor", g_qps[1])
                b.wait("tensor", "sv")
                for tb in range(nb2):
                    b.do("tensor", lambda e, tb=tb, J0=J0: e.transpose(
                        ps_kv[0:1, tb * 128:(tb + 1) * 128],
                        mst[:, 2 * (J0 + tb):2 * (J0 + tb) + 1],
                        idf[:, :]), None)
                    b.do("tensor", lambda e, tb=tb, J0=J0: e.transpose(
                        ps_kv[0:1, 512 + tb * 128:512 + (tb + 1) * 128],
                        mst[:, 2 * (J0 + tb) + 1:2 * (J0 + tb) + 2],
                        idf[:, :]), "st" if tb == nb2 - 1 else None)
                b.wait("vector", "st")
                b.do("vector", lambda e: e.tensor_copy(
                    rtmp[0:1, 0:CH], ps_kv[0:1, 0:CH]))
                b.do("vector", lambda e: e.tensor_copy(
                    rtmp[0:1, CH:2 * CH], ps_kv[0:1, 512:512 + CH]))
                b.wait("tensor", "sv")
                bcast_row(rtmp[0:1, 0:CH], CH, ps_kv[:, 0:CH], last_sem=False)
                bcast_row(rtmp[0:1, CH:2 * CH], CH, ps_kv[:, 512:512 + CH],
                          last_sem=True)
                b.wait("vector", "st")
                b.do("vector", lambda e, po=po: e.tensor_copy(
                    abc[:, po:po + CH], ps_kv[:, 0:CH]))
                b.do("vector", lambda e, po=po: e.tensor_copy(
                    cbc[:, po:po + CH], ps_kv[:, 512:512 + CH]))
                g_qps[0] = b.mark("sv")
                g_qps[1] = b.mark("sv")

            def emit_xh_load(c):
                b.wait_m("sync", g_at[0])
                b.do("sync", lambda e, c0=c * CH: e.dma_start(
                    out=xh[:].rearrange("p (k t) -> p k t", k=KD),
                    in_=xlns.rearrange("(k p) t -> p k t", p=128)
                        [:, :, c0:c0 + CH]), "sxl", 16)

            def emit_q_block(c, j):
                par = c & 1
                qoff = par * KHM * CH
                po = par * CH
                half = j & 1
                reg = ps_kv[:, half * 512:half * 512 + CH]
                b.wait_m("tensor", g_qps[half])
                b.wait("tensor", "sxl")
                for d in range(KD):
                    b.do("tensor", lambda e, j=j, d=d, reg=reg: e.matmul(
                        reg[:, 0:CH],
                        wqe[:, d * HM + j * 128: d * HM + (j + 1) * 128],
                        xh[:, d * CH:(d + 1) * CH],
                        start=(d == 0), stop=(d == KD - 1)),
                        "st" if d == KD - 1 else None)
                if j == KHM - 1:
                    g_at[0] = b.mark("st")
                b.wait("vector", "st")
                b.wait_m("vector", g_qt[half])
                jp = half * CH
                b.do("vector", lambda e, reg=reg, jp=jp, po=po: e.tensor_tensor(
                    qtmp[:, jp:jp + CH], reg[:, 0:CH], abc[:, po:po + CH],
                    ALU.mult))
                b.do("vector", lambda e, j=j, jp=jp, po=po: e.scalar_tensor_tensor(
                    qtmp[:, jp:jp + CH], cbc[:, po:po + CH], csq[:, j:j + 1],
                    qtmp[:, jp:jp + CH], ALU.mult, ALU.add))
                g_qps[half] = b.mark("sv")
                b.wait_m("vector", g_tm[half])
                b.do("vector", lambda e, j=j, jp=jp: e.tensor_scalar(
                    tmpf[:, jp:jp + CH], qtmp[:, jp:jp + CH],
                    bq_c[:, j:j + 1], 0.0, ALU.add, ALU.min))
                b.wait("scalar", "sv")
                b.do("scalar", lambda e, jp=jp: e.activation(
                    out=tmpf[:, jp:jp + CH], in_=tmpf[:, jp:jp + CH], func=AF.Exp))
                b.wait("vector", "ss")
                b.do("vector", lambda e, j=j, jp=jp: e.tensor_scalar(
                    qtmp[:, jp:jp + CH], qtmp[:, jp:jp + CH],
                    bq_c[:, j:j + 1], 0.0, ALU.add, ALU.max))
                b.do("vector", lambda e, j=j, jp=jp, qoff=qoff: e.tensor_add(
                    qe[:, qoff + j * CH:qoff + (j + 1) * CH],
                    tmpf[:, jp:jp + CH], qtmp[:, jp:jp + CH]))
                g_qt[half] = b.mark("sv")
                g_tm[half] = b.mark("sv")

            def emit_norm(c, qed):
                nonlocal g_ninv
                qoff = (c & 1) * KHM * CH
                b.wait_m("tensor", qed)
                b.wait_m("tensor", g_ninv)
                for j in range(KHM):
                    b.do("tensor", lambda e, j=j, qoff=qoff: e.matmul(
                        ps_kv[0:H, 0:CH], strukt[:, j * H:(j + 1) * H],
                        qe[:, qoff + j * CH:qoff + (j + 1) * CH],
                        start=(j == 0), stop=(j == KHM - 1),
                        skip_group_check=True),
                        "st" if j == KHM - 1 else None)
                b.wait("vector", "st")
                b.do("vector", lambda e: e.tensor_scalar(
                    ninv[:, 0:CH], ps_kv[0:H, 0:CH], ATT_EPS, None, ALU.add))
                b.do("vector", lambda e: e.reciprocal(ninv[:, 0:CH], ninv[:, 0:CH]))
                g_ninv = b.mark("sv")

            emit_ac_rows(0)
            emit_xh_load(0)
            for j in range(KHM):
                emit_q_block(0, j)
            qe_done = b.mark("sv")
            emit_norm(0, qe_done)

            for ci in range(NCH2):
                c0 = ci * CH
                par = ci & 1
                qoff = par * KHM * CH
                b.wait_m("tensor", g_ninv)
                for j in range(KHM):
                    slot = j % 4
                    reg = slots[slot]
                    b.wait_m("tensor", g_reg[slot])
                    b.do("tensor", lambda e, j=j, reg=reg: e.matmul(
                        reg[:, 0:CH], sel[:, j * 128:(j + 1) * 128], ninv[:, 0:CH],
                        start=True, stop=True, skip_group_check=True), "st")
                    b.wait("vector", "st")
                    b.do("vector", lambda e, j=j, reg=reg, qoff=qoff: e.tensor_tensor(
                        qe[:, qoff + j * CH:qoff + (j + 1) * CH],
                        qe[:, qoff + j * CH:qoff + (j + 1) * CH],
                        reg[:, 0:CH], ALU.mult))
                    g_reg[slot] = b.mark("sv")
                qn_done = b.mark("sv")
                b.wait_m("tensor", qn_done)
                for f2 in range(KD):
                    slot = f2 % 4
                    reg = slots[slot]
                    b.wait_m("tensor", g_reg[slot])
                    for jj in range(KHM):
                        b.do("tensor", lambda e, jj=jj, f2=f2, reg=reg, qoff=qoff:
                             e.matmul(
                            reg[:, 0:CH],
                            kvo[:, jj * D + f2 * 128: jj * D + f2 * 128 + 128],
                            qe[:, qoff + jj * CH:qoff + (jj + 1) * CH],
                            start=(jj == 0), stop=(jj == KHM - 1)),
                            "st" if jj == KHM - 1 else None)
                    pr = f2 & 1
                    b.wait_m("sync", g_xb[pr])
                    b.do("sync", lambda e, f2=f2, pr=pr, c0=c0: e.dma_start(
                        out=xb[:, pr * CH:(pr + 1) * CH],
                        in_=src[f2 * 128:(f2 + 1) * 128, c0:c0 + CH]),
                        "sx%d" % pr, 16)
                    b.wait("vector", "sx%d" % pr)
                    b.wait("vector", "st")
                    b.wait_m("vector", g_x2b[pr])
                    b.do("vector", lambda e, f2=f2, pr=pr, reg=reg:
                         e.scalar_tensor_tensor(
                        x2b[:, pr * CH:(pr + 1) * CH], reg[:, 0:CH],
                        bo_c[:, f2:f2 + 1], xb[:, pr * CH:(pr + 1) * CH],
                        ALU.add, ALU.add))
                    g_xb[pr] = b.mark("sv")
                    g_reg[slot] = b.mark("sv")
                    b.wait("scalar", "sv")
                    b.do("scalar", lambda e, f2=f2, pr=pr, c0=c0: e.dma_start(
                        out=dst[f2 * 128:(f2 + 1) * 128, c0:c0 + CH],
                        in_=x2b[:, pr * CH:(pr + 1) * CH]), "so%d" % pr, 16,
                        dma=True)
                    g_x2b[pr] = b.mark("so%d" % pr)
                    if ci + 1 < NCH2:
                        if f2 == 0:
                            emit_ac_rows(ci + 1)
                            emit_xh_load(ci + 1)
                        jh = f2 - 1
                        if 0 <= jh < KHM:
                            emit_q_block(ci + 1, jh)
                            if jh == KHM - 1:
                                qe_done = b.mark("sv")
                        if f2 == KHM + 1:
                            emit_norm(ci + 1, qe_done)

        def make_ffn_bufs(pfx):
            mgr = []
            def t(name, shape, dt):
                g = nc.sbuf_tensor(pfx + name, shape, dt)
                h = g.__enter__()
                mgr.append(g)
                return h
            bufs = dict(
                xln=t("xln", [128, KD * CHF], BF16),
                cb=t("cb", [128, KD * CHF], BF16),
                wbc=t("wbc", [128, E * CHF], F32),
                xb=t("xb", [128, 2 * CHF], F32),
                xh=t("xh", [128, 2 * CHF], BF16),
                xl=t("xl", [128, 2 * CHF], BF16),
                xq=t("xq", [128, 2 * CHF], BF16),
                tmpf=t("tmpf", [128, 2 * CHF], F32),
                sil=t("sil", [128, 2 * CHF], F32),
                cc=t("cc", [128, CHF], F32),
                wst=t("wst", [128, 3 * KD * 128], BF16),
                mbc=t("mbc", [128, CHF], BF16),
                isbc=t("isbc", [128, CHF], BF16),
                x1b=t("x1b", [128, 2 * CHF], F32),
                xbS=t("xbS", [128, 2 * CHF], F32),
                ysq=t("ysq", [128, 2 * CHF], BF16),
                lgt=t("lgt", [128, 4 * (CHF // 128) * E], F32),
                wrows=t("wrows", [8, CHF], F32),
                rawf=t("rawf", [8, CHF], F32),
                mstm=t("mstm", [128, 2 * (CHF // 128)], F32),
                v1=t("v1", [128, CHF // 128], F32),
                v2=t("v2", [128, CHF // 128], F32),
                w1=t("w1", [128, CHF // 128], F32),
                w2=t("w2", [128, CHF // 128], F32),
            )
            return bufs, mgr

        def close_bufs(mgr):
            for g in reversed(mgr):
                g.__exit__(None, None, None)

        fb, mgr = make_ffn_bufs("f_")
        phase_ffn(fb, src=x0, dst=x1s)
        close_bufs(mgr)
        b.barrier()

        amgr = []
        def at_t(name, shape, dt):
            g = nc.sbuf_tensor(name, shape, dt)
            h = g.__enter__()
            amgr.append(g)
            return h
        kv_sb = at_t("kv_sb", [128, HM], BF16)
        ksum_b = at_t("ksum_b", [1, HM], BF16)
        bv_b = at_t("bv_b", [1, D], BF16)
        strukt = at_t("strukt", [128, KHM * H], BF16)
        mst_tm = at_t("mst_tm", [128, 2 * (T // 128)], F32)
        mst_tmb = at_t("mst_tmb", [128, 2 * (T // 128)], BF16)

        p1mgr = []
        def p1_t(name, shape, dt):
            g = nc.sbuf_tensor(name, shape, dt)
            h = g.__enter__()
            p1mgr.append(g)
            return h
        p1 = dict(
            wvt=p1_t("p1_wvt", [128, KD * D], BF16),
            wke=p1_t("p1_wke", [128, KD * HM], BF16),
            wqh=p1_t("p1_wqh", [128, 2 * D], BF16),
            xh_res=p1_t("p1_xh_res", [128, 2 * KD * CHP1], BF16),
            xb=p1_t("p1_xb", [128, 2 * CHP1], F32),
            xq=p1_t("p1_xq", [128, 2 * CHP1], BF16),
            tmpf=p1_t("p1_tmpf", [128, 2 * CHP1], F32),
            kt=p1_t("p1_kt", [128, 2 * HM], BF16),
            vt=p1_t("p1_vt", [128, 2 * D], BF16),
            et0=p1_t("p1_et0", [128, HM], BF16),
            bkbc=p1_t("p1_bkbc", [128, HM], F32),
            csk=p1_t("p1_csk", [128, HM], BF16),
            csv=p1_t("p1_csv", [1, D], BF16),
            kcs_b=p1_t("p1_kcs_b", [1, HM], BF16),
            kv_sb=kv_sb, ksum_b=ksum_b, bv_b=bv_b,
            strukt=strukt, mst=mst_tm, mstb=mst_tmb,
        )
        p1["et1"] = p1["tmpf"]
        phase_attn1(p1, src=x1s)
        close_bufs(p1mgr)
        b.barrier()

        p2mgr = []
        def p2_t(name, shape, dt):
            g = nc.sbuf_tensor(name, shape, dt)
            h = g.__enter__()
            p2mgr.append(g)
            return h
        p2_xh = p2_t("p2_xh", [128, max(KD, H) * CHP2], BF16)
        p2 = dict(
            kvo=p2_t("p2_kvo", [128, KHM * D], BF16),
            wqe=p2_t("p2_wqe", [128, KD * HM], BF16),
            wqh=p2_t("p2_wqh", [128, 2 * D], BF16),
            xh=p2_xh,
            xb=p2_t("p2_xb", [128, 2 * CHP2], F32),
            tmpf=p2_t("p2_tmpf", [128, 2 * CHP2], F32),
            qe=p2_t("p2_qe", [128, 2 * KHM * CHP2], BF16),
            ninv=p2_t("p2_ninv", [H, CHP2], F32),
            x2b=p2_t("p2_x2b", [128, 2 * CHP2], F32),
            qtmp=p2_t("p2_qtmp", [128, 2 * CHP2], F32),
            sel=p2_t("p2_sel", [H, KHM * 128], F32),
            abc=p2_t("p2_abc", [128, 2 * CHP2], F32),
            cbc=p2_t("p2_cbc", [128, 2 * CHP2], F32),
            csq=p2_t("p2_csq", [128, KHM], F32),
            kv_sb=kv_sb, strukt=strukt, mst=mst_tm,
        )
        phase_attn2(p2, src=x1s, dst=x2s)
        close_bufs(p2mgr)
        close_bufs(amgr)
        b.barrier()

        gb, mgr2 = make_ffn_bufs("g_")
        phase_ffn(gb, src=x2s, dst=out)
        close_bufs(mgr2)

        for s in SEMS:
            if not s.startswith("so"):
                b.wait("sync", s)
        b.wait("sync", "so0")
        b.wait("sync", "so1")

        with nc.Block() as block:
            @block.sync
            def _(e):
                for fn in b.ops["sync"]:
                    fn(e, S)

            @block.gpsimd
            def _(e):
                for fn in b.ops["gpsimd"]:
                    fn(e, S)

            @block.scalar
            def _(e):
                for fn in b.ops["scalar"]:
                    fn(e, S)

            @block.vector
            def _(e):
                for fn in b.ops["vector"]:
                    fn(e, S)

            @block.tensor
            def _(e):
                for fn in b.ops["tensor"]:
                    fn(e, S)

    nc.finalize()
    return nc


CFG_FULL = dict(D=2048, T=2048, H=16, M=64, E=8, CHF=1024, CHP1=512, CHP2=512)

_nc_cache = {}


def host_weights(inputs):
    g = lambda k: np.ascontiguousarray(np.asarray(inputs[k], np.float32))
    w = {}
    w["expert_w"] = g("L0_ffn_expert_w")
    w["expert_b"] = g("L0_ffn_expert_b")
    w["router_w"] = g("L0_ffn_router_w")
    w["router_b"] = g("L0_ffn_router_b")
    w["proj_w"] = g("L0_ffn_proj_w")
    w["proj_b"] = g("L0_ffn_proj_b")
    w["norm_w"] = g("L0_ffn_norm_w")
    w["wq"] = g("L0_attn_wq")
    w["bq"] = g("L0_attn_bq")
    w["wk"] = g("L0_attn_wk")
    w["bk"] = g("L0_attn_bk")
    w["wvT"] = np.ascontiguousarray(g("L0_attn_wv").T)
    w["bv"] = g("L0_attn_bv")
    w["woT"] = np.ascontiguousarray(g("L0_attn_wo").T)
    w["bo"] = g("L0_attn_bo")
    w["wmap_qT"] = np.ascontiguousarray(g("L0_attn_wmap_q").T)
    w["wmap_kT"] = np.ascontiguousarray(g("L0_attn_wmap_k").T)
    w["idf32"] = np.eye(128, dtype=np.float32)
    E = 8
    sel8 = np.zeros((E, E * 128), np.float32)
    for e in range(E):
        sel8[e, e * 128:(e + 1) * 128] = 1.0
    w["sel8_c"] = sel8
    H = 16
    KHM = (H * 64) // 128
    selh = np.zeros((H, KHM * 128), np.float32)
    for j in range(KHM):
        selh[2 * j, j * 128: j * 128 + 64] = 1.0
        selh[2 * j + 1, j * 128 + 64: (j + 1) * 128] = 1.0
    w["selh_c"] = selh
    return w


def kernel(**inputs):
    cfg = CFG_FULL
    if "full" not in _nc_cache:
        _nc_cache["full"] = build_program(cfg)
    nc = _nc_cache["full"]

    x = np.asarray(inputs["L0_attn_hidden_states"], dtype=np.float32)
    B = x.shape[0]
    w = host_weights(inputs)
    in_maps = []
    for c in range(B):
        m = {"x0": np.ascontiguousarray(x[c].T)}
        m.update(w)
        in_maps.append(m)

    res = run_bass_kernel_spmd(nc, in_maps, core_ids=list(range(B)))
    outs = [np.ascontiguousarray(np.asarray(r["out_fm"]).T) for r in res.results]
    return np.stack(outs, axis=0).astype(np.float32)


# version 32
# speedup vs baseline: 1.0534x; 1.0534x over previous
import contextlib
import numpy as np
import ml_dtypes
import concourse.bass as bass
import concourse.mybir as mybir
from concourse.bass_utils import run_bass_kernel_spmd

F32 = mybir.dt.float32
BF16 = mybir.dt.bfloat16
AF = mybir.ActivationFunctionType
ALU = mybir.AluOpType
AX = mybir.AxisListType

ENGINES = ("sync", "gpsimd", "scalar", "vector", "tensor")
SEMS = ("sx0", "sx1", "sx2", "sx3", "sy0", "sy1", "sz0", "sz1", "sxm", "sxl", "sw0", "sw1", "sw2", "sw3", "so0", "so1", "st", "sv", "ss")

LN_EPS = 1e-5
ATT_EPS = 1e-6
RMS_EPS = 1e-6
NEG_BIG = -1e30


class Sched:

    AUTO = {"vector": "sv", "scalar": "ss"}

    def __init__(self):
        self.ops = {e: [] for e in ENGINES}
        self.cnt = {s: 0 for s in SEMS}

    def do(self, eng, build, sem=None, inc=1, dma=False):
        if eng in self.AUTO and not dma:
            own = self.AUTO[eng]
            prev = self.cnt[own]
            if prev > 0:
                self.ops[eng].append(
                    lambda e, S, s=own, v=prev: e.wait_ge(S[s], v))
            self.cnt[own] += 1
            self.ops[eng].append(
                lambda e, S, b=build, s=own: b(e).then_inc(S[s], 1))
            return self.cnt[own]
        if sem is None:
            self.ops[eng].append(lambda e, S, b=build: b(e))
            return None
        self.cnt[sem] += inc
        self.ops[eng].append(
            lambda e, S, b=build, s=sem, i=inc: b(e).then_inc(S[s], i))
        return self.cnt[sem]

    def wait(self, eng, sem, val=None):
        v = self.cnt[sem] if val is None else val
        if v > 0:
            self.ops[eng].append(lambda e, S, s=sem, v=v: e.wait_ge(S[s], v))

    def mark(self, sem):
        return (sem, self.cnt[sem])

    def wait_m(self, eng, mark):
        if mark is not None:
            self.wait(eng, mark[0], mark[1])

    def barrier(self):
        snap = dict(self.cnt)
        for eng in ENGINES:
            for s, v in snap.items():
                if v > 0:
                    self.wait(eng, s, v)


def build_program(cfg):
    D = cfg["D"]; T = cfg["T"]; H = cfg["H"]; M = cfg["M"]; E = cfg["E"]
    CHF = cfg["CHF"]; CHP1 = cfg["CHP1"]; CHP2 = cfg["CHP2"]
    KD = D // 128
    HM = H * M
    KHM = HM // 128
    assert H * 128 == D and M == 64 and E == 8

    def nsplit(n):
        out = []
        c = 0
        while c < n:
            w = min(512, n - c)
            out.append((c, c + w))
            c += w
        return out

    nc = bass.Bass("TRN2", target_bir_lowering=False)

    x0 = nc.dram_tensor("x0", [D, T], F32, kind="ExternalInput").ap()
    out = nc.dram_tensor("out_fm", [D, T], F32, kind="ExternalOutput").ap()
    x1s = nc.dram_tensor("x1s", [D, T], F32, kind="Internal").ap()
    x2s = nc.dram_tensor("x2s", [D, T], F32, kind="Internal").ap()
    xlns = nc.dram_tensor("xlns", [D, T], BF16, kind="Internal").ap()
    ysc = nc.dram_tensor("ysc", [D, T], BF16, kind="Internal").ap()

    EW = nc.dram_tensor("expert_w", [E, D, D], F32, kind="ExternalInput").ap()
    EB = nc.dram_tensor("expert_b", [E, D], F32, kind="ExternalInput").ap()
    RW = nc.dram_tensor("router_w", [D, E], F32, kind="ExternalInput").ap()
    RB = nc.dram_tensor("router_b", [E], F32, kind="ExternalInput").ap()
    PW = nc.dram_tensor("proj_w", [D, D], F32, kind="ExternalInput").ap()
    PB = nc.dram_tensor("proj_b", [D], F32, kind="ExternalInput").ap()
    NW = nc.dram_tensor("norm_w", [D], F32, kind="ExternalInput").ap()
    WQ = nc.dram_tensor("wq", [D, D], F32, kind="ExternalInput").ap()
    BQ = nc.dram_tensor("bq", [D], F32, kind="ExternalInput").ap()
    WK = nc.dram_tensor("wk", [D, D], F32, kind="ExternalInput").ap()
    BK = nc.dram_tensor("bk", [D], F32, kind="ExternalInput").ap()
    WVT = nc.dram_tensor("wvT", [D, D], F32, kind="ExternalInput").ap()
    BV = nc.dram_tensor("bv", [D], F32, kind="ExternalInput").ap()
    WOT = nc.dram_tensor("woT", [D, D], F32, kind="ExternalInput").ap()
    BO = nc.dram_tensor("bo", [D], F32, kind="ExternalInput").ap()
    WMQT = nc.dram_tensor("wmap_qT", [128, M], F32, kind="ExternalInput").ap()
    WMKT = nc.dram_tensor("wmap_kT", [128, M], F32, kind="ExternalInput").ap()
    IDF = nc.dram_tensor("idf32", [128, 128], F32, kind="ExternalInput").ap()
    SEL8 = nc.dram_tensor("sel8_c", [E, E * 128], F32, kind="ExternalInput").ap()
    SELH = nc.dram_tensor("selh_c", [H, KHM * 128], F32, kind="ExternalInput").ap()

    b = Sched()
    RT = max(CHF, CHP1, CHP2)

    with contextlib.ExitStack() as _stk:
        en = _stk.enter_context
        ps_kv = en(nc.psum_tensor("ps_kv", [128, 2048], F32))
        ps_a = en(nc.psum_tensor("ps_a", [128, 1024], F32))
        ps_b = en(nc.psum_tensor("ps_b", [128, 1024], F32))
        idf = en(nc.sbuf_tensor("idf_sb", [128, 128], F32))
        ones_bf = en(nc.sbuf_tensor("ones_bf", [128, 1], BF16))
        ones_f1 = en(nc.sbuf_tensor("ones_f1", [1, 128], F32))
        ones_fc = en(nc.sbuf_tensor("ones_fc", [128, 1], F32))
        rw_f = en(nc.sbuf_tensor("rw_f", [128, KD * E], F32))
        rw_hi = en(nc.sbuf_tensor("rw_hi", [128, KD * E], BF16))
        rw_lo = en(nc.sbuf_tensor("rw_lo", [128, KD * E], BF16))
        expb = en(nc.sbuf_tensor("expb", [128, E * KD], F32))
        projb = en(nc.sbuf_tensor("projb", [128, KD], F32))
        halfnw = en(nc.sbuf_tensor("halfnw", [128, KD], F32))
        bo_c = en(nc.sbuf_tensor("bo_c", [128, KD], F32))
        sbc = en(nc.sbuf_tensor("sbc", [128, 2 * E], F32))
        wmq = en(nc.sbuf_tensor("wmq", [128, M], BF16))
        wmk = en(nc.sbuf_tensor("wmk", [128, M], BF16))
        bqkr = en(nc.sbuf_tensor("bqkr", [128, 2 * H], BF16))
        bq_c = en(nc.sbuf_tensor("bq_c", [128, KHM], F32))
        rows_m = en(nc.sbuf_tensor("rows_m", [1, RT], F32))
        rows_is = en(nc.sbuf_tensor("rows_is", [1, RT], F32))
        rtmp = en(nc.sbuf_tensor("rtmp", [1, 2 * RT], F32))
        epsc = en(nc.sbuf_tensor("epsc", [1, 2], F32))
        sel8 = en(nc.sbuf_tensor("sel8", [E, E * 128], F32))
        S = {s: en(nc.semaphore(s)) for s in SEMS}

        def bcast_row(row_ap, n, ps_ap, last_sem=True):
            sp = nsplit(n)
            for i, (a0, a1) in enumerate(sp):
                b.do("tensor", lambda e, a0=a0, a1=a1: e.matmul(
                    ps_ap[:, a0:a1], ones_f1[:, :], row_ap[:, a0:a1],
                    start=True, stop=True, skip_group_check=True),
                    "st" if (last_sem and i == len(sp) - 1) else None)

        def newton_invsqrt(var_ap, eps_ap, out_ap, t0):
            b.do("vector", lambda e: e.tensor_scalar(var_ap, var_ap,
                                                     eps_ap, None, ALU.add))
            b.wait("scalar", "sv")
            b.do("scalar", lambda e: e.activation(out=t0, in_=var_ap,
                                                  func=AF.Sqrt))
            b.wait("vector", "ss")
            b.do("vector", lambda e: e.reciprocal(out_ap, t0))
            b.do("vector", lambda e: e.tensor_mul(t0, out_ap, out_ap))
            b.do("vector", lambda e: e.tensor_mul(t0, t0, var_ap))
            b.do("vector", lambda e: e.tensor_scalar(t0, t0, -0.5, 1.5,
                                                     ALU.mult, ALU.add))
            b.do("vector", lambda e: e.tensor_mul(out_ap, out_ap, t0))

        b.do("sync", lambda e: e.dma_start(out=idf[:], in_=IDF[:, :]), "sxm", 16)
        b.do("sync", lambda e: e.dma_start(
            out=rw_f[:].rearrange("p (k e) -> p k e", k=KD),
            in_=RW.rearrange("(k p) e -> p k e", p=128)), "sxm", 16)
        b.do("sync", lambda e: e.dma_start(
            out=expb[:].rearrange("p (e f) -> p e f", e=E),
            in_=EB.rearrange("e (f p) -> p e f", p=128),
            allow_slow_non_contiguous=True), "sxm", 16)
        b.do("sync", lambda e: e.dma_start(
            out=projb[:], in_=PB.rearrange("(f p) -> p f", p=128),
            allow_slow_non_contiguous=True), "sxm", 16)
        b.do("sync", lambda e: e.dma_start(
            out=halfnw[:], in_=NW.rearrange("(f p) -> p f", p=128),
            allow_slow_non_contiguous=True), "sxm", 16)
        b.do("sync", lambda e: e.dma_start(
            out=bo_c[:], in_=BO.rearrange("(f p) -> p f", p=128),
            allow_slow_non_contiguous=True), "sxm", 16)
        b.do("sync", lambda e: e.dma_start(
            out=rows_m[0:1, 0:E], in_=RB[None, :]), "sxm", 16)
        b.do("gpsimd", lambda e: e.dma_start(
            out=bqkr[:, 0:H], in_=BQ.rearrange("(h p) -> p h", p=128),
            allow_slow_non_contiguous=True), "sw0", 16)
        b.do("gpsimd", lambda e: e.dma_start(
            out=bqkr[:, H:2 * H], in_=BK.rearrange("(h p) -> p h", p=128),
            allow_slow_non_contiguous=True), "sw0", 16)
        b.do("gpsimd", lambda e: e.dma_start(out=wmq[:], in_=WMQT[:, :]), "sw0", 16)
        b.do("gpsimd", lambda e: e.dma_start(out=wmk[:], in_=WMKT[:, :]), "sw0", 16)

        b.do("vector", lambda e: e.memset(ones_bf[:], 1.0))
        b.do("vector", lambda e: e.memset(epsc[0:1, 0:1], LN_EPS))
        b.do("vector", lambda e: e.memset(epsc[0:1, 1:2], RMS_EPS))
        b.do("vector", lambda e: e.memset(ones_f1[:], 1.0))
        b.do("vector", lambda e: e.memset(ones_fc[:], 1.0))
        b.do("sync", lambda e: e.dma_start(out=sel8[:], in_=SEL8[:, :]), "sxm", 16)

        b.wait("scalar", "sxm"); b.wait("scalar", "sw0")
        b.do("scalar", lambda e: e.mul(halfnw[:], halfnw[:], 0.5))
        b.do("scalar", lambda e: e.copy(rw_hi[:], rw_f[:]))
        b.wait("vector", "ss")
        b.do("vector", lambda e: e.tensor_sub(rw_lo[:], rw_f[:], rw_hi[:]))

        b.wait("tensor", "sxm"); b.wait("tensor", "sw0")
        b.wait("tensor", "sv"); b.wait("tensor", "ss")
        for k in range(KD):
            b.do("tensor", lambda e, k=k: e.matmul(
                ps_a[0:1, 0:E], ones_fc[:, :], rw_f[:, k * E:(k + 1) * E],
                start=(k == 0), stop=(k == KD - 1), skip_group_check=True),
                "st" if k == KD - 1 else None)
        b.do("tensor", lambda e: e.matmul(
            ps_b[0:64, 0:KHM], wmq[:, :], bqkr[:, 0:H:2],
            start=True, stop=True, skip_group_check=True), None)
        b.do("tensor", lambda e: e.matmul(
            ps_b[64:128, 0:KHM], wmq[:, :], bqkr[:, 1:H:2],
            start=True, stop=True, skip_group_check=True), None)
        b.do("tensor", lambda e: e.matmul(
            ps_b[0:1, 512:512 + 1], ones_fc[0:1, 0:1], ones_fc[0:1, 0:1],
            start=True, stop=True, skip_group_check=True), "st")
        b.wait("vector", "st")
        b.do("vector", lambda e: e.tensor_copy(rtmp[0:1, 0:E], ps_a[0:1, 0:E]))
        b.do("vector", lambda e: e.tensor_copy(bq_c[:, :], ps_b[:, 0:KHM]))
        b.wait("tensor", "sv")
        bcast_row(rtmp[0:1, 0:E], E, ps_a[:, 0:E], last_sem=False)
        bcast_row(rows_m[0:1, 0:E], E, ps_a[:, 512:512 + E], last_sem=True)
        b.wait("vector", "st")
        b.do("vector", lambda e: e.tensor_copy(sbc[:, 0:E], ps_a[:, 0:E]))
        b.do("vector", lambda e: e.tensor_copy(sbc[:, E:2 * E], ps_a[:, 512:512 + E]))
        b.barrier()

        def emit_stats(src, c0, CH, bufs, router, stats_ps, raw_ps=None,
                       xh_res=None, xkey="xb", sems=("sx0", "sx1")):
            xb, xq = bufs[xkey], bufs["xq"]
            xh = xh_res if xh_res is not None else bufs["xh"]
            xl = bufs.get("xl")

            def hsl(d, par):
                if xh_res is not None:
                    return xh[:, d * CH:(d + 1) * CH]
                return xh[:, par * CH:(par + 1) * CH]
            g_xb = bufs.setdefault("_g_xbS", [None, None])
            g_xd = bufs.setdefault("_g_xd", [None, None])
            fm_s = {}
            fm_s2 = {}
            fm_r = {}

            for d in range(KD):
                par = d & 1
                b.wait_m("sync", g_xb[par])
                b.do("sync", lambda e, d=d, par=par: e.dma_start(
                    out=xb[:, par * CH:(par + 1) * CH],
                    in_=src[d * 128:(d + 1) * 128, c0:c0 + CH]), sems[par], 16)
                b.wait("scalar", sems[par])
                b.wait_m("scalar", g_xd[par])
                b.do("scalar", lambda e, d=d, par=par: e.copy(
                    hsl(d, par),
                    xb[:, par * CH:(par + 1) * CH]))
                b.do("scalar", lambda e, par=par: e.activation(
                    out=xq[:, par * CH:(par + 1) * CH],
                    in_=xb[:, par * CH:(par + 1) * CH], func=AF.Square))
                g_xb[par] = b.mark("ss")
                if router:
                    b.wait("vector", "ss")
                    b.wait_m("vector", g_xd[par])
                    b.do("vector", lambda e, d=d, par=par: e.tensor_sub(
                        xl[:, par * CH:(par + 1) * CH],
                        xb[:, par * CH:(par + 1) * CH],
                        hsl(d, par)))
                    g_xb[par] = b.mark("sv")
                b.wait("tensor", "ss")
                b.wait("tensor", "sv")
                for (a0, a1) in nsplit(CH):
                    last_s = (d == KD - 1 and a1 == CH)
                    fs = fm_s.pop(a0, True); fm_s[a0] = False if fs else False
                    fm_s[a0] = False
                    b.do("tensor", lambda e, d=d, par=par, a0=a0, a1=a1,
                         fm=fs: e.matmul(
                        stats_ps[0:1, a0:a1], ones_bf[:, :],
                        hsl(d, par)[:, a0:a1],
                        start=fm, stop=False, skip_group_check=True), None)
                    fs2 = fm_s2.pop(a0, True)
                    fm_s2[a0] = False
                    b.do("tensor", lambda e, par=par, a0=a0, a1=a1,
                         last=last_s, fm=fs2: e.matmul(
                        stats_ps[32:33, a0:a1], ones_bf[:, :],
                        xq[:, par * CH + a0: par * CH + a1],
                        start=fm, stop=last, skip_group_check=True),
                        "st" if (a1 == CH and not router) else None)
                    if router:
                        last_r = (d == KD - 1 and a1 == CH)
                        fr = fm_r.pop(a0, True)
                        fm_r[a0] = False
                        b.do("tensor", lambda e, d=d, par=par, a0=a0, a1=a1,
                             fm=fr: e.matmul(
                            raw_ps[0:8, a0:a1], rw_hi[:, d * E:(d + 1) * E],
                            hsl(d, par)[:, a0:a1],
                            start=fm, stop=False, skip_group_check=True), None)
                        b.do("tensor", lambda e, d=d, par=par, a0=a0, a1=a1: e.matmul(
                            raw_ps[0:8, a0:a1], rw_hi[:, d * E:(d + 1) * E],
                            xl[:, par * CH + a0: par * CH + a1],
                            start=False, stop=False, skip_group_check=True), None)
                        b.do("tensor", lambda e, d=d, par=par, a0=a0, a1=a1,
                             last=last_r: e.matmul(
                            raw_ps[0:8, a0:a1], rw_lo[:, d * E:(d + 1) * E],
                            hsl(d, par)[:, a0:a1],
                            start=False, stop=last, skip_group_check=True),
                            "st" if a1 == CH else None)
                g_xd[par] = b.mark("st")

        def emit_rows(bufs, router, stats_ps, raw_ps=None, CH=None):
            b.wait("vector", "st")
            b.do("vector", lambda e: e.tensor_scalar(
                rows_m[0:1, 0:CH], stats_ps[0:1, 0:CH], 1.0 / D, None, ALU.mult))
            b.do("vector", lambda e: e.tensor_scalar(
                rtmp[0:1, 0:CH], stats_ps[32:33, 0:CH], 1.0 / D, None, ALU.mult))
            if router:
                b.do("vector", lambda e: e.tensor_copy(
                    bufs["rawf"][0:8, 0:CH], raw_ps[0:8, 0:CH]))
            b.do("vector", lambda e: e.tensor_mul(
                rtmp[0:1, CH:2 * CH], rows_m[0:1, 0:CH], rows_m[0:1, 0:CH]))
            b.do("vector", lambda e: e.tensor_sub(
                rtmp[0:1, 0:CH], rtmp[0:1, 0:CH], rtmp[0:1, CH:2 * CH]))
            newton_invsqrt(rtmp[0:1, 0:CH], epsc[0:1, 0:1], rows_is[0:1, 0:CH],
                           rtmp[0:1, CH:2 * CH])

        def emit_bcasts(bufs, CH, bc_m, bc_i):
            mbc, isbc = bufs["mbc"], bufs["isbc"]
            b.wait("tensor", "sv")
            b.wait("tensor", "ss")
            bcast_row(rows_m[0:1, 0:CH], CH, bc_m, last_sem=False)
            bcast_row(rows_is[0:1, 0:CH], CH, bc_i, last_sem=True)
            b.wait("vector", "st")
            b.do("vector", lambda e: e.tensor_copy(mbc[:, 0:CH], bc_m))
            b.do("vector", lambda e: e.tensor_copy(isbc[:, 0:CH], bc_i))
            return b.mark("sv")

        def emit_xln_tile(src, c0, CH, bufs, d, mbc_done,
                          xkey="xb", sems=("sx0", "sx1")):
            xb = bufs[xkey]
            tmpf, xln, mbc, isbc = (bufs["tmpf"], bufs["xln"],
                                    bufs["mbc"], bufs["isbc"])
            gk = "_g_xb" if xkey == "xb" else "_g_xbS"
            g_xb = bufs.setdefault(gk, [None, None])
            par = d & 1
            b.wait_m("sync", g_xb[par])
            b.do("sync", lambda e, d=d, par=par: e.dma_start(
                out=xb[:, par * CH:(par + 1) * CH],
                in_=src[d * 128:(d + 1) * 128, c0:c0 + CH]), sems[par], 16)
            b.wait("vector", sems[par])
            b.wait_m("vector", mbc_done)
            b.do("vector", lambda e, par=par: e.tensor_sub(
                tmpf[:, par * CH:(par + 1) * CH],
                xb[:, par * CH:(par + 1) * CH], mbc[:, 0:CH]))
            g_xb[par] = b.mark("sv")
            b.do("vector", lambda e, d=d, par=par: e.tensor_mul(
                xln[:, d * CH:(d + 1) * CH],
                tmpf[:, par * CH:(par + 1) * CH], isbc[:, 0:CH]))
            bufs.setdefault("_xln_marks", [None] * KD)[d] = b.mark("sv")

        def emit_xln_inplace(bufs, d, CH, mbc_done):
            tmpf, xln, mbc, isbc = (bufs["tmpf"], bufs["xln"],
                                    bufs["mbc"], bufs["isbc"])
            par = d & 1
            b.wait_m("vector", mbc_done)
            b.do("vector", lambda e, d=d, par=par: e.tensor_sub(
                tmpf[:, par * CH:(par + 1) * CH],
                xln[:, d * CH:(d + 1) * CH], mbc[:, 0:CH]))
            b.do("vector", lambda e, d=d, par=par: e.tensor_mul(
                xln[:, d * CH:(d + 1) * CH],
                tmpf[:, par * CH:(par + 1) * CH], isbc[:, 0:CH]))
            bufs.setdefault("_xln_marks", [None] * KD)[d] = b.mark("sv")

        def emit_ln(src, c0, CH, bufs, router, stats_ps, raw_ps=None,
                    bc_m=None, bc_i=None, xh_res=None, skip_xln=False):
            emit_stats(src, c0, CH, bufs, router, stats_ps, raw_ps,
                       xh_res=xh_res)
            emit_rows(bufs, router, stats_ps, raw_ps, CH=CH)
            if skip_xln:
                return bufs.setdefault("_g_xbS", [None, None])
            if bc_m is None:
                bc_m, bc_i = ps_a[:, 0:CH], ps_b[:, 0:CH]
            md = emit_bcasts(bufs, CH, bc_m, bc_i)
            for d in range(KD):
                emit_xln_tile(src, c0, CH, bufs, d, md)
            return bufs["_g_xb"]

        def phase_ffn(bufs, src, dst):
            CH = CHF
            nb = CH // 128
            NB8 = nb * E
            xln = bufs["xln"]; cb = bufs["cb"]; wbc = bufs["wbc"]
            xb = bufs["xb"]; tmpf = bufs["tmpf"]; sil = bufs["sil"]
            cc = bufs["cc"]; wst = bufs["wst"]; x1b = bufs["x1b"]
            lgt = bufs["lgt"]; wrows = bufs["wrows"]
            mbc = bufs["mbc"]; isbc = bufs["isbc"]; mstm = bufs["mstm"]
            V1 = bufs["v1"]; V2 = bufs["v2"]; W1 = bufs["w1"]; W2 = bufs["w2"]
            WKW = KD * 128

            g_wst = [None, None, None, None]
            g_ps = [None, None]
            g_sil = [None, None]
            g_cc = None
            g_x1b = [None, None]

            nch = T // CH

            def emit_router_tm():
                sl3v = lgt[:, 3 * NB8:4 * NB8]
                b.wait("tensor", "sv")
                for tb in range(nb):
                    b.do("tensor", lambda e, tb=tb: e.transpose(
                        ps_kv[:, tb * 8:(tb + 1) * 8],
                        bufs["rawf"][0:8, tb * 128:(tb + 1) * 128],
                        idf[0:8, 0:8]), "st" if tb == nb - 1 else None)
                for tb in range(nb):
                    b.do("tensor", lambda e, tb=tb: e.transpose(
                        ps_kv[:, 64 + tb * 2:64 + tb * 2 + 1],
                        rows_m[0:1, tb * 128:(tb + 1) * 128],
                        idf[0:1, 0:1]), None)
                    b.do("tensor", lambda e, tb=tb: e.transpose(
                        ps_kv[:, 64 + tb * 2 + 1:64 + tb * 2 + 2],
                        rows_is[0:1, tb * 128:(tb + 1) * 128],
                        idf[0:1, 0:1]), "st" if tb == nb - 1 else None)
                b.wait("vector", "st")
                b.do("vector", lambda e, sl3v=sl3v: e.tensor_copy(
                    sl3v, ps_kv[:, 0:NB8]))
                b.do("vector", lambda e: e.tensor_copy(
                    mstm[:, 0:2 * nb], ps_kv[:, 64:64 + 2 * nb]))

            def emit_router_chain():
                sl = [lgt[:, i * NB8:(i + 1) * NB8] for i in range(4)]
                sl3 = [s.rearrange("p (t e) -> p t e", t=nb) for s in sl]
                m_tm = mstm[:, 0:2 * nb:2][:, :, None].broadcast_to([128, nb, E])
                i_tm = mstm[:, 1:2 * nb:2][:, :, None].broadcast_to([128, nb, E])
                S_b = sbc[:, 0:E][:, None, :].broadcast_to([128, nb, E])
                rb_b = sbc[:, E:2 * E][:, None, :].broadcast_to([128, nb, E])
                b.do("vector", lambda e: e.tensor_tensor(sl3[0], m_tm, S_b, ALU.mult))
                b.do("vector", lambda e: e.tensor_sub(sl[1], sl[3], sl[0]))
                b.do("vector", lambda e: e.tensor_tensor(sl3[1], sl3[1], i_tm, ALU.mult))
                b.do("vector", lambda e: e.tensor_tensor(sl3[0], sl3[1], rb_b, ALU.add))
                b.do("vector", lambda e: e.reduce_max(V1[:, :, None], sl3[0], axis=AX.X))
                v1b = V1[:, :, None].broadcast_to([128, nb, E])
                b.do("vector", lambda e: e.tensor_tensor(sl3[1], sl3[0], v1b, ALU.is_equal))
                b.do("vector", lambda e: e.scalar_tensor_tensor(
                    sl[2], sl[1], NEG_BIG, sl[0], ALU.mult, ALU.add))
                b.do("vector", lambda e: e.reduce_max(V2[:, :, None], sl3[2], axis=AX.X))
                v2b = V2[:, :, None].broadcast_to([128, nb, E])
                b.do("vector", lambda e: e.tensor_tensor(sl3[2], sl3[2], v2b, ALU.is_equal))
                b.do("vector", lambda e: e.tensor_sub(W2[:, :], V2[:, :], V1[:, :]))
                b.wait("scalar", "sv")
                b.do("scalar", lambda e: e.activation(out=W2[:, :], in_=W2[:, :], func=AF.Exp))
                b.wait("vector", "ss")
                b.do("vector", lambda e: e.tensor_scalar(W1[:, :], W2[:, :], 1.0, None, ALU.add))
                b.do("vector", lambda e: e.reciprocal(W1[:, :], W1[:, :]))
                b.do("vector", lambda e: e.tensor_mul(W2[:, :], W2[:, :], W1[:, :]))
                w1b = W1[:, :, None].broadcast_to([128, nb, E])
                w2b = W2[:, :, None].broadcast_to([128, nb, E])
                b.do("vector", lambda e: e.tensor_tensor(sl3[1], sl3[1], w1b, ALU.mult))
                b.do("vector", lambda e: e.tensor_tensor(sl3[2], sl3[2], w2b, ALU.mult))
                b.do("vector", lambda e: e.tensor_add(sl[0], sl[1], sl[2]))
                b.wait("tensor", "sv")
                for tb in range(nb):
                    b.do("tensor", lambda e, tb=tb, sl=sl: e.transpose(
                        ps_kv[0:8, tb * 128:(tb + 1) * 128],
                        sl[0][:, tb * 8:(tb + 1) * 8], idf[:, :]),
                        "st" if tb == nb - 1 else None)
                b.wait("vector", "st")
                b.do("vector", lambda e: e.tensor_copy(
                    wrows[0:8, 0:CH], ps_kv[0:8, 0:CH]))

            ST_PS = ps_a[:, 0:CH]
            RAW_PS = ps_a[64:72, 0:CH]
            md_pend = [None]
            for ci in range(nch):
                c0 = ci * CH
                if ci == 0:
                    emit_stats(src, c0, CH, bufs, router=True,
                               stats_ps=ST_PS, raw_ps=RAW_PS,
                               xh_res=xln, xkey="xbS", sems=("sx2", "sx3"))
                    emit_rows(bufs, True, ST_PS, RAW_PS, CH=CH)
                    md_pend[0] = emit_bcasts(bufs, CH, ps_kv[:, 0:CH],
                                             ps_kv[:, 1024:1024 + CH])
                    emit_router_tm()
                    for d in range(KD):
                        emit_xln_inplace(bufs, d, CH, md_pend[0])
                    emit_router_chain()
                g_xb = bufs.setdefault("_g_xb", [None, None])

                xmarks = bufs["_xln_marks"]
                sl = [lgt[:, i * NB8:(i + 1) * NB8] for i in range(4)]
                sl3 = [s.rearrange("p (t e) -> p t e", t=nb) for s in sl]

                def expert_mm_silu(f, e_i, first_pair=False):
                    it = f * E + e_i
                    par = it & 1
                    wp = it % 3
                    reg = (ps_a, ps_b)[par]
                    b.wait_m("gpsimd", g_wst[wp])
                    b.do("gpsimd", lambda e, e_i=e_i, f=f, wp=wp: e.dma_start(
                        out=wst[:, wp * WKW:(wp + 1) * WKW]
                            .rearrange("p (k f) -> p k f", k=KD),
                        in_=EW[e_i].rearrange("(k p) f -> p k f", p=128)
                            [:, :, f * 128:(f + 1) * 128]), "sw%d" % wp, 16)
                    b.wait("tensor", "sw%d" % wp)
                    b.wait_m("tensor", g_ps[par])
                    for k in range(KD):
                        if first_pair:
                            b.wait_m("tensor", xmarks[k])
                        for (a0, a1) in nsplit(CH):
                            last = (k == KD - 1 and a1 == CH)
                            b.do("tensor", lambda e, k=k, a0=a0, a1=a1,
                                 wp=wp, reg=reg: e.matmul(
                                reg[:, a0:a1],
                                wst[:, wp * WKW + k * 128: wp * WKW + (k + 1) * 128],
                                xln[:, k * CH + a0: k * CH + a1],
                                start=(k == 0), stop=(k == KD - 1)),
                                "st" if last else None)
                    g_wst[wp] = b.mark("st")
                    b.wait("scalar", "st")
                    b.wait_m("scalar", g_sil[par])
                    b.do("scalar", lambda e, e_i=e_i, f=f, par=par, reg=reg:
                         e.activation(
                        out=sil[:, par * CH:(par + 1) * CH], in_=reg[:, 0:CH],
                        func=AF.Silu,
                        bias=expb[:, e_i * KD + f: e_i * KD + f + 1]))
                    g_ps[par] = b.mark("ss")
                    return par

                def expert_combine(e_i, par, wbc_gate, g_cc_in):
                    b.wait("vector", "ss")
                    if e_i == 0:
                        b.wait_m("vector", wbc_gate)
                        if g_cc_in is not None:
                            b.wait_m("vector", g_cc_in)
                        b.do("vector", lambda e, e_i=e_i, par=par: e.tensor_tensor(
                            cc[:, 0:CH], sil[:, par * CH:(par + 1) * CH],
                            wbc[:, e_i * CH:(e_i + 1) * CH], ALU.mult))
                    else:
                        b.do("vector", lambda e, e_i=e_i, par=par: e.tensor_tensor(
                            tmpf[:, par * CH:(par + 1) * CH],
                            sil[:, par * CH:(par + 1) * CH],
                            wbc[:, e_i * CH:(e_i + 1) * CH], ALU.mult))
                        b.do("vector", lambda e, par=par: e.tensor_add(
                            cc[:, 0:CH], cc[:, 0:CH],
                            tmpf[:, par * CH:(par + 1) * CH]))
                    g_sil[par] = b.mark("sv")

                par0 = expert_mm_silu(0, 0, first_pair=True)
                par1 = expert_mm_silu(0, 1)

                b.wait("tensor", "ss")
                g_kreg = [b.mark("sv"), b.mark("sv")]
                for e_i in range(E):
                    kreg = (ps_kv[:, 0:1024], ps_b[:, 0:1024])[e_i & 1]
                    b.wait_m("tensor", g_kreg[e_i & 1])
                    b.wait("tensor", "sv")
                    for i_n, (a0, a1) in enumerate(nsplit(CH)):
                        b.do("tensor", lambda e, e_i=e_i, a0=a0, a1=a1, kreg=kreg: e.matmul(
                            kreg[:, a0:a1], sel8[:, e_i * 128:(e_i + 1) * 128],
                            wrows[0:E, a0:a1],
                            start=True, stop=True, skip_group_check=True),
                            "st" if a1 == CH else None)
                    b.wait("vector", "st")
                    b.do("vector", lambda e, e_i=e_i, kreg=kreg: e.tensor_copy(
                        wbc[:, e_i * CH:(e_i + 1) * CH], kreg[:, 0:CH]))
                    g_kreg[e_i & 1] = b.mark("sv")
                g_ps[1] = b.mark("sv")
                wbc_done = b.mark("sv")

                expert_combine(0, par0, wbc_done, g_cc)
                expert_combine(1, par1, wbc_done, None)
                for f in range(KD):
                    e_start = 2 if f == 0 else 0
                    for e_i in range(e_start, E):
                        par = expert_mm_silu(f, e_i)
                        expert_combine(e_i, par, wbc_done, g_cc if e_i == 0 else None)
                    b.wait("scalar", "sv")
                    b.do("scalar", lambda e, f=f: e.copy(
                        cb[:, f * CH:(f + 1) * CH], cc[:, 0:CH]))
                    g_cc = b.mark("ss")
                    if f == 2 and ci + 1 < nch:
                        emit_stats(src, c0 + CH, CH, bufs, router=True,
                                   stats_ps=ps_kv[:, 0:CH],
                                   raw_ps=ps_kv[64:72, 0:CH],
                                   xkey="xbS", sems=("sx2", "sx3"))
                        emit_rows(bufs, True, ps_kv[:, 0:CH],
                                  ps_kv[64:72, 0:CH], CH=CH)
                        md_pend[0] = emit_bcasts(bufs, CH, ps_kv[:, 0:CH],
                                                 ps_kv[:, 1024:1024 + CH])
                        emit_router_tm()
                    if f == 3 and ci + 1 < nch:
                        emit_router_chain()
                cb_done = b.mark("ss")
                fm_y = {}
                ysq_mark = [None, None]
                for f2 in range(KD):
                    par = f2 & 1
                    wp = f2 % 3
                    reg = (ps_a, ps_b)[par]
                    b.wait_m("gpsimd", g_wst[wp])
                    b.do("gpsimd", lambda e, f2=f2, wp=wp: e.dma_start(
                        out=wst[:, wp * WKW:(wp + 1) * WKW]
                            .rearrange("p (k f) -> p k f", k=KD),
                        in_=PW.rearrange("(k p) f -> p k f", p=128)
                            [:, :, f2 * 128:(f2 + 1) * 128]), "sw%d" % wp, 16)
                    b.wait("tensor", "sw%d" % wp)
                    b.wait_m("tensor", g_ps[par])
                    if f2 == 0:
                        b.wait_m("tensor", cb_done)
                    for k in range(KD):
                        for (a0, a1) in nsplit(CH):
                            last = (k == KD - 1 and a1 == CH)
                            b.do("tensor", lambda e, k=k, a0=a0, a1=a1, wp=wp,
                                 reg=reg: e.matmul(
                                reg[:, a0:a1],
                                wst[:, wp * WKW + k * 128: wp * WKW + (k + 1) * 128],
                                cb[:, k * CH + a0: k * CH + a1],
                                start=(k == 0), stop=(k == KD - 1)),
                                "st" if last else None)
                    g_wst[wp] = b.mark("st")
                    b.wait("vector", "st")
                    b.do("vector", lambda e, f2=f2, par=par, reg=reg:
                         e.scalar_tensor_tensor(
                        xln[:, f2 * CH:(f2 + 1) * CH], reg[:, 0:CH],
                        projb[:, f2:f2 + 1],
                        xln[:, f2 * CH:(f2 + 1) * CH], ALU.add, ALU.add))
                    g_ps[par] = b.mark("sv")
                    if ci + 1 < nch:
                        b.wait("scalar", "sv")
                        b.do("scalar", lambda e, f2=f2, par=par, c0=c0: e.dma_start(
                            out=ysc[f2 * 128:(f2 + 1) * 128, c0:c0 + CH],
                            in_=xln[:, f2 * CH:(f2 + 1) * CH]), "sy%d" % (f2 & 1),
                            16, dma=True)
                    b.wait("scalar", "sv")
                    b.wait_m("scalar", bufs.setdefault("_g_ysq", [None, None])[par])
                    b.do("scalar", lambda e, f2=f2, par=par: e.activation(
                        out=bufs["ysq"][:, par * CH:(par + 1) * CH],
                        in_=xln[:, f2 * CH:(f2 + 1) * CH], func=AF.Square))
                    ysq_mark[par] = b.mark("ss")

                    def y2_mms(fp, last_f2):
                        b.wait_m("tensor", ysq_mark[fp])
                        b.wait("tensor", "sv")
                        for (a0, a1) in nsplit(CH):
                            lastq = (last_f2 and a1 == CH)
                            fy = fm_y.pop(a0, True)
                            fm_y[a0] = False
                            b.do("tensor", lambda e, a0=a0, a1=a1, fp=fp,
                                 fm=fy, lastq=lastq: e.matmul(
                                ps_kv[0:1, 1024 + a0:1024 + a1], ones_bf[:, :],
                                bufs["ysq"][:, fp * CH + a0: fp * CH + a1],
                                start=fm, stop=lastq, skip_group_check=True),
                                "st" if a1 == CH else None)
                        bufs["_g_ysq"][fp] = b.mark("st")

                    if f2 > 0:
                        y2_mms((f2 - 1) & 1, False)
                    if f2 == KD - 1:
                        y2_mms(f2 & 1, True)
                    for d in ([f2 - 1] if f2 > 0 else []) + \
                             ([KD - 1] if f2 == KD - 1 else []):
                        if ci + 1 < nch:
                            b.wait("vector", "sy%d" % (d & 1))
                            b.wait_m("vector", ysq_mark[d & 1])
                            emit_xln_tile(src, c0 + CH, CH, bufs, d,
                                          md_pend[0], xkey="xbS",
                                          sems=("sx2", "sx3"))
                b.wait("vector", "st")
                b.do("vector", lambda e: e.tensor_scalar(
                    rtmp[0:1, 0:CH], ps_kv[0:1, 1024:1024 + CH], 1.0 / D, None, ALU.mult))
                newton_invsqrt(rtmp[0:1, 0:CH], epsc[0:1, 1:2], rows_m[0:1, 0:CH],
                               rtmp[0:1, CH:2 * CH])
                b.wait("tensor", "sv")
                bcast_row(rows_m[0:1, 0:CH], CH, ps_kv[:, 1024:1024 + CH])
                rms_done = b.mark("st")
                b.wait("sync", "sy0")
                b.wait("sync", "sy1")
                g_yld = bufs.setdefault("_g_yld", [None, None])
                g_ysq = bufs["_g_ysq"]
                last_ch = (ci + 1 == nch)
                for f2 in range(KD):
                    par = f2 & 1
                    b.wait_m("sync", g_xb[par])
                    b.do("sync", lambda e, f2=f2, par=par, c0=c0: e.dma_start(
                        out=xb[:, par * CH:(par + 1) * CH],
                        in_=src[f2 * 128:(f2 + 1) * 128, c0:c0 + CH]), "sx%d" % par, 16)
                    if not last_ch:
                        b.wait_m("sync", g_yld[par])
                        b.wait_m("sync", g_ysq[par])
                        b.do("sync", lambda e, f2=f2, par=par, c0=c0: e.dma_start(
                            out=bufs["ysq"][:, par * CH:(par + 1) * CH],
                            in_=ysc[f2 * 128:(f2 + 1) * 128, c0:c0 + CH]),
                            "sz%d" % par, 16)
                    b.wait("vector", "sx%d" % par)
                    b.wait_m("vector", rms_done)
                    if not last_ch:
                        b.wait("vector", "sz%d" % par)
                        b.do("vector", lambda e, f2=f2, par=par: e.tensor_tensor(
                            bufs["ysq"][:, par * CH:(par + 1) * CH],
                            bufs["ysq"][:, par * CH:(par + 1) * CH],
                            ps_kv[:, 1024:1024 + CH], ALU.mult))
                    else:
                        b.wait_m("vector", g_ysq[par])
                        b.do("vector", lambda e, f2=f2, par=par: e.tensor_tensor(
                            bufs["ysq"][:, par * CH:(par + 1) * CH],
                            xln[:, f2 * CH:(f2 + 1) * CH],
                            ps_kv[:, 1024:1024 + CH], ALU.mult))
                    b.wait_m("vector", g_x1b[par])
                    b.do("vector", lambda e, f2=f2, par=par: e.scalar_tensor_tensor(
                        x1b[:, par * CH:(par + 1) * CH],
                        bufs["ysq"][:, par * CH:(par + 1) * CH],
                        halfnw[:, f2:f2 + 1],
                        xb[:, par * CH:(par + 1) * CH], ALU.mult, ALU.add))
                    g_xb[par] = b.mark("sv")
                    g_yld[par] = b.mark("sv")
                    b.wait("scalar", "sv")
                    b.do("scalar", lambda e, f2=f2, par=par, c0=c0: e.dma_start(
                        out=dst[f2 * 128:(f2 + 1) * 128, c0:c0 + CH],
                        in_=x1b[:, par * CH:(par + 1) * CH]), "so%d" % par, 16,
                        dma=True)
                    g_x1b[par] = b.mark("so%d" % par)
                g_ysq[0] = b.mark("sv")
                g_ysq[1] = b.mark("sv")

        def g_xd_dummy(bufs, par):
            g = bufs.get("_g_xd")
            return g[par] if g else None

        def phase_attn1(bufs, src):
            CH = CHP1
            nb = CH // 128
            xh = bufs["xh_res"]; kt = bufs["kt"]; vt = bufs["vt"]
            wvt = bufs["wvt"]; wke = bufs["wke"]; wqh = bufs["wqh"]
            et0 = bufs["et0"]; et1 = bufs["et1"]
            kv_sb = bufs["kv_sb"]; ksum_r = rtmp[0:1, RT:RT + HM]
            ksum_b = bufs["ksum_b"]; bv_b = bufs["bv_b"]
            strukt = bufs["strukt"]; bk_row = rtmp[0:1, RT:RT + HM]
            bkbc = bufs["bkbc"]; csk = bufs["csk"]; csv = bufs["csv"]
            kcs_b = bufs["kcs_b"]
            mst = bufs["mst"]; mstb = bufs["mstb"]
            W2c = D // 2

            b.do("gpsimd", lambda e: e.dma_start(out=bv_b[:], in_=BV[None, :]), "sw0", 16)
            b.do("vector", lambda e: e.memset(strukt[:], 0.0))
            b.wait("tensor", "sv"); b.wait("tensor", "ss")
            for h in range(H):
                b.do("tensor", lambda e, h=h: e.matmul(
                    ps_b[0:1, h * M:(h + 1) * M],
                    bqkr[:, H + h:H + h + 1], wmk[:, :],
                    start=True, stop=True, skip_group_check=True),
                    "st" if h == H - 1 else None)
            b.wait("vector", "st")
            b.do("vector", lambda e: e.tensor_copy(bk_row[:, :], ps_b[0:1, 0:HM]))
            b.wait("tensor", "sv")
            bcast_row(bk_row[0:1, :], HM, ps_b[:, 0:HM], last_sem=True)
            b.wait("vector", "st")
            b.do("vector", lambda e: e.tensor_copy(bkbc[:, :], ps_b[:, 0:HM]))
            for k in range(KD):
                b.do("gpsimd", lambda e, k=k: e.dma_start(
                    out=wvt[:, k * D:(k + 1) * D],
                    in_=WVT[k * 128:(k + 1) * 128, :]), "sw0", 16)

            def build_weff(wsrc, wmap_sb, wdst):
                g_wqh = [None, None]
                g_psa = None
                for h in range(H):
                    wp = h & 1
                    b.wait_m("gpsimd", g_wqh[wp])
                    b.do("gpsimd", lambda e, h=h, wp=wp, wsrc=wsrc: e.dma_start(
                        out=wqh[:, wp * D:(wp + 1) * D],
                        in_=wsrc[h * 128:(h + 1) * 128, :]), "sw%d" % wp, 16)
                    b.wait("tensor", "sw%d" % wp)
                    b.wait_m("tensor", g_psa)
                    if h == 0:
                        b.wait("tensor", "sv")
                        b.wait("tensor", "ss")
                    for db in range(KD):
                        b.do("tensor", lambda e, db=db, wp=wp, wmap_sb=wmap_sb: e.matmul(
                            ps_a[:, db * M:(db + 1) * M],
                            wqh[:, wp * D + db * 128: wp * D + (db + 1) * 128],
                            wmap_sb[:, :], start=True, stop=True,
                            skip_group_check=True),
                            "st" if db == KD - 1 else None)
                    g_wqh[wp] = b.mark("st")
                    b.wait("vector", "st")
                    b.do("vector", lambda e, h=h, wdst=wdst: e.tensor_copy(
                        wdst[:].rearrange("p (k hm) -> p k hm", k=KD)
                            [:, :, h * M:(h + 1) * M],
                        ps_a[:, 0:KD * M].rearrange("p (k m) -> p k m", k=KD)))
                    g_psa = b.mark("sv")
                return b.mark("sv")

            kv_first = [True, True, True, True]
            ks_first = [True, True]
            kc_first = [True, True]
            g_kt = [None, None]
            g_vt = [None, None]
            g_ra = None
            g_rb = None
            total_tb = (T // CH) * nb
            tbc = [0]

            g_xh = [None, None]
            g_store = [None, None]
            for ci in range(T // CH):
                c0 = ci * CH
                J0 = c0 // 128
                cpar = ci & 1
                xhv = xh[:, cpar * KD * CH:(cpar + 1) * KD * CH]
                b.wait_m("scalar", g_xh[cpar])
                b.wait_m("scalar", g_store[cpar])
                b.wait_m("tensor", g_ra)
                if ci == 0:
                    g_ra = build_weff(WK, wmk, wke)
                    g_xb = emit_ln(src, c0, CH, bufs, router=False,
                                   stats_ps=ps_kv[:, 1024:1024 + CH],
                                   xh_res=xhv, skip_xln=True)
                else:
                    g_xb = emit_ln(src, c0, CH, bufs, router=False,
                                   stats_ps=ps_a[:, 0:CH], xh_res=xhv,
                                   skip_xln=True)
                if ci == 0:
                    b.wait("tensor", "sv")
                    for kd in range(KD):
                        for (a0, a1) in nsplit(HM):
                            b.do("tensor", lambda e, kd=kd, a0=a0, a1=a1: e.matmul(
                                ps_b[0:1, a0:a1], ones_bf[:, :],
                                wke[:, kd * HM + a0: kd * HM + a1],
                                start=(kd == 0), stop=(kd == KD - 1),
                                skip_group_check=True),
                                "st" if (kd == KD - 1 and a1 == HM) else None)
                    b.wait("vector", "st")
                    b.do("vector", lambda e: e.tensor_copy(rtmp[0:1, 0:HM], ps_b[0:1, 0:HM]))
                    b.wait("tensor", "sv")
                    bcast_row(rtmp[0:1, 0:HM], HM, ps_b[:, 0:HM], last_sem=True)
                    b.wait("vector", "st")
                    b.do("vector", lambda e: e.tensor_copy(csk[:, :], ps_b[:, 0:HM]))
                    b.wait("tensor", "sw0")
                    b.wait("tensor", "sv")
                    for kd in range(KD):
                        for (a0, a1) in nsplit(D):
                            b.do("tensor", lambda e, kd=kd, a0=a0, a1=a1: e.matmul(
                                ps_kv[0:1, a0:a1], ones_bf[:, :],
                                wvt[:, kd * D + a0: kd * D + a1],
                                start=(kd == 0), stop=(kd == KD - 1),
                                skip_group_check=True),
                                "st" if (kd == KD - 1 and a1 == D) else None)
                    b.wait("scalar", "st")
                    b.do("scalar", lambda e: e.copy(csv[:, :], ps_kv[0:1, 0:D]))
                b.do("vector", lambda e: e.tensor_mul(
                    rtmp[0:1, CH:2 * CH], rows_m[0:1, 0:CH], rows_is[0:1, 0:CH]))
                b.do("vector", lambda e: e.tensor_scalar(
                    rtmp[0:1, CH:2 * CH], rtmp[0:1, CH:2 * CH], -1.0, None, ALU.mult))
                b.wait("tensor", "sv")
                def emit_mst_tm(J0=J0):
                    b.wait("tensor", "ss")
                    for tb in range(nb):
                        b.do("tensor", lambda e, tb=tb: e.transpose(
                            ps_b[:, 512 + 2 * tb:512 + 2 * tb + 1],
                            rows_is[0:1, tb * 128:(tb + 1) * 128],
                            idf[0:1, 0:1]), None)
                        b.do("tensor", lambda e, tb=tb: e.transpose(
                            ps_b[:, 512 + 2 * tb + 1:512 + 2 * tb + 2],
                            rtmp[0:1, CH + tb * 128:CH + (tb + 1) * 128],
                            idf[0:1, 0:1]), "st" if tb == nb - 1 else None)
                    b.wait("vector", "st")
                    b.do("vector", lambda e, J0=J0: e.tensor_copy(
                        mst[:, 2 * J0:2 * (J0 + nb)], ps_b[:, 512:512 + 2 * nb]))
                    b.wait("scalar", "sv")
                    b.do("scalar", lambda e, J0=J0: e.copy(
                        mstb[:, 2 * J0:2 * (J0 + nb)],
                        mst[:, 2 * J0:2 * (J0 + nb)]))
                b.wait("sync", "ss")
                b.do("sync", lambda e, c0=c0, xhv=xhv: e.dma_start(
                    out=xlns.rearrange("(k p) t -> p k t", p=128)[:, :, c0:c0 + CH],
                    in_=xhv.rearrange("p (k t) -> p k t", k=KD)),
                    "sxm" if cpar == 0 else "sw3", 16)
                g_store[cpar] = b.mark("sxm" if cpar == 0 else "sw3")

                for tb in range(nb):
                    tbc[0] += 1
                    last_tb = tbc[0] == total_tb
                    par = tb & 1
                    J = J0 + tb
                    a_col = mst[:, 2 * J:2 * J + 1]
                    c_col = mst[:, 2 * J + 1:2 * J + 2]
                    c_colb = mstb[:, 2 * J + 1:2 * J + 2]
                    b.wait_m("tensor", g_ra)
                    b.wait("tensor", "ss")
                    if tb == 0:
                        b.wait("tensor", "sv")
                    for d in range(KD):
                        for (a0, a1) in nsplit(HM):
                            last = (d == KD - 1 and a1 == HM)
                            b.do("tensor", lambda e, d=d, tb=tb, a0=a0, a1=a1,
                                 xhv=xhv: e.matmul(
                                ps_a[:, a0:a1],
                                xhv[:, d * CH + tb * 128: d * CH + (tb + 1) * 128],
                                wke[:, d * HM + a0: d * HM + a1],
                                start=(d == 0), stop=(d == KD - 1)),
                                "st" if last else None)
                    if tb == 0:
                        emit_mst_tm()
                    b.wait("vector", "st")
                    b.wait_m("vector", g_kt[par])
                    b.do("vector", lambda e, a_col=a_col: e.scalar_tensor_tensor(
                        et0[:, 0:HM], ps_a[:, 0:HM], a_col, bkbc[:, :],
                        ALU.mult, ALU.add))
                    b.do("vector", lambda e, c_col=c_col: e.scalar_tensor_tensor(
                        et0[:, 0:HM], csk[:, :], c_col, et0[:, 0:HM],
                        ALU.mult, ALU.add))
                    b.do("vector", lambda e: e.tensor_scalar_min(
                        et1[:, 0:HM], et0[:, 0:HM], 0.0))
                    b.wait("scalar", "sv")
                    b.do("scalar", lambda e: e.activation(
                        out=et1[:, 0:HM], in_=et1[:, 0:HM], func=AF.Exp))
                    b.wait("vector", "ss")
                    b.do("vector", lambda e, par=par: e.scalar_tensor_tensor(
                        kt[:, par * HM:(par + 1) * HM], et0[:, 0:HM], 0.0,
                        et1[:, 0:HM], ALU.max, ALU.add))
                    g_ra = b.mark("sv")
                    kt_done = b.mark("sv")
                    for half in (0, 1):
                        reg = ps_b if half == 0 else ps_a
                        b.wait_m("tensor", g_rb if half == 0 else g_ra)
                        if tb == 0 and half == 0:
                            b.wait("tensor", "sv")
                        for d in range(KD):
                            for (a0, a1) in nsplit(W2c):
                                last = (d == KD - 1 and a1 == W2c)
                                b.do("tensor", lambda e, d=d, tb=tb, a0=a0, a1=a1,
                                     half=half, reg=reg, xhv=xhv: e.matmul(
                                    reg[:, a0:a1],
                                    xhv[:, d * CH + tb * 128: d * CH + (tb + 1) * 128],
                                    wvt[:, d * D + half * W2c + a0: d * D + half * W2c + a1],
                                    start=(d == 0), stop=(d == KD - 1)),
                                    "st" if last else None)
                        b.wait("vector", "st")
                        b.wait_m("vector", g_vt[par] if half == 0 else None)
                        b.do("vector", lambda e, par=par, half=half, reg=reg,
                             a_col=a_col: e.tensor_scalar(
                            vt[:, par * D + half * W2c: par * D + (half + 1) * W2c],
                            reg[:, 0:W2c], a_col, None, ALU.mult))
                        if half == 0:
                            g_rb = b.mark("sv")
                        else:
                            g_ra = b.mark("sv")
                    vt_done = b.mark("sv")
                    b.wait_m("tensor", kt_done)
                    b.wait_m("tensor", vt_done)
                    for h in range(H):
                        bank = (h * 64) // 512
                        fm = kv_first[bank]
                        kv_first[bank] = False
                        b.do("tensor", lambda e, h=h, par=par, fm=fm: e.matmul(
                            ps_kv[:, h * 64:(h + 1) * 64],
                            vt[:, par * D + h * 128: par * D + (h + 1) * 128],
                            kt[:, par * HM + h * M: par * HM + (h + 1) * M],
                            start=fm, stop=False, skip_group_check=True), None)
                    for i_k, (a0, a1) in enumerate(nsplit(HM)):
                        lastk = (a1 == HM)
                        fmk = ks_first[i_k] if i_k < 2 else False
                        if i_k < 2:
                            ks_first[i_k] = False
                        b.do("tensor", lambda e, par=par, a0=a0, a1=a1, fmk=fmk,
                             stp=(last_tb and lastk): e.matmul(
                            ps_kv[0:1, 1024 + a0:1024 + a1], ones_bf[:, :],
                            kt[:, par * HM + a0: par * HM + a1],
                            start=fmk, stop=False, skip_group_check=True), None)
                        fmc = kc_first[i_k] if i_k < 2 else False
                        if i_k < 2:
                            kc_first[i_k] = False
                        b.do("tensor", lambda e, par=par, a0=a0, a1=a1, fmc=fmc,
                             c_colb=c_colb,
                             stp=(last_tb and lastk): e.matmul(
                            ps_kv[32:33, 1024 + a0:1024 + a1], c_colb,
                            kt[:, par * HM + a0: par * HM + a1],
                            start=fmc, stop=stp, skip_group_check=True),
                            "st" if lastk else None)
                    g_kt[par] = b.mark("st")
                    g_vt[par] = b.mark("st")
                g_xh[cpar] = b.mark("st")

            b.wait("vector", "st")
            b.do("vector", lambda e: e.tensor_copy(ksum_r[:, :], ps_kv[0:1, 1024:1024 + HM]))
            b.do("vector", lambda e: e.tensor_copy(rtmp[0:1, 0:HM], ps_kv[32:33, 1024:1024 + HM]))
            b.wait("scalar", "sv")
            b.do("scalar", lambda e: e.copy(ksum_b[:, :], ksum_r[:, :]))
            b.do("scalar", lambda e: e.copy(kcs_b[:, :], rtmp[0:1, 0:HM]))
            b.wait("tensor", "ss")
            for h in range(H):
                b.do("tensor", lambda e, h=h: e.matmul(
                    ps_kv[:, h * 64:(h + 1) * 64],
                    bv_b[0:1, h * 128:(h + 1) * 128],
                    ksum_b[0:1, h * M:(h + 1) * M],
                    start=False, stop=False, skip_group_check=True), None)
                b.do("tensor", lambda e, h=h: e.matmul(
                    ps_kv[:, h * 64:(h + 1) * 64],
                    csv[0:1, h * 128:(h + 1) * 128],
                    kcs_b[0:1, h * M:(h + 1) * M],
                    start=False, stop=(h == H - 1), skip_group_check=True),
                    "st" if h == H - 1 else None)
            b.wait("vector", "st")
            b.do("vector", lambda e: e.tensor_copy(kv_sb[:, :], ps_kv[:, 0:HM]))
            b.wait("tensor", "sv")
            b.wait("tensor", "ss")
            for j in range(KHM):
                b.do("tensor", lambda e, j=j: e.transpose(
                    ps_a[:, j:j + 1], ksum_r[0:1, j * 128:(j + 1) * 128],
                    idf[0:1, 0:1]), "st" if j == KHM - 1 else None)
            b.wait("vector", "st")
            for j in range(KHM):
                b.do("vector", lambda e, j=j: e.tensor_copy(
                    strukt[0:64, j * H + 2 * j: j * H + 2 * j + 1],
                    ps_a[0:64, j:j + 1]))
                b.do("vector", lambda e, j=j: e.tensor_copy(
                    strukt[64:128, j * H + 2 * j + 1: j * H + 2 * j + 2],
                    ps_a[64:128, j:j + 1]))

        def phase_attn2(bufs, src, dst):
            CH = CHP2
            nb2 = CH // 128
            xh = bufs["xh"]; qe = bufs["qe"]; kvo = bufs["kvo"]
            wqe = bufs["wqe"]; wqh = bufs["wqh"]
            ninv = bufs["ninv"]; x2b = bufs["x2b"]; qtmp = bufs["qtmp"]
            xb = bufs["xb"]; tmpf = bufs["tmpf"]
            abc = bufs["abc"]; cbc = bufs["cbc"]; csq = bufs["csq"]
            kv_sb = bufs["kv_sb"]; strukt = bufs["strukt"]; sel = bufs["sel"]
            mst = bufs["mst"]
            b.do("sync", lambda e: e.dma_start(out=sel[:], in_=SELH[:, :]), "sxm", 16)
            b.wait("tensor", "sxm")
            g_wqh = [None, None]
            g_psa = None
            for h in range(H):
                wp = h & 1
                b.wait_m("gpsimd", g_wqh[wp])
                b.do("gpsimd", lambda e, h=h, wp=wp: e.dma_start(
                    out=wqh[:, wp * D:(wp + 1) * D],
                    in_=WQ[h * 128:(h + 1) * 128, :]), "sw%d" % wp, 16)
                b.wait("tensor", "sw%d" % wp)
                b.wait_m("tensor", g_psa)
                for db in range(KD):
                    b.do("tensor", lambda e, db=db, wp=wp: e.matmul(
                        ps_a[:, db * M:(db + 1) * M],
                        wqh[:, wp * D + db * 128: wp * D + (db + 1) * 128],
                        wmq[:, :], start=True, stop=True, skip_group_check=True),
                        "st" if db == KD - 1 else None)
                g_wqh[wp] = b.mark("st")
                b.wait("vector", "st")
                b.do("vector", lambda e, h=h: e.tensor_copy(
                    wqe[:].rearrange("p (k hm) -> p k hm", k=KD)
                        [:, :, h * M:(h + 1) * M],
                    ps_a[:, 0:KD * M].rearrange("p (k m) -> p k m", k=KD)))
                g_psa = b.mark("sv")
            b.wait_m("tensor", g_psa)
            for kd in range(KD):
                for (a0, a1) in nsplit(HM):
                    b.do("tensor", lambda e, kd=kd, a0=a0, a1=a1: e.matmul(
                        ps_b[0:1, a0:a1], ones_bf[:, :],
                        wqe[:, kd * HM + a0: kd * HM + a1],
                        start=(kd == 0), stop=(kd == KD - 1),
                        skip_group_check=True),
                        "st" if (kd == KD - 1 and a1 == HM) else None)
            b.wait("vector", "st")
            b.do("vector", lambda e: e.tensor_copy(rtmp[0:1, 0:HM], ps_b[0:1, 0:HM]))
            b.wait("tensor", "sv")
            for j in range(KHM):
                b.do("tensor", lambda e, j=j: e.transpose(
                    ps_b[:, 512 + j:512 + j + 1],
                    rtmp[0:1, j * 128:(j + 1) * 128],
                    idf[0:1, 0:1]), "st" if j == KHM - 1 else None)
            b.wait("vector", "st")
            b.do("vector", lambda e: e.tensor_copy(csq[:, :], ps_b[:, 512:512 + KHM]))

            PC = D // 512
            g_regE = None
            g_regO = None
            for h in range(H):
                wp = h & 1
                j = h // 2
                r0 = (h & 1) * 64
                b.wait_m("gpsimd", g_wqh[wp])
                b.do("gpsimd", lambda e, h=h, wp=wp: e.dma_start(
                    out=wqh[:, wp * D:(wp + 1) * D],
                    in_=WOT[h * 128:(h + 1) * 128, :]), "sw%d" % wp, 16)
                b.wait("tensor", "sw%d" % wp)
                b.wait("tensor", "sv")
                if r0 == 0:
                    b.wait_m("tensor", g_regE if j % 2 == 0 else g_regO)
                for i in range(PC):
                    if j % 2 == 0:
                        reg = ps_kv[:, i * 512:(i + 1) * 512]
                    else:
                        reg = (ps_a if i < 2 else ps_b)[:, (i % 2) * 512:
                                                        ((i % 2) + 1) * 512]
                    b.do("tensor", lambda e, h=h, wp=wp, i=i, r0=r0, reg=reg:
                         e.matmul(
                        reg[r0:r0 + 64, 0:512],
                        kv_sb[:, h * 64:(h + 1) * 64],
                        wqh[:, wp * D + i * 512: wp * D + (i + 1) * 512],
                        start=True, stop=True, skip_group_check=True),
                        "st" if i == PC - 1 else None)
                g_wqh[wp] = b.mark("st")
                if h & 1:
                    if j % 2 == 0:
                        b.wait("vector", "st")
                        b.do("vector", lambda e, j=j: e.tensor_copy(
                            kvo[:, j * D:(j + 1) * D], ps_kv[:, 0:D]))
                        g_regE = b.mark("sv")
                    else:
                        b.wait("scalar", "st")
                        for i in range(PC):
                            regc = (ps_a if i < 2 else ps_b)[:, (i % 2) * 512:
                                                             ((i % 2) + 1) * 512]
                            b.do("scalar", lambda e, j=j, i=i, regc=regc: e.copy(
                                kvo[:, j * D + i * 512: j * D + (i + 1) * 512],
                                regc[:, :]))
                        g_regO = b.mark("ss")
            b.wait("tensor", "sv")
            b.wait("tensor", "ss")

            slots = [ps_a[:, 0:512], ps_a[:, 512:1024],
                     ps_b[:, 0:512], ps_b[:, 512:1024]]
            g_reg = [None, None, b.mark("sv"), b.mark("sv")]
            g_qt = [None, None]
            g_x2b = [None, None]
            g_at = [None]
            g_ninv = None
            g_tm = [None, None]
            g_xb = [None, None]
            g_qps = [None, None]
            NCH2 = T // CH

            def emit_ac_rows(c):
                par = c & 1
                J0 = (c * CH) // 128
                po = par * CH
                b.wait_m("tensor", g_qps[0])
                b.wait_m("tensor", g_qps[1])
                b.wait("tensor", "sv")
                for tb in range(nb2):
                    b.do("tensor", lambda e, tb=tb, J0=J0: e.transpose(
                        ps_kv[0:1, tb * 128:(tb + 1) * 128],
                        mst[:, 2 * (J0 + tb):2 * (J0 + tb) + 1],
                        idf[:, :]), None)
                    b.do("tensor", lambda e, tb=tb, J0=J0: e.transpose(
                        ps_kv[0:1, 512 + tb * 128:512 + (tb + 1) * 128],
                        mst[:, 2 * (J0 + tb) + 1:2 * (J0 + tb) + 2],
                        idf[:, :]), "st" if tb == nb2 - 1 else None)
                b.wait("vector", "st")
                b.do("vector", lambda e: e.tensor_copy(
                    rtmp[0:1, 0:CH], ps_kv[0:1, 0:CH]))
                b.do("vector", lambda e: e.tensor_copy(
                    rtmp[0:1, CH:2 * CH], ps_kv[0:1, 512:512 + CH]))
                b.wait("tensor", "sv")
                bcast_row(rtmp[0:1, 0:CH], CH, ps_kv[:, 0:CH], last_sem=False)
                bcast_row(rtmp[0:1, CH:2 * CH], CH, ps_kv[:, 512:512 + CH],
                          last_sem=True)
                b.wait("vector", "st")
                b.do("vector", lambda e, po=po: e.tensor_copy(
                    abc[:, po:po + CH], ps_kv[:, 0:CH]))
                b.do("vector", lambda e, po=po: e.tensor_copy(
                    cbc[:, po:po + CH], ps_kv[:, 512:512 + CH]))
                g_qps[0] = b.mark("sv")
                g_qps[1] = b.mark("sv")

            def emit_xh_load(c):
                b.wait_m("sync", g_at[0])
                b.do("sync", lambda e, c0=c * CH: e.dma_start(
                    out=xh[:].rearrange("p (k t) -> p k t", k=KD),
                    in_=xlns.rearrange("(k p) t -> p k t", p=128)
                        [:, :, c0:c0 + CH]), "sxl", 16)

            def emit_q_block(c, j):
                par = c & 1
                qoff = par * KHM * CH
                po = par * CH
                half = j & 1
                reg = ps_kv[:, half * 512:half * 512 + CH]
                b.wait_m("tensor", g_qps[half])
                b.wait("tensor", "sxl")
                for d in range(KD):
                    b.do("tensor", lambda e, j=j, d=d, reg=reg: e.matmul(
                        reg[:, 0:CH],
                        wqe[:, d * HM + j * 128: d * HM + (j + 1) * 128],
                        xh[:, d * CH:(d + 1) * CH],
                        start=(d == 0), stop=(d == KD - 1)),
                        "st" if d == KD - 1 else None)
                if j == KHM - 1:
                    g_at[0] = b.mark("st")
                b.wait("vector", "st")
                b.wait_m("vector", g_qt[half])
                jp = half * CH
                b.do("vector", lambda e, reg=reg, jp=jp, po=po: e.tensor_tensor(
                    qtmp[:, jp:jp + CH], reg[:, 0:CH], abc[:, po:po + CH],
                    ALU.mult))
                b.do("vector", lambda e, j=j, jp=jp, po=po: e.scalar_tensor_tensor(
                    qtmp[:, jp:jp + CH], cbc[:, po:po + CH], csq[:, j:j + 1],
                    qtmp[:, jp:jp + CH], ALU.mult, ALU.add))
                g_qps[half] = b.mark("sv")
                b.wait_m("vector", g_tm[half])
                b.do("vector", lambda e, j=j, jp=jp: e.tensor_scalar(
                    tmpf[:, jp:jp + CH], qtmp[:, jp:jp + CH],
                    bq_c[:, j:j + 1], 0.0, ALU.add, ALU.min))
                b.wait("scalar", "sv")
                b.do("scalar", lambda e, jp=jp: e.activation(
                    out=tmpf[:, jp:jp + CH], in_=tmpf[:, jp:jp + CH], func=AF.Exp))
                b.wait("vector", "ss")
                b.do("vector", lambda e, j=j, jp=jp: e.tensor_scalar(
                    qtmp[:, jp:jp + CH], qtmp[:, jp:jp + CH],
                    bq_c[:, j:j + 1], 0.0, ALU.add, ALU.max))
                b.do("vector", lambda e, j=j, jp=jp, qoff=qoff: e.tensor_add(
                    qe[:, qoff + j * CH:qoff + (j + 1) * CH],
                    tmpf[:, jp:jp + CH], qtmp[:, jp:jp + CH]))
                g_qt[half] = b.mark("sv")
                g_tm[half] = b.mark("sv")

            def emit_norm(c, qed):
                nonlocal g_ninv
                qoff = (c & 1) * KHM * CH
                b.wait_m("tensor", qed)
                b.wait_m("tensor", g_ninv)
                for j in range(KHM):
                    b.do("tensor", lambda e, j=j, qoff=qoff: e.matmul(
                        ps_kv[0:H, 0:CH], strukt[:, j * H:(j + 1) * H],
                        qe[:, qoff + j * CH:qoff + (j + 1) * CH],
                        start=(j == 0), stop=(j == KHM - 1),
                        skip_group_check=True),
                        "st" if j == KHM - 1 else None)
                b.wait("vector", "st")
                b.do("vector", lambda e: e.tensor_scalar(
                    ninv[:, 0:CH], ps_kv[0:H, 0:CH], ATT_EPS, None, ALU.add))
                b.do("vector", lambda e: e.reciprocal(ninv[:, 0:CH], ninv[:, 0:CH]))
                g_ninv = b.mark("sv")

            emit_ac_rows(0)
            emit_xh_load(0)
            for j in range(KHM):
                emit_q_block(0, j)
            qe_done = b.mark("sv")
            emit_norm(0, qe_done)

            for ci in range(NCH2):
                c0 = ci * CH
                par = ci & 1
                qoff = par * KHM * CH
                b.wait_m("tensor", g_ninv)
                for j in range(KHM):
                    slot = j % 4
                    reg = slots[slot]
                    b.wait_m("tensor", g_reg[slot])
                    b.do("tensor", lambda e, j=j, reg=reg: e.matmul(
                        reg[:, 0:CH], sel[:, j * 128:(j + 1) * 128], ninv[:, 0:CH],
                        start=True, stop=True, skip_group_check=True), "st")
                    b.wait("vector", "st")
                    b.do("vector", lambda e, j=j, reg=reg, qoff=qoff: e.tensor_tensor(
                        qe[:, qoff + j * CH:qoff + (j + 1) * CH],
                        qe[:, qoff + j * CH:qoff + (j + 1) * CH],
                        reg[:, 0:CH], ALU.mult))
                    g_reg[slot] = b.mark("sv")
                qn_done = b.mark("sv")
                b.wait_m("tensor", qn_done)
                for f2 in range(KD):
                    slot = f2 % 4
                    reg = slots[slot]
                    b.wait_m("tensor", g_reg[slot])
                    for jj in range(KHM):
                        b.do("tensor", lambda e, jj=jj, f2=f2, reg=reg, qoff=qoff:
                             e.matmul(
                            reg[:, 0:CH],
                            kvo[:, jj * D + f2 * 128: jj * D + f2 * 128 + 128],
                            qe[:, qoff + jj * CH:qoff + (jj + 1) * CH],
                            start=(jj == 0), stop=(jj == KHM - 1)),
                            "st" if jj == KHM - 1 else None)
                    pr = f2 & 1
                    b.wait_m("sync", g_xb[pr])
                    b.do("sync", lambda e, f2=f2, pr=pr, c0=c0: e.dma_start(
                        out=xb[:, pr * CH:(pr + 1) * CH],
                        in_=src[f2 * 128:(f2 + 1) * 128, c0:c0 + CH]),
                        "sx%d" % pr, 16)
                    b.wait("vector", "sx%d" % pr)
                    b.wait("vector", "st")
                    b.wait_m("vector", g_x2b[pr])
                    b.do("vector", lambda e, f2=f2, pr=pr, reg=reg:
                         e.scalar_tensor_tensor(
                        x2b[:, pr * CH:(pr + 1) * CH], reg[:, 0:CH],
                        bo_c[:, f2:f2 + 1], xb[:, pr * CH:(pr + 1) * CH],
                        ALU.add, ALU.add))
                    g_xb[pr] = b.mark("sv")
                    g_reg[slot] = b.mark("sv")
                    b.wait("scalar", "sv")
                    b.do("scalar", lambda e, f2=f2, pr=pr, c0=c0: e.dma_start(
                        out=dst[f2 * 128:(f2 + 1) * 128, c0:c0 + CH],
                        in_=x2b[:, pr * CH:(pr + 1) * CH]), "so%d" % pr, 16,
                        dma=True)
                    g_x2b[pr] = b.mark("so%d" % pr)
                    if ci + 1 < NCH2:
                        if f2 == 0:
                            emit_ac_rows(ci + 1)
                            emit_xh_load(ci + 1)
                        jh = f2 - 1
                        if 0 <= jh < KHM:
                            emit_q_block(ci + 1, jh)
                            if jh == KHM - 1:
                                qe_done = b.mark("sv")
                        if f2 == KHM + 1:
                            emit_norm(ci + 1, qe_done)

        def make_ffn_bufs(pfx):
            mgr = []
            def t(name, shape, dt):
                g = nc.sbuf_tensor(pfx + name, shape, dt)
                h = g.__enter__()
                mgr.append(g)
                return h
            bufs = dict(
                xln=t("xln", [128, KD * CHF], BF16),
                cb=t("cb", [128, KD * CHF], BF16),
                wbc=t("wbc", [128, E * CHF], F32),
                xb=t("xb", [128, 2 * CHF], F32),
                xh=t("xh", [128, 2 * CHF], BF16),
                xl=t("xl", [128, 2 * CHF], BF16),
                xq=t("xq", [128, 2 * CHF], BF16),
                tmpf=t("tmpf", [128, 2 * CHF], F32),
                sil=t("sil", [128, 2 * CHF], F32),
                cc=t("cc", [128, CHF], F32),
                wst=t("wst", [128, 3 * KD * 128], BF16),
                mbc=t("mbc", [128, CHF], BF16),
                isbc=t("isbc", [128, CHF], BF16),
                x1b=t("x1b", [128, 2 * CHF], F32),
                xbS=t("xbS", [128, 2 * CHF], F32),
                ysq=t("ysq", [128, 2 * CHF], BF16),
                lgt=t("lgt", [128, 4 * (CHF // 128) * E], F32),
                wrows=t("wrows", [8, CHF], F32),
                rawf=t("rawf", [8, CHF], F32),
                mstm=t("mstm", [128, 2 * (CHF // 128)], F32),
                v1=t("v1", [128, CHF // 128], F32),
                v2=t("v2", [128, CHF // 128], F32),
                w1=t("w1", [128, CHF // 128], F32),
                w2=t("w2", [128, CHF // 128], F32),
            )
            return bufs, mgr

        def close_bufs(mgr):
            for g in reversed(mgr):
                g.__exit__(None, None, None)

        fb, mgr = make_ffn_bufs("f_")
        phase_ffn(fb, src=x0, dst=x1s)
        close_bufs(mgr)
        b.barrier()

        amgr = []
        def at_t(name, shape, dt):
            g = nc.sbuf_tensor(name, shape, dt)
            h = g.__enter__()
            amgr.append(g)
            return h
        kv_sb = at_t("kv_sb", [128, HM], BF16)
        ksum_b = at_t("ksum_b", [1, HM], BF16)
        bv_b = at_t("bv_b", [1, D], BF16)
        strukt = at_t("strukt", [128, KHM * H], BF16)
        mst_tm = at_t("mst_tm", [128, 2 * (T // 128)], F32)
        mst_tmb = at_t("mst_tmb", [128, 2 * (T // 128)], BF16)

        p1mgr = []
        def p1_t(name, shape, dt):
            g = nc.sbuf_tensor(name, shape, dt)
            h = g.__enter__()
            p1mgr.append(g)
            return h
        p1 = dict(
            wvt=p1_t("p1_wvt", [128, KD * D], BF16),
            wke=p1_t("p1_wke", [128, KD * HM], BF16),
            wqh=p1_t("p1_wqh", [128, 2 * D], BF16),
            xh_res=p1_t("p1_xh_res", [128, 2 * KD * CHP1], BF16),
            xb=p1_t("p1_xb", [128, 2 * CHP1], F32),
            xq=p1_t("p1_xq", [128, 2 * CHP1], BF16),
            tmpf=p1_t("p1_tmpf", [128, 2 * CHP1], F32),
            kt=p1_t("p1_kt", [128, 2 * HM], BF16),
            vt=p1_t("p1_vt", [128, 2 * D], BF16),
            et0=p1_t("p1_et0", [128, HM], BF16),
            bkbc=p1_t("p1_bkbc", [128, HM], F32),
            csk=p1_t("p1_csk", [128, HM], BF16),
            csv=p1_t("p1_csv", [1, D], BF16),
            kcs_b=p1_t("p1_kcs_b", [1, HM], BF16),
            kv_sb=kv_sb, ksum_b=ksum_b, bv_b=bv_b,
            strukt=strukt, mst=mst_tm, mstb=mst_tmb,
        )
        p1["et1"] = p1["tmpf"]
        phase_attn1(p1, src=x1s)
        close_bufs(p1mgr)
        b.barrier()

        p2mgr = []
        def p2_t(name, shape, dt):
            g = nc.sbuf_tensor(name, shape, dt)
            h = g.__enter__()
            p2mgr.append(g)
            return h
        p2_xh = p2_t("p2_xh", [128, max(KD, H) * CHP2], BF16)
        p2 = dict(
            kvo=p2_t("p2_kvo", [128, KHM * D], BF16),
            wqe=p2_t("p2_wqe", [128, KD * HM], BF16),
            wqh=p2_t("p2_wqh", [128, 2 * D], BF16),
            xh=p2_xh,
            xb=p2_t("p2_xb", [128, 2 * CHP2], F32),
            tmpf=p2_t("p2_tmpf", [128, 2 * CHP2], F32),
            qe=p2_t("p2_qe", [128, 2 * KHM * CHP2], BF16),
            ninv=p2_t("p2_ninv", [H, CHP2], F32),
            x2b=p2_t("p2_x2b", [128, 2 * CHP2], F32),
            qtmp=p2_t("p2_qtmp", [128, 2 * CHP2], F32),
            sel=p2_t("p2_sel", [H, KHM * 128], F32),
            abc=p2_t("p2_abc", [128, 2 * CHP2], F32),
            cbc=p2_t("p2_cbc", [128, 2 * CHP2], F32),
            csq=p2_t("p2_csq", [128, KHM], F32),
            kv_sb=kv_sb, strukt=strukt, mst=mst_tm,
        )
        phase_attn2(p2, src=x1s, dst=x2s)
        close_bufs(p2mgr)
        close_bufs(amgr)
        b.barrier()

        gb, mgr2 = make_ffn_bufs("g_")
        phase_ffn(gb, src=x2s, dst=out)
        close_bufs(mgr2)

        for s in SEMS:
            if not s.startswith("so"):
                b.wait("sync", s)
        b.wait("sync", "so0")
        b.wait("sync", "so1")

        with nc.Block() as block:
            @block.sync
            def _(e):
                for fn in b.ops["sync"]:
                    fn(e, S)

            @block.gpsimd
            def _(e):
                for fn in b.ops["gpsimd"]:
                    fn(e, S)

            @block.scalar
            def _(e):
                for fn in b.ops["scalar"]:
                    fn(e, S)

            @block.vector
            def _(e):
                for fn in b.ops["vector"]:
                    fn(e, S)

            @block.tensor
            def _(e):
                for fn in b.ops["tensor"]:
                    fn(e, S)

    nc.finalize()
    return nc


CFG_FULL = dict(D=2048, T=2048, H=16, M=64, E=8, CHF=1024, CHP1=512, CHP2=512)

_nc_cache = {}


def host_weights(inputs):
    g = lambda k: np.ascontiguousarray(np.asarray(inputs[k], np.float32))
    w = {}
    w["expert_w"] = g("L0_ffn_expert_w")
    w["expert_b"] = g("L0_ffn_expert_b")
    w["router_w"] = g("L0_ffn_router_w")
    w["router_b"] = g("L0_ffn_router_b")
    w["proj_w"] = g("L0_ffn_proj_w")
    w["proj_b"] = g("L0_ffn_proj_b")
    w["norm_w"] = g("L0_ffn_norm_w")
    w["wq"] = g("L0_attn_wq")
    w["bq"] = g("L0_attn_bq")
    w["wk"] = g("L0_attn_wk")
    w["bk"] = g("L0_attn_bk")
    w["wvT"] = np.ascontiguousarray(g("L0_attn_wv").T)
    w["bv"] = g("L0_attn_bv")
    w["woT"] = np.ascontiguousarray(g("L0_attn_wo").T)
    w["bo"] = g("L0_attn_bo")
    w["wmap_qT"] = np.ascontiguousarray(g("L0_attn_wmap_q").T)
    w["wmap_kT"] = np.ascontiguousarray(g("L0_attn_wmap_k").T)
    w["idf32"] = np.eye(128, dtype=np.float32)
    E = 8
    sel8 = np.zeros((E, E * 128), np.float32)
    for e in range(E):
        sel8[e, e * 128:(e + 1) * 128] = 1.0
    w["sel8_c"] = sel8
    H = 16
    KHM = (H * 64) // 128
    selh = np.zeros((H, KHM * 128), np.float32)
    for j in range(KHM):
        selh[2 * j, j * 128: j * 128 + 64] = 1.0
        selh[2 * j + 1, j * 128 + 64: (j + 1) * 128] = 1.0
    w["selh_c"] = selh
    return w


def kernel(**inputs):
    cfg = CFG_FULL
    if "full" not in _nc_cache:
        _nc_cache["full"] = build_program(cfg)
    nc = _nc_cache["full"]

    x = np.asarray(inputs["L0_attn_hidden_states"], dtype=np.float32)
    B = x.shape[0]
    w = host_weights(inputs)
    in_maps = []
    for c in range(B):
        m = {"x0": np.ascontiguousarray(x[c].T)}
        m.update(w)
        in_maps.append(m)

    res = run_bass_kernel_spmd(nc, in_maps, core_ids=list(range(B)))
    outs = [np.ascontiguousarray(np.asarray(r["out_fm"]).T) for r in res.results]
    return np.stack(outs, axis=0).astype(np.float32)


# version 33 (final)
# speedup vs baseline: 1.0705x; 1.0162x over previous
import contextlib
import numpy as np
import ml_dtypes
import concourse.bass as bass
import concourse.mybir as mybir
from concourse.bass_utils import run_bass_kernel_spmd

F32 = mybir.dt.float32
BF16 = mybir.dt.bfloat16
AF = mybir.ActivationFunctionType
ALU = mybir.AluOpType
AX = mybir.AxisListType

ENGINES = ("sync", "gpsimd", "scalar", "vector", "tensor")
SEMS = ("sx0", "sx1", "sx2", "sx3", "sy0", "sy1", "sz0", "sz1", "sxm", "sxl", "sw0", "sw1", "sw2", "sw3", "so0", "so1", "st", "sv", "ss")

LN_EPS = 1e-5
ATT_EPS = 1e-6
RMS_EPS = 1e-6
NEG_BIG = -1e30


class Sched:

    AUTO = {"vector": "sv", "scalar": "ss"}

    def __init__(self):
        self.ops = {e: [] for e in ENGINES}
        self.cnt = {s: 0 for s in SEMS}

    def do(self, eng, build, sem=None, inc=1, dma=False):
        if eng in self.AUTO and not dma:
            own = self.AUTO[eng]
            prev = self.cnt[own]
            if prev > 0:
                self.ops[eng].append(
                    lambda e, S, s=own, v=prev: e.wait_ge(S[s], v))
            self.cnt[own] += 1
            self.ops[eng].append(
                lambda e, S, b=build, s=own: b(e).then_inc(S[s], 1))
            return self.cnt[own]
        if sem is None:
            self.ops[eng].append(lambda e, S, b=build: b(e))
            return None
        self.cnt[sem] += inc
        self.ops[eng].append(
            lambda e, S, b=build, s=sem, i=inc: b(e).then_inc(S[s], i))
        return self.cnt[sem]

    def wait(self, eng, sem, val=None):
        v = self.cnt[sem] if val is None else val
        if v > 0:
            self.ops[eng].append(lambda e, S, s=sem, v=v: e.wait_ge(S[s], v))

    def mark(self, sem):
        return (sem, self.cnt[sem])

    def wait_m(self, eng, mark):
        if mark is not None:
            self.wait(eng, mark[0], mark[1])

    def barrier(self):
        snap = dict(self.cnt)
        for eng in ENGINES:
            for s, v in snap.items():
                if v > 0:
                    self.wait(eng, s, v)


def build_program(cfg):
    D = cfg["D"]; T = cfg["T"]; H = cfg["H"]; M = cfg["M"]; E = cfg["E"]
    CHF = cfg["CHF"]; CHP1 = cfg["CHP1"]; CHP2 = cfg["CHP2"]
    KD = D // 128
    HM = H * M
    KHM = HM // 128
    assert H * 128 == D and M == 64 and E == 8

    def nsplit(n):
        out = []
        c = 0
        while c < n:
            w = min(512, n - c)
            out.append((c, c + w))
            c += w
        return out

    nc = bass.Bass("TRN2", target_bir_lowering=False)

    x0 = nc.dram_tensor("x0", [D, T], F32, kind="ExternalInput").ap()
    out = nc.dram_tensor("out_fm", [D, T], F32, kind="ExternalOutput").ap()
    x1s = nc.dram_tensor("x1s", [D, T], F32, kind="Internal").ap()
    x2s = nc.dram_tensor("x2s", [D, T], F32, kind="Internal").ap()
    xlns = nc.dram_tensor("xlns", [D, T], BF16, kind="Internal").ap()
    ysc = nc.dram_tensor("ysc", [D, T], BF16, kind="Internal").ap()

    EW = nc.dram_tensor("expert_w", [E, D, D], F32, kind="ExternalInput").ap()
    EB = nc.dram_tensor("expert_b", [E, D], F32, kind="ExternalInput").ap()
    RW = nc.dram_tensor("router_w", [D, E], F32, kind="ExternalInput").ap()
    RB = nc.dram_tensor("router_b", [E], F32, kind="ExternalInput").ap()
    PW = nc.dram_tensor("proj_w", [D, D], F32, kind="ExternalInput").ap()
    PB = nc.dram_tensor("proj_b", [D], F32, kind="ExternalInput").ap()
    NW = nc.dram_tensor("norm_w", [D], F32, kind="ExternalInput").ap()
    WQ = nc.dram_tensor("wq", [D, D], F32, kind="ExternalInput").ap()
    BQ = nc.dram_tensor("bq", [D], F32, kind="ExternalInput").ap()
    WK = nc.dram_tensor("wk", [D, D], F32, kind="ExternalInput").ap()
    BK = nc.dram_tensor("bk", [D], F32, kind="ExternalInput").ap()
    WVT = nc.dram_tensor("wvT", [D, D], F32, kind="ExternalInput").ap()
    BV = nc.dram_tensor("bv", [D], F32, kind="ExternalInput").ap()
    WOT = nc.dram_tensor("woT", [D, D], F32, kind="ExternalInput").ap()
    BO = nc.dram_tensor("bo", [D], F32, kind="ExternalInput").ap()
    WMQT = nc.dram_tensor("wmap_qT", [128, M], F32, kind="ExternalInput").ap()
    WMKT = nc.dram_tensor("wmap_kT", [128, M], F32, kind="ExternalInput").ap()
    IDF = nc.dram_tensor("idf32", [128, 128], F32, kind="ExternalInput").ap()
    SEL8 = nc.dram_tensor("sel8_c", [E, E * 128], F32, kind="ExternalInput").ap()
    SELH = nc.dram_tensor("selh_c", [H, KHM * 128], F32, kind="ExternalInput").ap()

    b = Sched()
    RT = max(CHF, CHP1, CHP2)

    with contextlib.ExitStack() as _stk:
        en = _stk.enter_context
        ps_kv = en(nc.psum_tensor("ps_kv", [128, 2048], F32))
        ps_a = en(nc.psum_tensor("ps_a", [128, 1024], F32))
        ps_b = en(nc.psum_tensor("ps_b", [128, 1024], F32))
        idf = en(nc.sbuf_tensor("idf_sb", [128, 128], F32))
        ones_bf = en(nc.sbuf_tensor("ones_bf", [128, 1], BF16))
        ones_f1 = en(nc.sbuf_tensor("ones_f1", [1, 128], F32))
        ones_fc = en(nc.sbuf_tensor("ones_fc", [128, 1], F32))
        rw_f = en(nc.sbuf_tensor("rw_f", [128, KD * E], F32))
        rw_hi = en(nc.sbuf_tensor("rw_hi", [128, KD * E], BF16))
        rw_lo = en(nc.sbuf_tensor("rw_lo", [128, KD * E], BF16))
        expb = en(nc.sbuf_tensor("expb", [128, E * KD], F32))
        projb = en(nc.sbuf_tensor("projb", [128, KD], F32))
        halfnw = en(nc.sbuf_tensor("halfnw", [128, KD], F32))
        bo_c = en(nc.sbuf_tensor("bo_c", [128, KD], F32))
        sbc = en(nc.sbuf_tensor("sbc", [128, 2 * E], F32))
        wmq = en(nc.sbuf_tensor("wmq", [128, M], BF16))
        wmk = en(nc.sbuf_tensor("wmk", [128, M], BF16))
        bqkr = en(nc.sbuf_tensor("bqkr", [128, 2 * H], BF16))
        bq_c = en(nc.sbuf_tensor("bq_c", [128, KHM], F32))
        rows_m = en(nc.sbuf_tensor("rows_m", [1, RT], F32))
        rows_is = en(nc.sbuf_tensor("rows_is", [1, RT], F32))
        rtmp = en(nc.sbuf_tensor("rtmp", [1, 2 * RT], F32))
        epsc = en(nc.sbuf_tensor("epsc", [1, 2], F32))
        sel8 = en(nc.sbuf_tensor("sel8", [E, E * 128], F32))
        S = {s: en(nc.semaphore(s)) for s in SEMS}

        def bcast_row(row_ap, n, ps_ap, last_sem=True):
            sp = nsplit(n)
            for i, (a0, a1) in enumerate(sp):
                b.do("tensor", lambda e, a0=a0, a1=a1: e.matmul(
                    ps_ap[:, a0:a1], ones_f1[:, :], row_ap[:, a0:a1],
                    start=True, stop=True, skip_group_check=True),
                    "st" if (last_sem and i == len(sp) - 1) else None)

        def newton_invsqrt(var_ap, eps_ap, out_ap, t0):
            b.do("vector", lambda e: e.tensor_scalar(var_ap, var_ap,
                                                     eps_ap, None, ALU.add))
            b.wait("scalar", "sv")
            b.do("scalar", lambda e: e.activation(out=t0, in_=var_ap,
                                                  func=AF.Sqrt))
            b.wait("vector", "ss")
            b.do("vector", lambda e: e.reciprocal(out_ap, t0))
            b.do("vector", lambda e: e.tensor_mul(t0, out_ap, out_ap))
            b.do("vector", lambda e: e.tensor_mul(t0, t0, var_ap))
            b.do("vector", lambda e: e.tensor_scalar(t0, t0, -0.5, 1.5,
                                                     ALU.mult, ALU.add))
            b.do("vector", lambda e: e.tensor_mul(out_ap, out_ap, t0))

        b.do("sync", lambda e: e.dma_start(out=idf[:], in_=IDF[:, :]), "sxm", 16)
        b.do("sync", lambda e: e.dma_start(
            out=rw_f[:].rearrange("p (k e) -> p k e", k=KD),
            in_=RW.rearrange("(k p) e -> p k e", p=128)), "sxm", 16)
        b.do("sync", lambda e: e.dma_start(
            out=expb[:].rearrange("p (e f) -> p e f", e=E),
            in_=EB.rearrange("e (f p) -> p e f", p=128),
            allow_slow_non_contiguous=True), "sxm", 16)
        b.do("sync", lambda e: e.dma_start(
            out=projb[:], in_=PB.rearrange("(f p) -> p f", p=128),
            allow_slow_non_contiguous=True), "sxm", 16)
        b.do("sync", lambda e: e.dma_start(
            out=halfnw[:], in_=NW.rearrange("(f p) -> p f", p=128),
            allow_slow_non_contiguous=True), "sxm", 16)
        b.do("sync", lambda e: e.dma_start(
            out=bo_c[:], in_=BO.rearrange("(f p) -> p f", p=128),
            allow_slow_non_contiguous=True), "sxm", 16)
        b.do("sync", lambda e: e.dma_start(
            out=rows_m[0:1, 0:E], in_=RB[None, :]), "sxm", 16)
        b.do("gpsimd", lambda e: e.dma_start(
            out=bqkr[:, 0:H], in_=BQ.rearrange("(h p) -> p h", p=128),
            allow_slow_non_contiguous=True), "sw0", 16)
        b.do("gpsimd", lambda e: e.dma_start(
            out=bqkr[:, H:2 * H], in_=BK.rearrange("(h p) -> p h", p=128),
            allow_slow_non_contiguous=True), "sw0", 16)
        b.do("gpsimd", lambda e: e.dma_start(out=wmq[:], in_=WMQT[:, :]), "sw0", 16)
        b.do("gpsimd", lambda e: e.dma_start(out=wmk[:], in_=WMKT[:, :]), "sw0", 16)

        b.do("vector", lambda e: e.memset(ones_bf[:], 1.0))
        b.do("vector", lambda e: e.memset(epsc[0:1, 0:1], LN_EPS))
        b.do("vector", lambda e: e.memset(epsc[0:1, 1:2], RMS_EPS))
        b.do("vector", lambda e: e.memset(ones_f1[:], 1.0))
        b.do("vector", lambda e: e.memset(ones_fc[:], 1.0))
        b.do("sync", lambda e: e.dma_start(out=sel8[:], in_=SEL8[:, :]), "sxm", 16)

        b.wait("scalar", "sxm"); b.wait("scalar", "sw0")
        b.do("scalar", lambda e: e.mul(halfnw[:], halfnw[:], 0.5))
        b.do("scalar", lambda e: e.copy(rw_hi[:], rw_f[:]))
        b.wait("vector", "ss")
        b.do("vector", lambda e: e.tensor_sub(rw_lo[:], rw_f[:], rw_hi[:]))

        b.wait("tensor", "sxm"); b.wait("tensor", "sw0")
        b.wait("tensor", "sv"); b.wait("tensor", "ss")
        for k in range(KD):
            b.do("tensor", lambda e, k=k: e.matmul(
                ps_a[0:1, 0:E], ones_fc[:, :], rw_f[:, k * E:(k + 1) * E],
                start=(k == 0), stop=(k == KD - 1), skip_group_check=True),
                "st" if k == KD - 1 else None)
        b.do("tensor", lambda e: e.matmul(
            ps_b[0:64, 0:KHM], wmq[:, :], bqkr[:, 0:H:2],
            start=True, stop=True, skip_group_check=True), None)
        b.do("tensor", lambda e: e.matmul(
            ps_b[64:128, 0:KHM], wmq[:, :], bqkr[:, 1:H:2],
            start=True, stop=True, skip_group_check=True), None)
        b.do("tensor", lambda e: e.matmul(
            ps_b[0:1, 512:512 + 1], ones_fc[0:1, 0:1], ones_fc[0:1, 0:1],
            start=True, stop=True, skip_group_check=True), "st")
        b.wait("vector", "st")
        b.do("vector", lambda e: e.tensor_copy(rtmp[0:1, 0:E], ps_a[0:1, 0:E]))
        b.do("vector", lambda e: e.tensor_copy(bq_c[:, :], ps_b[:, 0:KHM]))
        b.wait("tensor", "sv")
        bcast_row(rtmp[0:1, 0:E], E, ps_a[:, 0:E], last_sem=False)
        bcast_row(rows_m[0:1, 0:E], E, ps_a[:, 512:512 + E], last_sem=True)
        b.wait("vector", "st")
        b.do("vector", lambda e: e.tensor_copy(sbc[:, 0:E], ps_a[:, 0:E]))
        b.do("vector", lambda e: e.tensor_copy(sbc[:, E:2 * E], ps_a[:, 512:512 + E]))
        b.barrier()

        def emit_stats(src, c0, CH, bufs, router, stats_ps, raw_ps=None,
                       xh_res=None, xkey="xb", sems=("sx0", "sx1")):
            xb, xq = bufs[xkey], bufs["xq"]
            xh = xh_res if xh_res is not None else bufs["xh"]
            xl = bufs.get("xl")

            def hsl(d, par):
                if xh_res is not None:
                    return xh[:, d * CH:(d + 1) * CH]
                return xh[:, par * CH:(par + 1) * CH]
            g_xb = bufs.setdefault("_g_xbS", [None, None])
            g_xd = bufs.setdefault("_g_xd", [None, None])
            fm_s = {}
            fm_s2 = {}
            fm_r = {}

            for d in range(KD):
                par = d & 1
                b.wait_m("sync", g_xb[par])
                b.do("sync", lambda e, d=d, par=par: e.dma_start(
                    out=xb[:, par * CH:(par + 1) * CH],
                    in_=src[d * 128:(d + 1) * 128, c0:c0 + CH]), sems[par], 16)
                b.wait("scalar", sems[par])
                b.wait_m("scalar", g_xd[par])
                b.do("scalar", lambda e, d=d, par=par: e.copy(
                    hsl(d, par),
                    xb[:, par * CH:(par + 1) * CH]))
                b.do("scalar", lambda e, par=par: e.activation(
                    out=xq[:, par * CH:(par + 1) * CH],
                    in_=xb[:, par * CH:(par + 1) * CH], func=AF.Square))
                g_xb[par] = b.mark("ss")
                if router:
                    b.wait("vector", "ss")
                    b.wait_m("vector", g_xd[par])
                    b.do("vector", lambda e, d=d, par=par: e.tensor_sub(
                        xl[:, par * CH:(par + 1) * CH],
                        xb[:, par * CH:(par + 1) * CH],
                        hsl(d, par)))
                    g_xb[par] = b.mark("sv")
                b.wait("tensor", "ss")
                b.wait("tensor", "sv")
                for (a0, a1) in nsplit(CH):
                    last_s = (d == KD - 1 and a1 == CH)
                    fs = fm_s.pop(a0, True); fm_s[a0] = False if fs else False
                    fm_s[a0] = False
                    b.do("tensor", lambda e, d=d, par=par, a0=a0, a1=a1,
                         fm=fs: e.matmul(
                        stats_ps[0:1, a0:a1], ones_bf[:, :],
                        hsl(d, par)[:, a0:a1],
                        start=fm, stop=False, skip_group_check=True), None)
                    fs2 = fm_s2.pop(a0, True)
                    fm_s2[a0] = False
                    b.do("tensor", lambda e, par=par, a0=a0, a1=a1,
                         last=last_s, fm=fs2: e.matmul(
                        stats_ps[32:33, a0:a1], ones_bf[:, :],
                        xq[:, par * CH + a0: par * CH + a1],
                        start=fm, stop=last, skip_group_check=True),
                        "st" if (a1 == CH and not router) else None)
                    if router:
                        last_r = (d == KD - 1 and a1 == CH)
                        fr = fm_r.pop(a0, True)
                        fm_r[a0] = False
                        b.do("tensor", lambda e, d=d, par=par, a0=a0, a1=a1,
                             fm=fr: e.matmul(
                            raw_ps[0:8, a0:a1], rw_hi[:, d * E:(d + 1) * E],
                            hsl(d, par)[:, a0:a1],
                            start=fm, stop=False, skip_group_check=True), None)
                        b.do("tensor", lambda e, d=d, par=par, a0=a0, a1=a1: e.matmul(
                            raw_ps[0:8, a0:a1], rw_hi[:, d * E:(d + 1) * E],
                            xl[:, par * CH + a0: par * CH + a1],
                            start=False, stop=False, skip_group_check=True), None)
                        b.do("tensor", lambda e, d=d, par=par, a0=a0, a1=a1,
                             last=last_r: e.matmul(
                            raw_ps[0:8, a0:a1], rw_lo[:, d * E:(d + 1) * E],
                            hsl(d, par)[:, a0:a1],
                            start=False, stop=last, skip_group_check=True),
                            "st" if a1 == CH else None)
                g_xd[par] = b.mark("st")

        def emit_rows(bufs, router, stats_ps, raw_ps=None, CH=None):
            b.wait("vector", "st")
            b.do("vector", lambda e: e.tensor_scalar(
                rows_m[0:1, 0:CH], stats_ps[0:1, 0:CH], 1.0 / D, None, ALU.mult))
            b.do("vector", lambda e: e.tensor_scalar(
                rtmp[0:1, 0:CH], stats_ps[32:33, 0:CH], 1.0 / D, None, ALU.mult))
            if router:
                b.do("vector", lambda e: e.tensor_copy(
                    bufs["rawf"][0:8, 0:CH], raw_ps[0:8, 0:CH]))
            b.do("vector", lambda e: e.tensor_mul(
                rtmp[0:1, CH:2 * CH], rows_m[0:1, 0:CH], rows_m[0:1, 0:CH]))
            b.do("vector", lambda e: e.tensor_sub(
                rtmp[0:1, 0:CH], rtmp[0:1, 0:CH], rtmp[0:1, CH:2 * CH]))
            newton_invsqrt(rtmp[0:1, 0:CH], epsc[0:1, 0:1], rows_is[0:1, 0:CH],
                           rtmp[0:1, CH:2 * CH])

        def emit_bcasts(bufs, CH, bc_m, bc_i):
            mbc, isbc = bufs["mbc"], bufs["isbc"]
            b.wait("tensor", "sv")
            b.wait("tensor", "ss")
            bcast_row(rows_m[0:1, 0:CH], CH, bc_m, last_sem=False)
            bcast_row(rows_is[0:1, 0:CH], CH, bc_i, last_sem=True)
            b.wait("vector", "st")
            b.do("vector", lambda e: e.tensor_copy(mbc[:, 0:CH], bc_m))
            b.do("vector", lambda e: e.tensor_copy(isbc[:, 0:CH], bc_i))
            return b.mark("sv")

        def emit_xln_tile(src, c0, CH, bufs, d, mbc_done,
                          xkey="xb", sems=("sx0", "sx1")):
            xb = bufs[xkey]
            tmpf, xln, mbc, isbc = (bufs["tmpf"], bufs["xln"],
                                    bufs["mbc"], bufs["isbc"])
            gk = "_g_xb" if xkey == "xb" else "_g_xbS"
            g_xb = bufs.setdefault(gk, [None, None])
            par = d & 1
            b.wait_m("sync", g_xb[par])
            b.do("sync", lambda e, d=d, par=par: e.dma_start(
                out=xb[:, par * CH:(par + 1) * CH],
                in_=src[d * 128:(d + 1) * 128, c0:c0 + CH]), sems[par], 16)
            b.wait("vector", sems[par])
            b.wait_m("vector", mbc_done)
            b.do("vector", lambda e, par=par: e.tensor_sub(
                tmpf[:, par * CH:(par + 1) * CH],
                xb[:, par * CH:(par + 1) * CH], mbc[:, 0:CH]))
            g_xb[par] = b.mark("sv")
            b.do("vector", lambda e, d=d, par=par: e.tensor_mul(
                xln[:, d * CH:(d + 1) * CH],
                tmpf[:, par * CH:(par + 1) * CH], isbc[:, 0:CH]))
            bufs.setdefault("_xln_marks", [None] * KD)[d] = b.mark("sv")

        def emit_xln_inplace(bufs, d, CH, mbc_done):
            tmpf, xln, mbc, isbc = (bufs["tmpf"], bufs["xln"],
                                    bufs["mbc"], bufs["isbc"])
            par = d & 1
            b.wait_m("vector", mbc_done)
            b.do("vector", lambda e, d=d, par=par: e.tensor_sub(
                tmpf[:, par * CH:(par + 1) * CH],
                xln[:, d * CH:(d + 1) * CH], mbc[:, 0:CH]))
            b.do("vector", lambda e, d=d, par=par: e.tensor_mul(
                xln[:, d * CH:(d + 1) * CH],
                tmpf[:, par * CH:(par + 1) * CH], isbc[:, 0:CH]))
            bufs.setdefault("_xln_marks", [None] * KD)[d] = b.mark("sv")

        def emit_ln(src, c0, CH, bufs, router, stats_ps, raw_ps=None,
                    bc_m=None, bc_i=None, xh_res=None, skip_xln=False):
            emit_stats(src, c0, CH, bufs, router, stats_ps, raw_ps,
                       xh_res=xh_res)
            emit_rows(bufs, router, stats_ps, raw_ps, CH=CH)
            if skip_xln:
                return bufs.setdefault("_g_xbS", [None, None])
            if bc_m is None:
                bc_m, bc_i = ps_a[:, 0:CH], ps_b[:, 0:CH]
            md = emit_bcasts(bufs, CH, bc_m, bc_i)
            for d in range(KD):
                emit_xln_tile(src, c0, CH, bufs, d, md)
            return bufs["_g_xb"]

        def phase_ffn(bufs, src, dst):
            CH = CHF
            nb = CH // 128
            NB8 = nb * E
            xln = bufs["xln"]; cb = bufs["cb"]; wbc = bufs["wbc"]
            xb = bufs["xb"]; tmpf = bufs["tmpf"]; sil = bufs["sil"]
            cc = bufs["cc"]; wst = bufs["wst"]; x1b = bufs["x1b"]
            lgt = bufs["lgt"]; wrows = bufs["wrows"]
            mbc = bufs["mbc"]; isbc = bufs["isbc"]; mstm = bufs["mstm"]
            V1 = bufs["v1"]; V2 = bufs["v2"]; W1 = bufs["w1"]; W2 = bufs["w2"]
            WKW = KD * 128

            g_wst = [None, None, None, None]
            g_ps = [None, None]
            g_sil = [None, None]
            g_cc = None
            g_x1b = [None, None]

            nch = T // CH

            def emit_router_tm():
                sl3v = lgt[:, 3 * NB8:4 * NB8]
                b.wait("tensor", "sv")
                for tb in range(nb):
                    b.do("tensor", lambda e, tb=tb: e.transpose(
                        ps_kv[:, tb * 8:(tb + 1) * 8],
                        bufs["rawf"][0:8, tb * 128:(tb + 1) * 128],
                        idf[0:8, 0:8]), "st" if tb == nb - 1 else None)
                for tb in range(nb):
                    b.do("tensor", lambda e, tb=tb: e.transpose(
                        ps_kv[:, 64 + tb * 2:64 + tb * 2 + 1],
                        rows_m[0:1, tb * 128:(tb + 1) * 128],
                        idf[0:1, 0:1]), None)
                    b.do("tensor", lambda e, tb=tb: e.transpose(
                        ps_kv[:, 64 + tb * 2 + 1:64 + tb * 2 + 2],
                        rows_is[0:1, tb * 128:(tb + 1) * 128],
                        idf[0:1, 0:1]), "st" if tb == nb - 1 else None)
                b.wait("vector", "st")
                b.do("vector", lambda e, sl3v=sl3v: e.tensor_copy(
                    sl3v, ps_kv[:, 0:NB8]))
                b.do("vector", lambda e: e.tensor_copy(
                    mstm[:, 0:2 * nb], ps_kv[:, 64:64 + 2 * nb]))

            def emit_router_chain():
                sl = [lgt[:, i * NB8:(i + 1) * NB8] for i in range(4)]
                sl3 = [s.rearrange("p (t e) -> p t e", t=nb) for s in sl]
                m_tm = mstm[:, 0:2 * nb:2][:, :, None].broadcast_to([128, nb, E])
                i_tm = mstm[:, 1:2 * nb:2][:, :, None].broadcast_to([128, nb, E])
                S_b = sbc[:, 0:E][:, None, :].broadcast_to([128, nb, E])
                rb_b = sbc[:, E:2 * E][:, None, :].broadcast_to([128, nb, E])
                b.do("vector", lambda e: e.tensor_tensor(sl3[0], m_tm, S_b, ALU.mult))
                b.do("vector", lambda e: e.tensor_sub(sl[1], sl[3], sl[0]))
                b.do("vector", lambda e: e.tensor_tensor(sl3[1], sl3[1], i_tm, ALU.mult))
                b.do("vector", lambda e: e.tensor_tensor(sl3[0], sl3[1], rb_b, ALU.add))
                b.do("vector", lambda e: e.reduce_max(V1[:, :, None], sl3[0], axis=AX.X))
                v1b = V1[:, :, None].broadcast_to([128, nb, E])
                b.do("vector", lambda e: e.tensor_tensor(sl3[1], sl3[0], v1b, ALU.is_equal))
                b.do("vector", lambda e: e.scalar_tensor_tensor(
                    sl[2], sl[1], NEG_BIG, sl[0], ALU.mult, ALU.add))
                b.do("vector", lambda e: e.reduce_max(V2[:, :, None], sl3[2], axis=AX.X))
                v2b = V2[:, :, None].broadcast_to([128, nb, E])
                b.do("vector", lambda e: e.tensor_tensor(sl3[2], sl3[2], v2b, ALU.is_equal))
                b.do("vector", lambda e: e.tensor_sub(W2[:, :], V2[:, :], V1[:, :]))
                b.wait("scalar", "sv")
                b.do("scalar", lambda e: e.activation(out=W2[:, :], in_=W2[:, :], func=AF.Exp))
                b.wait("vector", "ss")
                b.do("vector", lambda e: e.tensor_scalar(W1[:, :], W2[:, :], 1.0, None, ALU.add))
                b.do("vector", lambda e: e.reciprocal(W1[:, :], W1[:, :]))
                b.do("vector", lambda e: e.tensor_mul(W2[:, :], W2[:, :], W1[:, :]))
                w1b = W1[:, :, None].broadcast_to([128, nb, E])
                w2b = W2[:, :, None].broadcast_to([128, nb, E])
                b.do("vector", lambda e: e.tensor_tensor(sl3[1], sl3[1], w1b, ALU.mult))
                b.do("vector", lambda e: e.tensor_tensor(sl3[2], sl3[2], w2b, ALU.mult))
                b.do("vector", lambda e: e.tensor_add(sl[0], sl[1], sl[2]))
                b.wait("tensor", "sv")
                for tb in range(nb):
                    b.do("tensor", lambda e, tb=tb, sl=sl: e.transpose(
                        ps_kv[0:8, tb * 128:(tb + 1) * 128],
                        sl[0][:, tb * 8:(tb + 1) * 8], idf[:, :]),
                        "st" if tb == nb - 1 else None)
                b.wait("vector", "st")
                b.do("vector", lambda e: e.tensor_copy(
                    wrows[0:8, 0:CH], ps_kv[0:8, 0:CH]))

            ST_PS = ps_a[:, 0:CH]
            RAW_PS = ps_a[64:72, 0:CH]
            md_pend = [None]
            for ci in range(nch):
                c0 = ci * CH
                if ci == 0:
                    emit_stats(src, c0, CH, bufs, router=True,
                               stats_ps=ST_PS, raw_ps=RAW_PS,
                               xh_res=xln, xkey="xbS", sems=("sx2", "sx3"))
                    emit_rows(bufs, True, ST_PS, RAW_PS, CH=CH)
                    md_pend[0] = emit_bcasts(bufs, CH, ps_kv[:, 0:CH],
                                             ps_kv[:, 1024:1024 + CH])
                    emit_router_tm()
                    for d in range(KD):
                        emit_xln_inplace(bufs, d, CH, md_pend[0])
                    emit_router_chain()
                g_xb = bufs.setdefault("_g_xb", [None, None])

                xmarks = bufs["_xln_marks"]
                sl = [lgt[:, i * NB8:(i + 1) * NB8] for i in range(4)]
                sl3 = [s.rearrange("p (t e) -> p t e", t=nb) for s in sl]

                def expert_mm_silu(f, e_i, first_pair=False):
                    it = f * E + e_i
                    par = it & 1
                    wp = it % 3
                    reg = (ps_a, ps_b)[par]
                    b.wait_m("gpsimd", g_wst[wp])
                    b.do("gpsimd", lambda e, e_i=e_i, f=f, wp=wp: e.dma_start(
                        out=wst[:, wp * WKW:(wp + 1) * WKW]
                            .rearrange("p (k f) -> p k f", k=KD),
                        in_=EW[e_i].rearrange("(k p) f -> p k f", p=128)
                            [:, :, f * 128:(f + 1) * 128]), "sw%d" % wp, 16)
                    b.wait("tensor", "sw%d" % wp)
                    b.wait_m("tensor", g_ps[par])
                    for k in range(KD):
                        if first_pair:
                            b.wait_m("tensor", xmarks[k])
                        for (a0, a1) in nsplit(CH):
                            last = (k == KD - 1 and a1 == CH)
                            b.do("tensor", lambda e, k=k, a0=a0, a1=a1,
                                 wp=wp, reg=reg: e.matmul(
                                reg[:, a0:a1],
                                wst[:, wp * WKW + k * 128: wp * WKW + (k + 1) * 128],
                                xln[:, k * CH + a0: k * CH + a1],
                                start=(k == 0), stop=(k == KD - 1)),
                                "st" if last else None)
                    g_wst[wp] = b.mark("st")
                    b.wait("scalar", "st")
                    b.wait_m("scalar", g_sil[par])
                    b.do("scalar", lambda e, e_i=e_i, f=f, par=par, reg=reg:
                         e.activation(
                        out=sil[:, par * CH:(par + 1) * CH], in_=reg[:, 0:CH],
                        func=AF.Silu,
                        bias=expb[:, e_i * KD + f: e_i * KD + f + 1]))
                    g_ps[par] = b.mark("ss")
                    return par

                def expert_combine(e_i, par, wbc_gate, g_cc_in):
                    b.wait("vector", "ss")
                    if e_i == 0:
                        b.wait_m("vector", wbc_gate)
                        if g_cc_in is not None:
                            b.wait_m("vector", g_cc_in)
                        b.do("vector", lambda e, e_i=e_i, par=par: e.tensor_tensor(
                            cc[:, 0:CH], sil[:, par * CH:(par + 1) * CH],
                            wbc[:, e_i * CH:(e_i + 1) * CH], ALU.mult))
                    else:
                        b.do("vector", lambda e, e_i=e_i, par=par: e.tensor_tensor(
                            tmpf[:, par * CH:(par + 1) * CH],
                            sil[:, par * CH:(par + 1) * CH],
                            wbc[:, e_i * CH:(e_i + 1) * CH], ALU.mult))
                        b.do("vector", lambda e, par=par: e.tensor_add(
                            cc[:, 0:CH], cc[:, 0:CH],
                            tmpf[:, par * CH:(par + 1) * CH]))
                    g_sil[par] = b.mark("sv")

                par0 = expert_mm_silu(0, 0, first_pair=True)
                par1 = expert_mm_silu(0, 1)

                b.wait("tensor", "ss")
                g_kreg = [b.mark("sv"), b.mark("sv")]
                for e_i in range(E):
                    kreg = (ps_kv[:, 0:1024], ps_b[:, 0:1024])[e_i & 1]
                    b.wait_m("tensor", g_kreg[e_i & 1])
                    b.wait("tensor", "sv")
                    for i_n, (a0, a1) in enumerate(nsplit(CH)):
                        b.do("tensor", lambda e, e_i=e_i, a0=a0, a1=a1, kreg=kreg: e.matmul(
                            kreg[:, a0:a1], sel8[:, e_i * 128:(e_i + 1) * 128],
                            wrows[0:E, a0:a1],
                            start=True, stop=True, skip_group_check=True),
                            "st" if a1 == CH else None)
                    b.wait("vector", "st")
                    b.do("vector", lambda e, e_i=e_i, kreg=kreg: e.tensor_copy(
                        wbc[:, e_i * CH:(e_i + 1) * CH], kreg[:, 0:CH]))
                    g_kreg[e_i & 1] = b.mark("sv")
                g_ps[1] = b.mark("sv")
                wbc_done = b.mark("sv")

                expert_combine(0, par0, wbc_done, g_cc)
                expert_combine(1, par1, wbc_done, None)
                for f in range(KD):
                    e_start = 2 if f == 0 else 0
                    for e_i in range(e_start, E):
                        par = expert_mm_silu(f, e_i)
                        expert_combine(e_i, par, wbc_done, g_cc if e_i == 0 else None)
                    b.wait("scalar", "sv")
                    b.do("scalar", lambda e, f=f: e.copy(
                        cb[:, f * CH:(f + 1) * CH], cc[:, 0:CH]))
                    g_cc = b.mark("ss")
                    if f == 2 and ci + 1 < nch:
                        emit_stats(src, c0 + CH, CH, bufs, router=True,
                                   stats_ps=ps_kv[:, 0:CH],
                                   raw_ps=ps_kv[64:72, 0:CH],
                                   xkey="xbS", sems=("sx2", "sx3"))
                        emit_rows(bufs, True, ps_kv[:, 0:CH],
                                  ps_kv[64:72, 0:CH], CH=CH)
                        md_pend[0] = emit_bcasts(bufs, CH, ps_kv[:, 0:CH],
                                                 ps_kv[:, 1024:1024 + CH])
                        emit_router_tm()
                    if f == 3 and ci + 1 < nch:
                        emit_router_chain()
                cb_done = b.mark("ss")
                fm_y = {}
                ysq_mark = [None, None]
                for f2 in range(KD):
                    par = f2 & 1
                    wp = f2 % 3
                    reg = (ps_a, ps_b)[par]
                    b.wait_m("gpsimd", g_wst[wp])
                    b.do("gpsimd", lambda e, f2=f2, wp=wp: e.dma_start(
                        out=wst[:, wp * WKW:(wp + 1) * WKW]
                            .rearrange("p (k f) -> p k f", k=KD),
                        in_=PW.rearrange("(k p) f -> p k f", p=128)
                            [:, :, f2 * 128:(f2 + 1) * 128]), "sw%d" % wp, 16)
                    b.wait("tensor", "sw%d" % wp)
                    b.wait_m("tensor", g_ps[par])
                    if f2 == 0:
                        b.wait_m("tensor", cb_done)
                    for k in range(KD):
                        for (a0, a1) in nsplit(CH):
                            last = (k == KD - 1 and a1 == CH)
                            b.do("tensor", lambda e, k=k, a0=a0, a1=a1, wp=wp,
                                 reg=reg: e.matmul(
                                reg[:, a0:a1],
                                wst[:, wp * WKW + k * 128: wp * WKW + (k + 1) * 128],
                                cb[:, k * CH + a0: k * CH + a1],
                                start=(k == 0), stop=(k == KD - 1)),
                                "st" if last else None)
                    g_wst[wp] = b.mark("st")
                    b.wait("vector", "st")
                    b.do("vector", lambda e, f2=f2, par=par, reg=reg:
                         e.scalar_tensor_tensor(
                        xln[:, f2 * CH:(f2 + 1) * CH], reg[:, 0:CH],
                        projb[:, f2:f2 + 1],
                        xln[:, f2 * CH:(f2 + 1) * CH], ALU.add, ALU.add))
                    g_ps[par] = b.mark("sv")
                    if ci + 1 < nch:
                        b.wait("scalar", "sv")
                        b.do("scalar", lambda e, f2=f2, par=par, c0=c0: e.dma_start(
                            out=ysc[f2 * 128:(f2 + 1) * 128, c0:c0 + CH],
                            in_=xln[:, f2 * CH:(f2 + 1) * CH]), "sy%d" % (f2 & 1),
                            16, dma=True)
                    b.wait("scalar", "sv")
                    b.wait_m("scalar", bufs.setdefault("_g_ysq", [None, None])[par])
                    b.do("scalar", lambda e, f2=f2, par=par: e.activation(
                        out=bufs["ysq"][:, par * CH:(par + 1) * CH],
                        in_=xln[:, f2 * CH:(f2 + 1) * CH], func=AF.Square))
                    ysq_mark[par] = b.mark("ss")

                    def y2_mms(fp, last_f2):
                        b.wait_m("tensor", ysq_mark[fp])
                        b.wait("tensor", "sv")
                        for (a0, a1) in nsplit(CH):
                            lastq = (last_f2 and a1 == CH)
                            fy = fm_y.pop(a0, True)
                            fm_y[a0] = False
                            b.do("tensor", lambda e, a0=a0, a1=a1, fp=fp,
                                 fm=fy, lastq=lastq: e.matmul(
                                ps_kv[0:1, 1024 + a0:1024 + a1], ones_bf[:, :],
                                bufs["ysq"][:, fp * CH + a0: fp * CH + a1],
                                start=fm, stop=lastq, skip_group_check=True),
                                "st" if a1 == CH else None)
                        bufs["_g_ysq"][fp] = b.mark("st")

                    if f2 > 0:
                        y2_mms((f2 - 1) & 1, False)
                    if f2 == KD - 1:
                        y2_mms(f2 & 1, True)
                    for d in ([f2 - 1] if f2 > 0 else []) + \
                             ([KD - 1] if f2 == KD - 1 else []):
                        if ci + 1 < nch:
                            b.wait("vector", "sy%d" % (d & 1))
                            b.wait_m("vector", ysq_mark[d & 1])
                            emit_xln_tile(src, c0 + CH, CH, bufs, d,
                                          md_pend[0], xkey="xbS",
                                          sems=("sx2", "sx3"))
                b.wait("vector", "st")
                b.do("vector", lambda e: e.tensor_scalar(
                    rtmp[0:1, 0:CH], ps_kv[0:1, 1024:1024 + CH], 1.0 / D, None, ALU.mult))
                newton_invsqrt(rtmp[0:1, 0:CH], epsc[0:1, 1:2], rows_m[0:1, 0:CH],
                               rtmp[0:1, CH:2 * CH])
                b.wait("tensor", "sv")
                bcast_row(rows_m[0:1, 0:CH], CH, ps_kv[:, 1024:1024 + CH])
                rms_done = b.mark("st")
                b.wait("sync", "sy0")
                b.wait("sync", "sy1")
                g_yld = bufs.setdefault("_g_yld", [None, None])
                g_ysq = bufs["_g_ysq"]
                last_ch = (ci + 1 == nch)
                for f2 in range(KD):
                    par = f2 & 1
                    b.wait_m("sync", g_xb[par])
                    b.do("sync", lambda e, f2=f2, par=par, c0=c0: e.dma_start(
                        out=xb[:, par * CH:(par + 1) * CH],
                        in_=src[f2 * 128:(f2 + 1) * 128, c0:c0 + CH]), "sx%d" % par, 16)
                    if not last_ch:
                        b.wait_m("sync", g_yld[par])
                        b.wait_m("sync", g_ysq[par])
                        b.do("sync", lambda e, f2=f2, par=par, c0=c0: e.dma_start(
                            out=bufs["ysq"][:, par * CH:(par + 1) * CH],
                            in_=ysc[f2 * 128:(f2 + 1) * 128, c0:c0 + CH]),
                            "sz%d" % par, 16)
                    b.wait("vector", "sx%d" % par)
                    b.wait_m("vector", rms_done)
                    if not last_ch:
                        b.wait("vector", "sz%d" % par)
                        b.do("vector", lambda e, f2=f2, par=par: e.tensor_tensor(
                            bufs["ysq"][:, par * CH:(par + 1) * CH],
                            bufs["ysq"][:, par * CH:(par + 1) * CH],
                            ps_kv[:, 1024:1024 + CH], ALU.mult))
                    else:
                        b.wait_m("vector", g_ysq[par])
                        b.do("vector", lambda e, f2=f2, par=par: e.tensor_tensor(
                            bufs["ysq"][:, par * CH:(par + 1) * CH],
                            xln[:, f2 * CH:(f2 + 1) * CH],
                            ps_kv[:, 1024:1024 + CH], ALU.mult))
                    b.wait_m("vector", g_x1b[par])
                    b.do("vector", lambda e, f2=f2, par=par: e.scalar_tensor_tensor(
                        x1b[:, par * CH:(par + 1) * CH],
                        bufs["ysq"][:, par * CH:(par + 1) * CH],
                        halfnw[:, f2:f2 + 1],
                        xb[:, par * CH:(par + 1) * CH], ALU.mult, ALU.add))
                    g_xb[par] = b.mark("sv")
                    g_yld[par] = b.mark("sv")
                    b.wait("scalar", "sv")
                    b.do("scalar", lambda e, f2=f2, par=par, c0=c0: e.dma_start(
                        out=dst[f2 * 128:(f2 + 1) * 128, c0:c0 + CH],
                        in_=x1b[:, par * CH:(par + 1) * CH]), "so%d" % par, 16,
                        dma=True)
                    g_x1b[par] = b.mark("so%d" % par)
                g_ysq[0] = b.mark("sv")
                g_ysq[1] = b.mark("sv")

        def g_xd_dummy(bufs, par):
            g = bufs.get("_g_xd")
            return g[par] if g else None

        def phase_attn1(bufs, src):
            CH = CHP1
            nb = CH // 128
            xh = bufs["xh_res"]; kt = bufs["kt"]; vt = bufs["vt"]
            wvt = bufs["wvt"]; wke = bufs["wke"]; wqh = bufs["wqh"]
            et0 = bufs["et0"]; et1 = bufs["et1"]
            kv_sb = bufs["kv_sb"]; ksum_r = rtmp[0:1, RT:RT + HM]
            ksum_b = bufs["ksum_b"]; bv_b = bufs["bv_b"]
            strukt = bufs["strukt"]; bk_row = rtmp[0:1, RT:RT + HM]
            bkbc = bufs["bkbc"]; csk = bufs["csk"]; csv = bufs["csv"]
            kcs_b = bufs["kcs_b"]
            mst = bufs["mst"]; mstb = bufs["mstb"]
            W2c = D // 2

            b.do("gpsimd", lambda e: e.dma_start(out=bv_b[:], in_=BV[None, :]), "sw0", 16)
            b.do("vector", lambda e: e.memset(strukt[:], 0.0))
            b.wait("tensor", "sv"); b.wait("tensor", "ss")
            for h in range(H):
                b.do("tensor", lambda e, h=h: e.matmul(
                    ps_b[0:1, h * M:(h + 1) * M],
                    bqkr[:, H + h:H + h + 1], wmk[:, :],
                    start=True, stop=True, skip_group_check=True),
                    "st" if h == H - 1 else None)
            b.wait("vector", "st")
            b.do("vector", lambda e: e.tensor_copy(bk_row[:, :], ps_b[0:1, 0:HM]))
            b.wait("tensor", "sv")
            bcast_row(bk_row[0:1, :], HM, ps_b[:, 0:HM], last_sem=True)
            b.wait("vector", "st")
            b.do("vector", lambda e: e.tensor_copy(bkbc[:, :], ps_b[:, 0:HM]))
            for k in range(KD):
                b.do("gpsimd", lambda e, k=k: e.dma_start(
                    out=wvt[:, k * D:(k + 1) * D],
                    in_=WVT[k * 128:(k + 1) * 128, :]), "sw0", 16)

            def build_weff(wsrc, wmap_sb, wdst):
                g_wqh = [None, None]
                g_psa = None
                for h in range(H):
                    wp = h & 1
                    b.wait_m("gpsimd", g_wqh[wp])
                    b.do("gpsimd", lambda e, h=h, wp=wp, wsrc=wsrc: e.dma_start(
                        out=wqh[:, wp * D:(wp + 1) * D],
                        in_=wsrc[h * 128:(h + 1) * 128, :]), "sw%d" % wp, 16)
                    b.wait("tensor", "sw%d" % wp)
                    b.wait_m("tensor", g_psa)
                    if h == 0:
                        b.wait("tensor", "sv")
                        b.wait("tensor", "ss")
                    for db in range(KD):
                        b.do("tensor", lambda e, db=db, wp=wp, wmap_sb=wmap_sb: e.matmul(
                            ps_a[:, db * M:(db + 1) * M],
                            wqh[:, wp * D + db * 128: wp * D + (db + 1) * 128],
                            wmap_sb[:, :], start=True, stop=True,
                            skip_group_check=True),
                            "st" if db == KD - 1 else None)
                    g_wqh[wp] = b.mark("st")
                    b.wait("vector", "st")
                    b.do("vector", lambda e, h=h, wdst=wdst: e.tensor_copy(
                        wdst[:].rearrange("p (k hm) -> p k hm", k=KD)
                            [:, :, h * M:(h + 1) * M],
                        ps_a[:, 0:KD * M].rearrange("p (k m) -> p k m", k=KD)))
                    g_psa = b.mark("sv")
                return b.mark("sv")

            kv_first = [True, True, True, True]
            ks_first = [True, True]
            kc_first = [True, True]
            g_kt = [None, None]
            g_vt = [None, None]
            g_ra = None
            g_rb = None
            total_tb = (T // CH) * nb
            tbc = [0]

            g_xh = [None, None]
            g_store = [None, None]
            for ci in range(T // CH):
                c0 = ci * CH
                J0 = c0 // 128
                cpar = ci & 1
                xhv = xh[:, cpar * KD * CH:(cpar + 1) * KD * CH]
                b.wait_m("scalar", g_xh[cpar])
                b.wait_m("scalar", g_store[cpar])
                b.wait_m("tensor", g_ra)
                if ci == 0:
                    g_ra = build_weff(WK, wmk, wke)
                    g_xb = emit_ln(src, c0, CH, bufs, router=False,
                                   stats_ps=ps_kv[:, 1024:1024 + CH],
                                   xh_res=xhv, skip_xln=True)
                else:
                    g_xb = emit_ln(src, c0, CH, bufs, router=False,
                                   stats_ps=ps_a[:, 0:CH], xh_res=xhv,
                                   skip_xln=True)
                if ci == 0:
                    b.wait("tensor", "sv")
                    for kd in range(KD):
                        for (a0, a1) in nsplit(HM):
                            b.do("tensor", lambda e, kd=kd, a0=a0, a1=a1: e.matmul(
                                ps_b[0:1, a0:a1], ones_bf[:, :],
                                wke[:, kd * HM + a0: kd * HM + a1],
                                start=(kd == 0), stop=(kd == KD - 1),
                                skip_group_check=True),
                                "st" if (kd == KD - 1 and a1 == HM) else None)
                    b.wait("vector", "st")
                    b.do("vector", lambda e: e.tensor_copy(rtmp[0:1, 0:HM], ps_b[0:1, 0:HM]))
                    b.wait("tensor", "sv")
                    bcast_row(rtmp[0:1, 0:HM], HM, ps_b[:, 0:HM], last_sem=True)
                    b.wait("vector", "st")
                    b.do("vector", lambda e: e.tensor_copy(csk[:, :], ps_b[:, 0:HM]))
                    b.wait("tensor", "sw0")
                    b.wait("tensor", "sv")
                    for kd in range(KD):
                        for (a0, a1) in nsplit(D):
                            b.do("tensor", lambda e, kd=kd, a0=a0, a1=a1: e.matmul(
                                ps_kv[0:1, a0:a1], ones_bf[:, :],
                                wvt[:, kd * D + a0: kd * D + a1],
                                start=(kd == 0), stop=(kd == KD - 1),
                                skip_group_check=True),
                                "st" if (kd == KD - 1 and a1 == D) else None)
                    b.wait("scalar", "st")
                    b.do("scalar", lambda e: e.copy(csv[:, :], ps_kv[0:1, 0:D]))
                b.do("vector", lambda e: e.tensor_mul(
                    rtmp[0:1, CH:2 * CH], rows_m[0:1, 0:CH], rows_is[0:1, 0:CH]))
                b.do("vector", lambda e: e.tensor_scalar(
                    rtmp[0:1, CH:2 * CH], rtmp[0:1, CH:2 * CH], -1.0, None, ALU.mult))
                b.wait("tensor", "sv")
                def emit_mst_tm(J0=J0):
                    b.wait("tensor", "ss")
                    for tb in range(nb):
                        b.do("tensor", lambda e, tb=tb: e.transpose(
                            ps_b[:, 512 + 2 * tb:512 + 2 * tb + 1],
                            rows_is[0:1, tb * 128:(tb + 1) * 128],
                            idf[0:1, 0:1]), None)
                        b.do("tensor", lambda e, tb=tb: e.transpose(
                            ps_b[:, 512 + 2 * tb + 1:512 + 2 * tb + 2],
                            rtmp[0:1, CH + tb * 128:CH + (tb + 1) * 128],
                            idf[0:1, 0:1]), "st" if tb == nb - 1 else None)
                    b.wait("vector", "st")
                    b.do("vector", lambda e, J0=J0: e.tensor_copy(
                        mst[:, 2 * J0:2 * (J0 + nb)], ps_b[:, 512:512 + 2 * nb]))
                    b.wait("scalar", "sv")
                    b.do("scalar", lambda e, J0=J0: e.copy(
                        mstb[:, 2 * J0:2 * (J0 + nb)],
                        mst[:, 2 * J0:2 * (J0 + nb)]))
                b.wait("sync", "ss")
                b.do("sync", lambda e, c0=c0, xhv=xhv: e.dma_start(
                    out=xlns.rearrange("(k p) t -> p k t", p=128)[:, :, c0:c0 + CH],
                    in_=xhv.rearrange("p (k t) -> p k t", k=KD)),
                    "sxm" if cpar == 0 else "sw3", 16)
                g_store[cpar] = b.mark("sxm" if cpar == 0 else "sw3")

                for tb in range(nb):
                    tbc[0] += 1
                    last_tb = tbc[0] == total_tb
                    par = tb & 1
                    J = J0 + tb
                    a_col = mst[:, 2 * J:2 * J + 1]
                    c_col = mst[:, 2 * J + 1:2 * J + 2]
                    c_colb = mstb[:, 2 * J + 1:2 * J + 2]
                    b.wait_m("tensor", g_ra)
                    b.wait("tensor", "ss")
                    if tb == 0:
                        b.wait("tensor", "sv")
                    for d in range(KD):
                        for (a0, a1) in nsplit(HM):
                            last = (d == KD - 1 and a1 == HM)
                            b.do("tensor", lambda e, d=d, tb=tb, a0=a0, a1=a1,
                                 xhv=xhv: e.matmul(
                                ps_a[:, a0:a1],
                                xhv[:, d * CH + tb * 128: d * CH + (tb + 1) * 128],
                                wke[:, d * HM + a0: d * HM + a1],
                                start=(d == 0), stop=(d == KD - 1)),
                                "st" if last else None)
                    if tb == 0:
                        emit_mst_tm()
                    b.wait("vector", "st")
                    b.wait_m("vector", g_kt[par])
                    b.do("vector", lambda e, a_col=a_col: e.scalar_tensor_tensor(
                        et0[:, 0:HM], ps_a[:, 0:HM], a_col, bkbc[:, :],
                        ALU.mult, ALU.add))
                    b.do("vector", lambda e, c_col=c_col: e.scalar_tensor_tensor(
                        et0[:, 0:HM], csk[:, :], c_col, et0[:, 0:HM],
                        ALU.mult, ALU.add))
                    b.do("vector", lambda e: e.tensor_scalar_min(
                        et1[:, 0:HM], et0[:, 0:HM], 0.0))
                    b.wait("scalar", "sv")
                    b.do("scalar", lambda e: e.activation(
                        out=et1[:, 0:HM], in_=et1[:, 0:HM], func=AF.Exp))
                    b.wait("vector", "ss")
                    b.do("vector", lambda e, par=par: e.scalar_tensor_tensor(
                        kt[:, par * HM:(par + 1) * HM], et0[:, 0:HM], 0.0,
                        et1[:, 0:HM], ALU.max, ALU.add))
                    g_ra = b.mark("sv")
                    kt_done = b.mark("sv")
                    for half in (0, 1):
                        reg = ps_b if half == 0 else ps_a
                        b.wait_m("tensor", g_rb if half == 0 else g_ra)
                        if tb == 0 and half == 0:
                            b.wait("tensor", "sv")
                        for d in range(KD):
                            for (a0, a1) in nsplit(W2c):
                                last = (d == KD - 1 and a1 == W2c)
                                b.do("tensor", lambda e, d=d, tb=tb, a0=a0, a1=a1,
                                     half=half, reg=reg, xhv=xhv: e.matmul(
                                    reg[:, a0:a1],
                                    xhv[:, d * CH + tb * 128: d * CH + (tb + 1) * 128],
                                    wvt[:, d * D + half * W2c + a0: d * D + half * W2c + a1],
                                    start=(d == 0), stop=(d == KD - 1)),
                                    "st" if last else None)
                        b.wait("vector", "st")
                        b.wait_m("vector", g_vt[par] if half == 0 else None)
                        b.do("vector", lambda e, par=par, half=half, reg=reg,
                             a_col=a_col: e.tensor_scalar(
                            vt[:, par * D + half * W2c: par * D + (half + 1) * W2c],
                            reg[:, 0:W2c], a_col, None, ALU.mult))
                        if half == 0:
                            g_rb = b.mark("sv")
                        else:
                            g_ra = b.mark("sv")
                    vt_done = b.mark("sv")
                    b.wait_m("tensor", kt_done)
                    b.wait_m("tensor", vt_done)
                    for h in range(H):
                        bank = (h * 64) // 512
                        fm = kv_first[bank]
                        kv_first[bank] = False
                        b.do("tensor", lambda e, h=h, par=par, fm=fm: e.matmul(
                            ps_kv[:, h * 64:(h + 1) * 64],
                            vt[:, par * D + h * 128: par * D + (h + 1) * 128],
                            kt[:, par * HM + h * M: par * HM + (h + 1) * M],
                            start=fm, stop=False, skip_group_check=True), None)
                    for i_k, (a0, a1) in enumerate(nsplit(HM)):
                        lastk = (a1 == HM)
                        fmk = ks_first[i_k] if i_k < 2 else False
                        if i_k < 2:
                            ks_first[i_k] = False
                        b.do("tensor", lambda e, par=par, a0=a0, a1=a1, fmk=fmk,
                             stp=(last_tb and lastk): e.matmul(
                            ps_kv[0:1, 1024 + a0:1024 + a1], ones_bf[:, :],
                            kt[:, par * HM + a0: par * HM + a1],
                            start=fmk, stop=False, skip_group_check=True), None)
                        fmc = kc_first[i_k] if i_k < 2 else False
                        if i_k < 2:
                            kc_first[i_k] = False
                        b.do("tensor", lambda e, par=par, a0=a0, a1=a1, fmc=fmc,
                             c_colb=c_colb,
                             stp=(last_tb and lastk): e.matmul(
                            ps_kv[32:33, 1024 + a0:1024 + a1], c_colb,
                            kt[:, par * HM + a0: par * HM + a1],
                            start=fmc, stop=stp, skip_group_check=True),
                            "st" if lastk else None)
                    g_kt[par] = b.mark("st")
                    g_vt[par] = b.mark("st")
                g_xh[cpar] = b.mark("st")

            b.wait("vector", "st")
            b.do("vector", lambda e: e.tensor_copy(ksum_r[:, :], ps_kv[0:1, 1024:1024 + HM]))
            b.do("vector", lambda e: e.tensor_copy(rtmp[0:1, 0:HM], ps_kv[32:33, 1024:1024 + HM]))
            b.wait("scalar", "sv")
            b.do("scalar", lambda e: e.copy(ksum_b[:, :], ksum_r[:, :]))
            b.do("scalar", lambda e: e.copy(kcs_b[:, :], rtmp[0:1, 0:HM]))
            b.wait("tensor", "ss")
            for h in range(H):
                b.do("tensor", lambda e, h=h: e.matmul(
                    ps_kv[:, h * 64:(h + 1) * 64],
                    bv_b[0:1, h * 128:(h + 1) * 128],
                    ksum_b[0:1, h * M:(h + 1) * M],
                    start=False, stop=False, skip_group_check=True), None)
                b.do("tensor", lambda e, h=h: e.matmul(
                    ps_kv[:, h * 64:(h + 1) * 64],
                    csv[0:1, h * 128:(h + 1) * 128],
                    kcs_b[0:1, h * M:(h + 1) * M],
                    start=False, stop=(h == H - 1), skip_group_check=True),
                    "st" if h == H - 1 else None)
            b.wait("vector", "st")
            b.do("vector", lambda e: e.tensor_copy(kv_sb[:, :], ps_kv[:, 0:HM]))
            b.wait("tensor", "sv")
            b.wait("tensor", "ss")
            for j in range(KHM):
                b.do("tensor", lambda e, j=j: e.transpose(
                    ps_a[:, j:j + 1], ksum_r[0:1, j * 128:(j + 1) * 128],
                    idf[0:1, 0:1]), "st" if j == KHM - 1 else None)
            b.wait("vector", "st")
            for j in range(KHM):
                b.do("vector", lambda e, j=j: e.tensor_copy(
                    strukt[0:64, j * H + 2 * j: j * H + 2 * j + 1],
                    ps_a[0:64, j:j + 1]))
                b.do("vector", lambda e, j=j: e.tensor_copy(
                    strukt[64:128, j * H + 2 * j + 1: j * H + 2 * j + 2],
                    ps_a[64:128, j:j + 1]))

        def phase_attn2(bufs, src, dst):
            CH = CHP2
            nb2 = CH // 128
            xh = bufs["xh"]; qe = bufs["qe"]; kvo = bufs["kvo"]
            wqe = bufs["wqe"]; wqh = bufs["wqh"]
            ninv = bufs["ninv"]; x2b = bufs["x2b"]; qtmp = bufs["qtmp"]
            xb = bufs["xb"]; tmpf = bufs["tmpf"]
            abc = bufs["abc"]; cbc = bufs["cbc"]; csq = bufs["csq"]
            kv_sb = bufs["kv_sb"]; strukt = bufs["strukt"]; sel = bufs["sel"]
            mst = bufs["mst"]
            b.do("sync", lambda e: e.dma_start(out=sel[:], in_=SELH[:, :]), "sxm", 16)
            b.wait("tensor", "sxm")
            g_wqh = [None, None]
            g_psa = None
            for h in range(H):
                wp = h & 1
                b.wait_m("gpsimd", g_wqh[wp])
                b.do("gpsimd", lambda e, h=h, wp=wp: e.dma_start(
                    out=wqh[:, wp * D:(wp + 1) * D],
                    in_=WQ[h * 128:(h + 1) * 128, :]), "sw%d" % wp, 16)
                b.wait("tensor", "sw%d" % wp)
                b.wait_m("tensor", g_psa)
                for db in range(KD):
                    b.do("tensor", lambda e, db=db, wp=wp: e.matmul(
                        ps_a[:, db * M:(db + 1) * M],
                        wqh[:, wp * D + db * 128: wp * D + (db + 1) * 128],
                        wmq[:, :], start=True, stop=True, skip_group_check=True),
                        "st" if db == KD - 1 else None)
                g_wqh[wp] = b.mark("st")
                b.wait("vector", "st")
                b.do("vector", lambda e, h=h: e.tensor_copy(
                    wqe[:].rearrange("p (k hm) -> p k hm", k=KD)
                        [:, :, h * M:(h + 1) * M],
                    ps_a[:, 0:KD * M].rearrange("p (k m) -> p k m", k=KD)))
                g_psa = b.mark("sv")
            PC = D // 512
            g_regE = None
            g_regO = None
            for h in range(H):
                wp = h & 1
                j = h // 2
                r0 = (h & 1) * 64
                b.wait_m("gpsimd", g_wqh[wp])
                b.do("gpsimd", lambda e, h=h, wp=wp: e.dma_start(
                    out=wqh[:, wp * D:(wp + 1) * D],
                    in_=WOT[h * 128:(h + 1) * 128, :]), "sw%d" % wp, 16)
                b.wait("tensor", "sw%d" % wp)
                b.wait("tensor", "sv")
                if r0 == 0:
                    b.wait_m("tensor", g_regE if j % 2 == 0 else g_regO)
                for i in range(PC):
                    if j % 2 == 0:
                        reg = ps_kv[:, i * 512:(i + 1) * 512]
                    else:
                        reg = (ps_a if i < 2 else ps_b)[:, (i % 2) * 512:
                                                        ((i % 2) + 1) * 512]
                    b.do("tensor", lambda e, h=h, wp=wp, i=i, r0=r0, reg=reg:
                         e.matmul(
                        reg[r0:r0 + 64, 0:512],
                        kv_sb[:, h * 64:(h + 1) * 64],
                        wqh[:, wp * D + i * 512: wp * D + (i + 1) * 512],
                        start=True, stop=True, skip_group_check=True),
                        "st" if i == PC - 1 else None)
                g_wqh[wp] = b.mark("st")
                if h & 1:
                    if j % 2 == 0:
                        b.wait("vector", "st")
                        b.do("vector", lambda e, j=j: e.tensor_copy(
                            kvo[:, j * D:(j + 1) * D], ps_kv[:, 0:D]))
                        g_regE = b.mark("sv")
                    else:
                        b.wait("scalar", "st")
                        for i in range(PC):
                            regc = (ps_a if i < 2 else ps_b)[:, (i % 2) * 512:
                                                             ((i % 2) + 1) * 512]
                            b.do("scalar", lambda e, j=j, i=i, regc=regc: e.copy(
                                kvo[:, j * D + i * 512: j * D + (i + 1) * 512],
                                regc[:, :]))
                        g_regO = b.mark("ss")
            b.wait("tensor", "sv")
            b.wait("tensor", "ss")

            b.wait_m("tensor", g_psa)
            for kd in range(KD):
                for (a0, a1) in nsplit(HM):
                    b.do("tensor", lambda e, kd=kd, a0=a0, a1=a1: e.matmul(
                        ps_b[0:1, a0:a1], ones_bf[:, :],
                        wqe[:, kd * HM + a0: kd * HM + a1],
                        start=(kd == 0), stop=(kd == KD - 1),
                        skip_group_check=True),
                        "st" if (kd == KD - 1 and a1 == HM) else None)
            b.wait("vector", "st")
            b.do("vector", lambda e: e.tensor_copy(rtmp[0:1, 0:HM], ps_b[0:1, 0:HM]))
            b.wait("tensor", "sv")
            for j in range(KHM):
                b.do("tensor", lambda e, j=j: e.transpose(
                    ps_b[:, 512 + j:512 + j + 1],
                    rtmp[0:1, j * 128:(j + 1) * 128],
                    idf[0:1, 0:1]), "st" if j == KHM - 1 else None)
            b.wait("vector", "st")
            b.do("vector", lambda e: e.tensor_copy(csq[:, :], ps_b[:, 512:512 + KHM]))

            slots = [ps_a[:, 0:512], ps_a[:, 512:1024],
                     ps_b[:, 0:512], ps_b[:, 512:1024]]
            g_reg = [None, None, b.mark("sv"), b.mark("sv")]
            g_qt = [None, None]
            g_x2b = [None, None]
            g_at = [None]
            g_ninv = None
            g_tm = [None, None]
            g_xb = [None, None]
            g_qps = [None, None]
            NCH2 = T // CH

            def emit_ac_rows(c):
                par = c & 1
                J0 = (c * CH) // 128
                po = par * CH
                b.wait_m("tensor", g_qps[0])
                b.wait_m("tensor", g_qps[1])
                b.wait("tensor", "sv")
                for tb in range(nb2):
                    b.do("tensor", lambda e, tb=tb, J0=J0: e.transpose(
                        ps_kv[0:1, tb * 128:(tb + 1) * 128],
                        mst[:, 2 * (J0 + tb):2 * (J0 + tb) + 1],
                        idf[:, :]), None)
                    b.do("tensor", lambda e, tb=tb, J0=J0: e.transpose(
                        ps_kv[0:1, 512 + tb * 128:512 + (tb + 1) * 128],
                        mst[:, 2 * (J0 + tb) + 1:2 * (J0 + tb) + 2],
                        idf[:, :]), "st" if tb == nb2 - 1 else None)
                b.wait("vector", "st")
                b.do("vector", lambda e: e.tensor_copy(
                    rtmp[0:1, 0:CH], ps_kv[0:1, 0:CH]))
                b.do("vector", lambda e: e.tensor_copy(
                    rtmp[0:1, CH:2 * CH], ps_kv[0:1, 512:512 + CH]))
                b.wait("tensor", "sv")
                bcast_row(rtmp[0:1, 0:CH], CH, ps_kv[:, 0:CH], last_sem=False)
                bcast_row(rtmp[0:1, CH:2 * CH], CH, ps_kv[:, 512:512 + CH],
                          last_sem=True)
                b.wait("vector", "st")
                b.do("vector", lambda e, po=po: e.tensor_copy(
                    abc[:, po:po + CH], ps_kv[:, 0:CH]))
                b.do("vector", lambda e, po=po: e.tensor_copy(
                    cbc[:, po:po + CH], ps_kv[:, 512:512 + CH]))
                g_qps[0] = b.mark("sv")
                g_qps[1] = b.mark("sv")

            def emit_xh_load(c):
                b.wait_m("sync", g_at[0])
                b.do("sync", lambda e, c0=c * CH: e.dma_start(
                    out=xh[:].rearrange("p (k t) -> p k t", k=KD),
                    in_=xlns.rearrange("(k p) t -> p k t", p=128)
                        [:, :, c0:c0 + CH]), "sxl", 16)

            def emit_q_block(c, j):
                par = c & 1
                qoff = par * KHM * CH
                po = par * CH
                half = j & 1
                reg = ps_kv[:, half * 512:half * 512 + CH]
                b.wait_m("tensor", g_qps[half])
                b.wait("tensor", "sxl")
                for d in range(KD):
                    b.do("tensor", lambda e, j=j, d=d, reg=reg: e.matmul(
                        reg[:, 0:CH],
                        wqe[:, d * HM + j * 128: d * HM + (j + 1) * 128],
                        xh[:, d * CH:(d + 1) * CH],
                        start=(d == 0), stop=(d == KD - 1)),
                        "st" if d == KD - 1 else None)
                if j == KHM - 1:
                    g_at[0] = b.mark("st")
                b.wait("vector", "st")
                b.wait_m("vector", g_qt[half])
                jp = half * CH
                b.do("vector", lambda e, reg=reg, jp=jp, po=po: e.tensor_tensor(
                    qtmp[:, jp:jp + CH], reg[:, 0:CH], abc[:, po:po + CH],
                    ALU.mult))
                b.do("vector", lambda e, j=j, jp=jp, po=po: e.scalar_tensor_tensor(
                    qtmp[:, jp:jp + CH], cbc[:, po:po + CH], csq[:, j:j + 1],
                    qtmp[:, jp:jp + CH], ALU.mult, ALU.add))
                g_qps[half] = b.mark("sv")
                b.wait_m("vector", g_tm[half])
                b.do("vector", lambda e, j=j, jp=jp: e.tensor_scalar(
                    tmpf[:, jp:jp + CH], qtmp[:, jp:jp + CH],
                    bq_c[:, j:j + 1], 0.0, ALU.add, ALU.min))
                b.wait("scalar", "sv")
                b.do("scalar", lambda e, jp=jp: e.activation(
                    out=tmpf[:, jp:jp + CH], in_=tmpf[:, jp:jp + CH], func=AF.Exp))
                b.wait("vector", "ss")
                b.do("vector", lambda e, j=j, jp=jp: e.tensor_scalar(
                    qtmp[:, jp:jp + CH], qtmp[:, jp:jp + CH],
                    bq_c[:, j:j + 1], 0.0, ALU.add, ALU.max))
                b.do("vector", lambda e, j=j, jp=jp, qoff=qoff: e.tensor_add(
                    qe[:, qoff + j * CH:qoff + (j + 1) * CH],
                    tmpf[:, jp:jp + CH], qtmp[:, jp:jp + CH]))
                g_qt[half] = b.mark("sv")
                g_tm[half] = b.mark("sv")

            def emit_norm(c, qed):
                nonlocal g_ninv
                qoff = (c & 1) * KHM * CH
                b.wait_m("tensor", qed)
                b.wait_m("tensor", g_ninv)
                for j in range(KHM):
                    b.do("tensor", lambda e, j=j, qoff=qoff: e.matmul(
                        ps_kv[0:H, 0:CH], strukt[:, j * H:(j + 1) * H],
                        qe[:, qoff + j * CH:qoff + (j + 1) * CH],
                        start=(j == 0), stop=(j == KHM - 1),
                        skip_group_check=True),
                        "st" if j == KHM - 1 else None)
                b.wait("vector", "st")
                b.do("vector", lambda e: e.tensor_scalar(
                    ninv[:, 0:CH], ps_kv[0:H, 0:CH], ATT_EPS, None, ALU.add))
                b.do("vector", lambda e: e.reciprocal(ninv[:, 0:CH], ninv[:, 0:CH]))
                g_ninv = b.mark("sv")

            emit_ac_rows(0)
            emit_xh_load(0)
            for j in range(KHM):
                emit_q_block(0, j)
            qe_done = b.mark("sv")
            emit_norm(0, qe_done)

            for ci in range(NCH2):
                c0 = ci * CH
                par = ci & 1
                qoff = par * KHM * CH
                b.wait_m("tensor", g_ninv)
                for j in range(KHM):
                    slot = j % 4
                    reg = slots[slot]
                    b.wait_m("tensor", g_reg[slot])
                    b.do("tensor", lambda e, j=j, reg=reg: e.matmul(
                        reg[:, 0:CH], sel[:, j * 128:(j + 1) * 128], ninv[:, 0:CH],
                        start=True, stop=True, skip_group_check=True), "st")
                    b.wait("vector", "st")
                    b.do("vector", lambda e, j=j, reg=reg, qoff=qoff: e.tensor_tensor(
                        qe[:, qoff + j * CH:qoff + (j + 1) * CH],
                        qe[:, qoff + j * CH:qoff + (j + 1) * CH],
                        reg[:, 0:CH], ALU.mult))
                    g_reg[slot] = b.mark("sv")
                qn_done = b.mark("sv")
                b.wait_m("tensor", qn_done)
                for f2 in range(KD):
                    slot = f2 % 4
                    reg = slots[slot]
                    b.wait_m("tensor", g_reg[slot])
                    for jj in range(KHM):
                        b.do("tensor", lambda e, jj=jj, f2=f2, reg=reg, qoff=qoff:
                             e.matmul(
                            reg[:, 0:CH],
                            kvo[:, jj * D + f2 * 128: jj * D + f2 * 128 + 128],
                            qe[:, qoff + jj * CH:qoff + (jj + 1) * CH],
                            start=(jj == 0), stop=(jj == KHM - 1)),
                            "st" if jj == KHM - 1 else None)
                    pr = f2 & 1
                    b.wait_m("sync", g_xb[pr])
                    b.do("sync", lambda e, f2=f2, pr=pr, c0=c0: e.dma_start(
                        out=xb[:, pr * CH:(pr + 1) * CH],
                        in_=src[f2 * 128:(f2 + 1) * 128, c0:c0 + CH]),
                        "sx%d" % pr, 16)
                    b.wait("vector", "sx%d" % pr)
                    b.wait("vector", "st")
                    b.wait_m("vector", g_x2b[pr])
                    b.do("vector", lambda e, f2=f2, pr=pr, reg=reg:
                         e.scalar_tensor_tensor(
                        x2b[:, pr * CH:(pr + 1) * CH], reg[:, 0:CH],
                        bo_c[:, f2:f2 + 1], xb[:, pr * CH:(pr + 1) * CH],
                        ALU.add, ALU.add))
                    g_xb[pr] = b.mark("sv")
                    g_reg[slot] = b.mark("sv")
                    b.wait("scalar", "sv")
                    b.do("scalar", lambda e, f2=f2, pr=pr, c0=c0: e.dma_start(
                        out=dst[f2 * 128:(f2 + 1) * 128, c0:c0 + CH],
                        in_=x2b[:, pr * CH:(pr + 1) * CH]), "so%d" % pr, 16,
                        dma=True)
                    g_x2b[pr] = b.mark("so%d" % pr)
                    if ci + 1 < NCH2:
                        if f2 == 0:
                            emit_ac_rows(ci + 1)
                            emit_xh_load(ci + 1)
                        jh = f2 - 1
                        if 0 <= jh < KHM:
                            emit_q_block(ci + 1, jh)
                            if jh == KHM - 1:
                                qe_done = b.mark("sv")
                        if f2 == KHM + 1:
                            emit_norm(ci + 1, qe_done)

        def make_ffn_bufs(pfx):
            mgr = []
            def t(name, shape, dt):
                g = nc.sbuf_tensor(pfx + name, shape, dt)
                h = g.__enter__()
                mgr.append(g)
                return h
            bufs = dict(
                xln=t("xln", [128, KD * CHF], BF16),
                cb=t("cb", [128, KD * CHF], BF16),
                wbc=t("wbc", [128, E * CHF], F32),
                xb=t("xb", [128, 2 * CHF], F32),
                xh=t("xh", [128, 2 * CHF], BF16),
                xl=t("xl", [128, 2 * CHF], BF16),
                xq=t("xq", [128, 2 * CHF], BF16),
                tmpf=t("tmpf", [128, 2 * CHF], F32),
                sil=t("sil", [128, 2 * CHF], F32),
                cc=t("cc", [128, CHF], F32),
                wst=t("wst", [128, 3 * KD * 128], BF16),
                mbc=t("mbc", [128, CHF], BF16),
                isbc=t("isbc", [128, CHF], BF16),
                x1b=t("x1b", [128, 2 * CHF], F32),
                xbS=t("xbS", [128, 2 * CHF], F32),
                ysq=t("ysq", [128, 2 * CHF], BF16),
                lgt=t("lgt", [128, 4 * (CHF // 128) * E], F32),
                wrows=t("wrows", [8, CHF], F32),
                rawf=t("rawf", [8, CHF], F32),
                mstm=t("mstm", [128, 2 * (CHF // 128)], F32),
                v1=t("v1", [128, CHF // 128], F32),
                v2=t("v2", [128, CHF // 128], F32),
                w1=t("w1", [128, CHF // 128], F32),
                w2=t("w2", [128, CHF // 128], F32),
            )
            return bufs, mgr

        def close_bufs(mgr):
            for g in reversed(mgr):
                g.__exit__(None, None, None)

        fb, mgr = make_ffn_bufs("f_")
        phase_ffn(fb, src=x0, dst=x1s)
        close_bufs(mgr)
        b.barrier()

        amgr = []
        def at_t(name, shape, dt):
            g = nc.sbuf_tensor(name, shape, dt)
            h = g.__enter__()
            amgr.append(g)
            return h
        kv_sb = at_t("kv_sb", [128, HM], BF16)
        ksum_b = at_t("ksum_b", [1, HM], BF16)
        bv_b = at_t("bv_b", [1, D], BF16)
        strukt = at_t("strukt", [128, KHM * H], BF16)
        mst_tm = at_t("mst_tm", [128, 2 * (T // 128)], F32)
        mst_tmb = at_t("mst_tmb", [128, 2 * (T // 128)], BF16)

        p1mgr = []
        def p1_t(name, shape, dt):
            g = nc.sbuf_tensor(name, shape, dt)
            h = g.__enter__()
            p1mgr.append(g)
            return h
        p1 = dict(
            wvt=p1_t("p1_wvt", [128, KD * D], BF16),
            wke=p1_t("p1_wke", [128, KD * HM], BF16),
            wqh=p1_t("p1_wqh", [128, 2 * D], BF16),
            xh_res=p1_t("p1_xh_res", [128, 2 * KD * CHP1], BF16),
            xb=p1_t("p1_xb", [128, 2 * CHP1], F32),
            xq=p1_t("p1_xq", [128, 2 * CHP1], BF16),
            tmpf=p1_t("p1_tmpf", [128, 2 * CHP1], F32),
            kt=p1_t("p1_kt", [128, 2 * HM], BF16),
            vt=p1_t("p1_vt", [128, 2 * D], BF16),
            et0=p1_t("p1_et0", [128, HM], BF16),
            bkbc=p1_t("p1_bkbc", [128, HM], F32),
            csk=p1_t("p1_csk", [128, HM], BF16),
            csv=p1_t("p1_csv", [1, D], BF16),
            kcs_b=p1_t("p1_kcs_b", [1, HM], BF16),
            kv_sb=kv_sb, ksum_b=ksum_b, bv_b=bv_b,
            strukt=strukt, mst=mst_tm, mstb=mst_tmb,
        )
        p1["et1"] = p1["tmpf"]
        phase_attn1(p1, src=x1s)
        close_bufs(p1mgr)
        b.barrier()

        p2mgr = []
        def p2_t(name, shape, dt):
            g = nc.sbuf_tensor(name, shape, dt)
            h = g.__enter__()
            p2mgr.append(g)
            return h
        p2_xh = p2_t("p2_xh", [128, max(KD, H) * CHP2], BF16)
        p2 = dict(
            kvo=p2_t("p2_kvo", [128, KHM * D], BF16),
            wqe=p2_t("p2_wqe", [128, KD * HM], BF16),
            wqh=p2_t("p2_wqh", [128, 2 * D], BF16),
            xh=p2_xh,
            xb=p2_t("p2_xb", [128, 2 * CHP2], F32),
            tmpf=p2_t("p2_tmpf", [128, 2 * CHP2], F32),
            qe=p2_t("p2_qe", [128, 2 * KHM * CHP2], BF16),
            ninv=p2_t("p2_ninv", [H, CHP2], F32),
            x2b=p2_t("p2_x2b", [128, 2 * CHP2], F32),
            qtmp=p2_t("p2_qtmp", [128, 2 * CHP2], F32),
            sel=p2_t("p2_sel", [H, KHM * 128], F32),
            abc=p2_t("p2_abc", [128, 2 * CHP2], F32),
            cbc=p2_t("p2_cbc", [128, 2 * CHP2], F32),
            csq=p2_t("p2_csq", [128, KHM], F32),
            kv_sb=kv_sb, strukt=strukt, mst=mst_tm,
        )
        phase_attn2(p2, src=x1s, dst=x2s)
        close_bufs(p2mgr)
        close_bufs(amgr)
        b.barrier()

        gb, mgr2 = make_ffn_bufs("g_")
        phase_ffn(gb, src=x2s, dst=out)
        close_bufs(mgr2)

        for s in SEMS:
            if not s.startswith("so"):
                b.wait("sync", s)
        b.wait("sync", "so0")
        b.wait("sync", "so1")

        with nc.Block() as block:
            @block.sync
            def _(e):
                for fn in b.ops["sync"]:
                    fn(e, S)

            @block.gpsimd
            def _(e):
                for fn in b.ops["gpsimd"]:
                    fn(e, S)

            @block.scalar
            def _(e):
                for fn in b.ops["scalar"]:
                    fn(e, S)

            @block.vector
            def _(e):
                for fn in b.ops["vector"]:
                    fn(e, S)

            @block.tensor
            def _(e):
                for fn in b.ops["tensor"]:
                    fn(e, S)

    nc.finalize()
    return nc


CFG_FULL = dict(D=2048, T=2048, H=16, M=64, E=8, CHF=1024, CHP1=512, CHP2=512)

_nc_cache = {}


def host_weights(inputs):
    g = lambda k: np.ascontiguousarray(np.asarray(inputs[k], np.float32))
    w = {}
    w["expert_w"] = g("L0_ffn_expert_w")
    w["expert_b"] = g("L0_ffn_expert_b")
    w["router_w"] = g("L0_ffn_router_w")
    w["router_b"] = g("L0_ffn_router_b")
    w["proj_w"] = g("L0_ffn_proj_w")
    w["proj_b"] = g("L0_ffn_proj_b")
    w["norm_w"] = g("L0_ffn_norm_w")
    w["wq"] = g("L0_attn_wq")
    w["bq"] = g("L0_attn_bq")
    w["wk"] = g("L0_attn_wk")
    w["bk"] = g("L0_attn_bk")
    w["wvT"] = np.ascontiguousarray(g("L0_attn_wv").T)
    w["bv"] = g("L0_attn_bv")
    w["woT"] = np.ascontiguousarray(g("L0_attn_wo").T)
    w["bo"] = g("L0_attn_bo")
    w["wmap_qT"] = np.ascontiguousarray(g("L0_attn_wmap_q").T)
    w["wmap_kT"] = np.ascontiguousarray(g("L0_attn_wmap_k").T)
    w["idf32"] = np.eye(128, dtype=np.float32)
    E = 8
    sel8 = np.zeros((E, E * 128), np.float32)
    for e in range(E):
        sel8[e, e * 128:(e + 1) * 128] = 1.0
    w["sel8_c"] = sel8
    H = 16
    KHM = (H * 64) // 128
    selh = np.zeros((H, KHM * 128), np.float32)
    for j in range(KHM):
        selh[2 * j, j * 128: j * 128 + 64] = 1.0
        selh[2 * j + 1, j * 128 + 64: (j + 1) * 128] = 1.0
    w["selh_c"] = selh
    return w


def kernel(**inputs):
    cfg = CFG_FULL
    if "full" not in _nc_cache:
        _nc_cache["full"] = build_program(cfg)
    nc = _nc_cache["full"]

    x = np.asarray(inputs["L0_attn_hidden_states"], dtype=np.float32)
    B = x.shape[0]
    w = host_weights(inputs)
    in_maps = []
    for c in range(B):
        m = {"x0": np.ascontiguousarray(x[c].T)}
        m.update(w)
        in_maps.append(m)

    res = run_bass_kernel_spmd(nc, in_maps, core_ids=list(range(B)))
    outs = [np.ascontiguousarray(np.asarray(r["out_fm"]).T) for r in res.results]
    return np.stack(outs, axis=0).astype(np.float32)
